# version 08
# baseline (speedup 1.0000x reference)
import numpy as np
import ml_dtypes
from contextlib import ExitStack

import concourse.bass as bass
import concourse.mybir as mybir
from concourse.bass_utils import run_bass_kernel_spmd

F32 = mybir.dt.float32
BF16 = mybir.dt.bfloat16
AF = mybir.ActivationFunctionType
ALU = mybir.AluOpType
AX = mybir.AxisListType
BF16NP = ml_dtypes.bfloat16

D = 4096
T = 2048
NIB = D // 128
NTC = T // 512
NCORES = 8



class Ctx:

    def __init__(self, nc, es, pfx=""):
        self.nc = nc
        self.es = es
        self.pfx = pfx
        self.h = {}
        self.n = {}

    def sem(self, name):
        if name not in self.h:
            self.h[name] = self.nc.alloc_semaphore(self.pfx + name)
            self.n[name] = 0
        return self.h[name]

    def sbuf(self, name, shape, dt):
        return self.nc.sbuf_tensor(self.pfx + name, shape, dt)

    def psum(self, name, shape, dt):
        return self.nc.psum_tensor(self.pfx + name, shape, dt)

    def inline(self, data, name):
        return self.nc.inline_tensor(data, name=self.pfx + name).ap()

    def dram(self, name, shape, dt):
        return self.nc.dram_tensor(self.pfx + name, shape, dt, kind="Internal").ap()

    def finals(self):
        return [(self.h[n], self.n[n]) for n in self.h if self.n[n] > 0]

    def inc(self, ins, name, k=1):
        ins.then_inc(self.sem(name), k)
        self.n[name] += k
        return self.n[name]

    def val(self, name):
        self.sem(name)
        return self.n[name]


def cast_load(g, ctx, sem, dst, src):
    return ctx.inc(g.dma_start(out=dst, in_=src), sem, 16)



def emit_a1(nc, ctx, es, io):
    xin = io["xin"]
    w = io["w_qkv0"]
    bias = io["b_qkv0"]
    q_fm = io["q_fm"]
    k_fm = io["k_fm"]
    v_tok = io["v_tok"]
    wbf = ctx.dram("wbf_qkv0", [D, 3 * D], BF16)
    bbf = ctx.dram("bbf_qkv0", [3 * D], BF16)

    x_sb = es.enter_context(ctx.sbuf("x_sb", [128, NIB, T], BF16))
    ones = es.enter_context(ctx.sbuf("ones", [1, 512], BF16))
    with nc.Block(no_gpsimd_drain=True) as blk:
        @blk.gpsimd
        def _(g):
            cast_load(g, ctx, "xld", x_sb[:], xin.rearrange("(ib p) t -> p ib t", p=128))
            ctx.inc(g.dma_start(out=bbf[:], in_=bias), "wcb", 16)
            for j in range(8):
                cs = slice(j * 1536, (j + 1) * 1536)
                ctx.inc(g.dma_start(out=wbf[:, cs], in_=w[:, cs]), "wc%d" % j, 16)

        @blk.vector
        def _(v):
            ctx.inc(v.memset(ones[:], 1.0), "xld0", 1)

    sub = es
    wt = [sub.enter_context(ctx.sbuf(f"wt{i}", [128, NIB + 1, 128], BF16)) for i in range(2)]
    po = [sub.enter_context(ctx.psum(f"po{i}", [128, 512], F32)) for i in range(2)]
    ob_out = [sub.enter_context(ctx.sbuf(f"qk_out{i}", [128, T], BF16)) for i in range(2)]

    NOB_QK = 64
    with nc.Block(no_gpsimd_drain=True) as blk:
        @blk.sync
        def _(sp):
            for ob in range(NOB_QK):
                if ob >= 2:
                    sp.wait_ge(ctx.sem("pe"), (ob - 1) * NTC)
                sp.wait_ge(ctx.sem("wc%d" % ((ob * 128 + 127) // 1536)), 16)
                sp.wait_ge(ctx.sem("wcb"), 16)
                sem = "wld%d" % (ob % 2)
                ctx.inc(sp.dma_start(out=wt[ob % 2][:, 0:NIB, :],
                                     in_=wbf.rearrange("(ib p) o -> p ib o", p=128)[:, :, ob * 128:(ob + 1) * 128]),
                        sem, 16)
                ctx.inc(sp.dma_start(out=wt[ob % 2][0:1, NIB, :],
                                     in_=bbf[None, ob * 128:(ob + 1) * 128]),
                        sem, 16)

        @blk.tensor
        def _(t):
            t.wait_ge(ctx.sem("xld"), 16)
            t.wait_ge(ctx.sem("xld0"), 1)
            for ob in range(NOB_QK):
                t.wait_ge(ctx.sem("wld%d" % (ob % 2)), 32 * (ob // 2 + 1))
                for tc in range(NTC):
                    bank = (ob * NTC + tc) % 2
                    grp = ob * NTC + tc
                    if grp >= 2:
                        t.wait_ge(ctx.sem("ev"), grp - 1)
                    for ib in range(NIB):
                        t.matmul(po[bank][:], wt[ob % 2][:, ib, :],
                                 x_sb[:, ib, tc * 512:(tc + 1) * 512],
                                 start=(ib == 0), stop=False)
                    ins = t.matmul(po[bank][:], wt[ob % 2][0:1, NIB, :],
                                   ones[0:1, :], start=False, stop=True)
                    ctx.inc(ins, "pe", 1)

        @blk.scalar
        def _(s):
            for ob in range(NOB_QK):
                if ob >= 2:
                    s.wait_ge(ctx.sem("st%d" % (ob % 2)), 16 * (ob // 2))
                for tc in range(NTC):
                    grp = ob * NTC + tc
                    bank = grp % 2
                    s.wait_ge(ctx.sem("pe"), grp + 1)
                    ctx.inc(s.copy(out=ob_out[ob % 2][:, tc * 512:(tc + 1) * 512],
                                   in_=po[bank][:]), "ev", 1)

        @blk.gpsimd
        def _(g):
            for ob in range(NOB_QK):
                g.wait_ge(ctx.sem("ev"), (ob + 1) * NTC)
                dst = q_fm if ob < 32 else k_fm
                row = (ob % 32) * 128
                ctx.inc(g.dma_start(out=dst[row:row + 128, :], in_=ob_out[ob % 2][:]),
                        "st%d" % (ob % 2), 16)


    sub = es
    wv = sub.enter_context(ctx.sbuf("wv", [128, NIB + 1, 512], BF16))
    pv = [sub.enter_context(ctx.psum(f"pv{i}", [128, 512], F32)) for i in range(2)]
    v_out = sub.enter_context(ctx.sbuf("v_out", [128, 16, 512], BF16))

    NOC = 8
    with nc.Block(no_gpsimd_drain=True) as blk:
        @blk.sync
        def _(sp):
            for oc in range(NOC):
                if oc >= 1:
                    sp.wait_ge(ctx.sem("pev"), oc * 16)
                sp.wait_ge(ctx.sem("wc%d" % ((2 * D + oc * 512 + 511) // 1536)), 16)
                sp.wait_ge(ctx.sem("wcb"), 16)
                ctx.inc(sp.dma_start(out=wv[:, 0:NIB, :],
                                     in_=wbf.rearrange("(ib p) o -> p ib o", p=128)[:, :, 2 * D + oc * 512: 2 * D + (oc + 1) * 512]),
                        "wvld", 16)
                ctx.inc(sp.dma_start(out=wv[0:1, NIB, :],
                                     in_=bbf[None, 2 * D + oc * 512: 2 * D + (oc + 1) * 512]),
                        "wvld", 16)

        @blk.tensor
        def _(t):
            for oc in range(NOC):
                t.wait_ge(ctx.sem("wvld"), 32 * (oc + 1))
                for tb in range(16):
                    grp = oc * 16 + tb
                    bank = grp % 2
                    if grp >= 2:
                        t.wait_ge(ctx.sem("evv"), grp - 1)
                    for ib in range(NIB):
                        t.matmul(pv[bank][:], x_sb[:, ib, tb * 128:(tb + 1) * 128],
                                 wv[:, ib, :], start=(ib == 0), stop=False)
                    ins = t.matmul(pv[bank][:], ones[0:1, 0:128],
                                   wv[0:1, NIB, :], start=False, stop=True)
                    ctx.inc(ins, "pev", 1)

        @blk.scalar
        def _(s):
            for oc in range(NOC):
                if oc >= 1:
                    s.wait_ge(ctx.sem("stv"), 16 * oc)
                for tb in range(16):
                    grp = oc * 16 + tb
                    bank = grp % 2
                    s.wait_ge(ctx.sem("pev"), grp + 1)
                    ctx.inc(s.copy(out=v_out[:, tb, :], in_=pv[bank][:]), "evv", 1)

        @blk.gpsimd
        def _(g):
            for oc in range(NOC):
                g.wait_ge(ctx.sem("evv"), (oc + 1) * 16)
                ctx.inc(g.dma_start(
                    out=v_tok.rearrange("(tb p) o -> p tb o", p=128)[:, :, oc * 512:(oc + 1) * 512],
                    in_=v_out[:]), "stv", 16)
            g.wait_ge(ctx.sem("st0"), 16 * (NOB_QK // 2))
            g.wait_ge(ctx.sem("st1"), 16 * (NOB_QK // 2))
            g.wait_ge(ctx.sem("stv"), 16 * NOC)



def build_a1():
    nc = bass.Bass("TRN2", target_bir_lowering=False)
    io = {}
    io["xin"] = nc.dram_tensor("xin", [D, T], F32, kind="ExternalInput").ap()
    io["w_qkv0"] = nc.dram_tensor("w_qkv0", [D, 3 * D], F32, kind="ExternalInput").ap()
    io["b_qkv0"] = nc.dram_tensor("b_qkv0", [3 * D], F32, kind="ExternalInput").ap()
    io["q_fm"] = nc.dram_tensor("q_fm", [D, T], BF16, kind="ExternalOutput").ap()
    io["k_fm"] = nc.dram_tensor("k_fm", [D, T], BF16, kind="ExternalOutput").ap()
    io["v_tok"] = nc.dram_tensor("v_tok", [T, D], BF16, kind="ExternalOutput").ap()
    es = ExitStack()
    ctx = Ctx(nc, es, "1_")
    emit_a1(nc, ctx, es, io)
    return nc




SQ128 = float(1.0 / np.sqrt(128.0))


def emit_a2(nc, ctx, es, io):
    q_fm = io["q_fm"]
    k_fm = io["k_fm"]
    v_tok = io["v_tok"]
    xin = io["xin"]
    w_o = io["w_o0"]
    b_o = io["b_o0"]
    x1 = io["x1"]

    wbf = ctx.dram("wbf_o0", [D, D], BF16)
    bbf = ctx.dram("bbf_o0", [D], BF16)

    NH = 32
    NKB = 32
    NQC = NTC
    NNC = 12
    NEXP = NQC * NKB

    ctx_sb = es.enter_context(ctx.sbuf("ctx_sb", [128, NH, T], BF16))
    ones = es.enter_context(ctx.sbuf("ones", [1, 512], BF16))
    onescol = es.enter_context(ctx.sbuf("onescol", [128, 1], BF16))
    ones32 = es.enter_context(ctx.sbuf("ones32", [1, 128], F32))
    u_mx = es.enter_context(ctx.sbuf("u_mx", [1, 4], F32))
    st1 = ExitStack()
    qh = [st1.enter_context(ctx.sbuf(f"qh{i}", [128, T], BF16)) for i in range(2)]
    kh = [st1.enter_context(ctx.sbuf(f"kh{i}", [128, 2 * T], BF16)) for i in range(2)]
    vh = [st1.enter_context(ctx.sbuf(f"vh{i}", [128, NKB, 128], BF16)) for i in range(2)]
    e_sb = [st1.enter_context(ctx.sbuf(f"e_sb{i}", [128, 1024], BF16)) for i in range(3)]
    sq = [st1.enter_context(ctx.sbuf(f"sq{i}", [128, 512], BF16)) for i in range(2)]
    bias_u = [st1.enter_context(ctx.sbuf(f"bias_u{i}", [128, 1], F32)) for i in range(2)]
    zrow = st1.enter_context(ctx.sbuf("zrow", [1, 512], F32))
    bz_sb = st1.enter_context(ctx.sbuf("bz_sb", [128, 512], F32))

    psc = [es.enter_context(ctx.psum(f"psc{i}", [128, 2, 512], F32)) for i in range(2)]
    pctx = [es.enter_context(ctx.psum(f"pctx{i}", [128, 512], F32)) for i in range(2)]
    pz = [es.enter_context(ctx.psum("pz0", [1, 512], F32))]
    pz.append(pz[0])
    pb = es.enter_context(ctx.psum("pb", [128, 512], F32))
    pbu = pb[:, 0:1]

    NPR = NKB // 2
    NEXP2 = NQC * NPR

    with nc.Block(no_gpsimd_drain=True) as blk:
        @blk.vector
        def _(v):
            v.memset(ones[:], 1.0)
            v.memset(ones32[:], 1.0)
            ctx.inc(v.memset(onescol[:], 1.0), "init", 1)

        @blk.gpsimd
        def _(g):
            ctx.inc(g.dma_start(out=bbf[:], in_=b_o), "wcast", 16)
            for j in range(4):
                cs = slice(j * 1024, (j + 1) * 1024)
                ctx.inc(g.dma_start(out=wbf[:, cs], in_=w_o[:, cs]), "wcast", 16)

    with nc.Block(no_gpsimd_drain=True) as blk:
        @blk.sync
        def _(sp):
            for h in range(NH):
                if h >= 2:
                    sp.wait_ge(ctx.sem("peb"), NQC * (h - 1))
                sem = "hld%d" % (h % 2)
                r = slice(h * 128, (h + 1) * 128)
                ctx.inc(sp.dma_start(out=qh[h % 2][:], in_=q_fm[r, :]), sem, 16)
                ctx.inc(sp.dma_start(out=kh[h % 2][:], in_=k_fm[r, :]), sem, 16)
                ctx.inc(sp.dma_start(
                    out=vh[h % 2][:],
                    in_=v_tok.rearrange("(tb p) o -> p tb o", p=128)[:, :, r]), sem, 16)

        @blk.scalar
        def _(s):
            for h in range(NH):
                s.wait_ge(ctx.sem("hld%d" % (h % 2)), 48 * (h // 2 + 1))
                for j in range(NNC):
                    src, c = (0, j) if j < NQC else (1, j - NQC)
                    buf = qh[h % 2] if src == 0 else kh[h % 2]
                    if h * NNC + j >= 2:
                        s.wait_ge(ctx.sem("pen"), h * NNC + j - 1)
                    ctx.inc(s.activation(out=sq[j % 2][:], in_=buf[:, c * 512:(c + 1) * 512],
                                         func=AF.Square), "actn", 1)
                s.wait_ge(ctx.sem("dven"), h + 1)
                for qc in range(NQC):
                    for pr in range(NPR):
                        i = (h * NQC + qc) * NPR + pr
                        s.wait_ge(ctx.sem("pes"), i + 1)
                        if i >= 3:
                            s.wait_ge(ctx.sem("pez"), i - 2)
                        ctx.inc(s.activation(out=e_sb[i % 3][:], in_=psc[i % 2][:],
                                             func=AF.Exp, bias=bias_u[h % 2][:],
                                             scale=SQ128), "acte", 1)

        @blk.tensor
        def _(t):
            t.wait_ge(ctx.sem("init"), 1)
            for h in range(NH):
                for j in range(NNC):
                    t.wait_ge(ctx.sem("actn"), h * NNC + j + 1)
                    if h * NNC + j >= 2:
                        t.wait_ge(ctx.sem("dvemx"), h * NNC + j - 1)
                    if h >= 1 and j < 2:
                        t.wait_ge(ctx.sem("acte"), h * NEXP2)
                    ctx.inc(t.matmul(psc[j % 2][0:1, 0, :], onescol[:], sq[j % 2][:],
                                     start=True, stop=True), "pen", 1)
                t.wait_ge(ctx.sem("dveu"), h + 1)
                if h >= 1:
                    t.wait_ge(ctx.sem("dven"), h)
                    t.wait_ge(ctx.sem("dvec"), h * NQC)
                ctx.inc(t.matmul(pbu, ones32[0:1, :], u_mx[0:1, 2:3],
                                 start=True, stop=True), "pebu", 1)
                for qc in range(NQC):
                    g = h * NQC + qc
                    qsl = slice(qc * 512, (qc + 1) * 512)
                    for pr in range(NPR + 1):
                        if pr < NPR:
                            i = g * NPR + pr
                            if i >= 2:
                                t.wait_ge(ctx.sem("acte"), i - 1)
                            if pr < 2:
                                t.wait_ge(ctx.sem("dvemx"), (h + 1) * NNC)
                            t.matmul(psc[i % 2][:, 0, :], kh[h % 2][:, (2 * pr) * 128:(2 * pr + 1) * 128],
                                     qh[h % 2][:, qsl], start=True, stop=True)
                            ctx.inc(t.matmul(psc[i % 2][:, 1, :], kh[h % 2][:, (2 * pr + 1) * 128:(2 * pr + 2) * 128],
                                             qh[h % 2][:, qsl], start=True, stop=True), "pes", 1)
                        if pr >= 1:
                            pc_ = pr - 1
                            i = g * NPR + pc_
                            t.wait_ge(ctx.sem("acte"), i + 1)
                            if pc_ == 0 and g >= 2:
                                t.wait_ge(ctx.sem("dvec"), g - 1)
                            if pc_ == 0 and g >= 1:
                                t.wait_ge(ctx.sem("dvez"), 2 * g)
                            t.matmul(pctx[g % 2][:], vh[h % 2][:, 2 * pc_, :], e_sb[i % 3][:, 0:512],
                                     start=(pc_ == 0), stop=False)
                            t.matmul(pctx[g % 2][:], vh[h % 2][:, 2 * pc_ + 1, :], e_sb[i % 3][:, 512:1024],
                                     start=False, stop=(pc_ == NPR - 1))
                            t.matmul(pz[0][:], onescol[:], e_sb[i % 3][:, 0:512],
                                     start=(pc_ == 0), stop=False)
                            ctx.inc(t.matmul(pz[0][:], onescol[:], e_sb[i % 3][:, 512:1024],
                                             start=False, stop=(pc_ == NPR - 1)), "pez", 1)
                    t.wait_ge(ctx.sem("dvez"), 2 * g + 2)
                    if g >= 1:
                        t.wait_ge(ctx.sem("dvec"), g)
                    if qc == 0:
                        t.wait_ge(ctx.sem("dven"), h + 1)
                    ctx.inc(t.matmul(pb[:], ones32[0:1, :], zrow[0:1, :],
                                     start=True, stop=True), "peb", 1)

        @blk.vector
        def _(v):
            for h in range(NH):
                v.memset(u_mx[0:1, 0:1], -1e30)
                for j in range(NNC):
                    v.wait_ge(ctx.sem("pen"), h * NNC + j + 1)
                    v.tensor_reduce(u_mx[0:1, 1:2], psc[j % 2][0:1, 0, :], axis=AX.X, op=ALU.max)
                    v.drain()
                    ctx.inc(v.tensor_tensor(out=u_mx[0:1, 0:1], in0=u_mx[0:1, 0:1],
                                            in1=u_mx[0:1, 1:2], op=ALU.max), "dvemx", 1)
                    v.drain()
                if h >= 1:
                    v.wait_ge(ctx.sem("pebu"), h)
                v.drain()
                ctx.inc(v.tensor_scalar(out=u_mx[0:1, 2:3], in0=u_mx[0:1, 0:1],
                                        scalar1=-SQ128, scalar2=None, op0=ALU.mult), "dveu", 1)
                v.wait_ge(ctx.sem("pebu"), h + 1)
                if h >= 2:
                    v.wait_ge(ctx.sem("acte"), (h - 1) * NEXP2)
                ctx.inc(v.tensor_copy(bias_u[h % 2][:], pbu), "dven", 1)
                for qc in range(NQC):
                    g = h * NQC + qc
                    qsl = slice(qc * 512, (qc + 1) * 512)
                    v.wait_ge(ctx.sem("pez"), (g + 1) * NPR)
                    ctx.inc(v.tensor_scalar(out=zrow[:], in0=pz[0][:],
                                            scalar1=1e-30, scalar2=None, op0=ALU.add), "dvez", 1)
                    v.drain()
                    ctx.inc(v.reciprocal(zrow[:], zrow[:]), "dvez", 1)
                    v.wait_ge(ctx.sem("peb"), g + 1)
                    v.tensor_copy(bz_sb[:], pb[:])
                    v.drain()
                    ctx.inc(v.tensor_tensor(out=ctx_sb[:, h, qsl], in0=pctx[g % 2][:],
                                            in1=bz_sb[:], op=ALU.mult), "dvec", 1)

    NG1 = NH * NQC

    st1.close()
    wo = [es.enter_context(ctx.sbuf(f"wo{i}", [128, NIB + 1, 128], BF16)) for i in range(2)]

    def sync_loads(sp, tc, ob, i):
        if i == 0:
            sp.wait_ge(ctx.sem("wcast"), 16 * 5)
            sp.wait_ge(ctx.sem("dvec"), NG1)
        if i >= 2:
            sp.wait_ge(ctx.sem("Ape2"), i - 1)
        sem = "wld2%d" % (i % 2)
        ctx.inc(sp.dma_start(out=wo[i % 2][:, 0:NIB, :],
                             in_=wbf.rearrange("(ib p) o -> p ib o", p=128)[:, :, ob * 128:(ob + 1) * 128]),
                sem, 16)
        ctx.inc(sp.dma_start(out=wo[i % 2][0:1, NIB, :],
                             in_=bbf[None, ob * 128:(ob + 1) * 128]), sem, 16)

    def pe_mms(t, tc, ob, i, psum):
        if i == 0:
            t.wait_ge(ctx.sem("dvec"), NG1)
        t.wait_ge(ctx.sem("wld2%d" % (i % 2)), 32 * (i // 2 + 1))
        for ib in range(NIB):
            t.matmul(psum[:], wo[i % 2][:, ib, :],
                     ctx_sb[:, ib, tc * 512:(tc + 1) * 512],
                     start=(ib == 0), stop=False)
        return t.matmul(psum[:], wo[i % 2][0:1, NIB, :], ones[0:1, :],
                        start=False, stop=True)

    ln_out_phase(nc, ctx, es, "A", xin, x1, pe_mms, sync_loads,
                 [psc[0][:, 0, :], psc[1][:, 0, :]],
                 [psc[0][0:1, 1, :], psc[1][0:1, 1, :]],
                 pb, pctx[0], ones32, onescol)



def build_a2():
    nc = bass.Bass("TRN2", target_bir_lowering=False)
    io = {}
    io["q_fm"] = nc.dram_tensor("q_fm", [D, T], BF16, kind="ExternalInput").ap()
    io["k_fm"] = nc.dram_tensor("k_fm", [D, 2 * T], BF16, kind="ExternalInput").ap()
    io["v_tok"] = nc.dram_tensor("v_tok", [2 * T, D], BF16, kind="ExternalInput").ap()
    io["xin"] = nc.dram_tensor("xin", [D, T], F32, kind="ExternalInput").ap()
    io["w_o0"] = nc.dram_tensor("w_o0", [D, D], F32, kind="ExternalInput").ap()
    io["b_o0"] = nc.dram_tensor("b_o0", [D], F32, kind="ExternalInput").ap()
    io["x1"] = nc.dram_tensor("x1", [D, T], F32, kind="ExternalOutput").ap()
    es = ExitStack()
    ctx = Ctx(nc, es, "2_")
    emit_a2(nc, ctx, es, io)
    return nc




def ln_out_phase(nc, ctx, es, pfx, xres, xout, pe_mms, sync_loads,
                 psc, pz, pb, pbm, ones32, onescol,
                 pre_sync=None, pre_tensor=None, pre_vector=None,
                 pre_scalar=None, pre_gpsimd=None, first_tensor_wait=None,
                 sc_ob=None, nob=NIB, eps=1e-5, dmodel=None):
    S = lambda n: ctx.sem(pfx + n)
    V = lambda n: ctx.val(pfx + n)
    dmodel = dmodel or (128 * nob)

    xt = [es.enter_context(ctx.sbuf(pfx + f"xt{i}", [128, 512], F32)) for i in range(2)]
    ysb = es.enter_context(ctx.sbuf(pfx + "ysb", [128, nob, 512], BF16))
    y2 = [es.enter_context(ctx.sbuf(pfx + f"y2_{i}", [128, 512], BF16)) for i in range(2)]
    m_row = es.enter_context(ctx.sbuf(pfx + "m_row", [1, 512], F32))
    v_row = es.enter_context(ctx.sbuf(pfx + "v_row", [1, 512], F32))
    m2_row = es.enter_context(ctx.sbuf(pfx + "m2_row", [1, 512], F32))
    ve_row = es.enter_context(ctx.sbuf(pfx + "ve_row", [1, 512], F32))
    s_row = es.enter_context(ctx.sbuf(pfx + "s_row", [1, 512], F32))
    t_row = m2_row
    r_row = es.enter_context(ctx.sbuf(pfx + "r_row", [1, 512], F32))
    mr_row = ve_row
    b1_sb = es.enter_context(ctx.sbuf(pfx + "b1_sb", [128, 512], F32))
    b2_sb = es.enter_context(ctx.sbuf(pfx + "b2_sb", [128, 512], F32))
    xoA1 = es.enter_context(ctx.sbuf(pfx + "xoA", [128, 512], F32))
    xoB = [es.enter_context(ctx.sbuf(pfx + f"xoB{i}", [128, 512], F32)) for i in range(2)]

    with nc.Block(no_gpsimd_drain=True) as blk:
        @blk.sync
        def _(sp):
            for tc in range(NTC):
                if pre_sync:
                    pre_sync(sp, tc)
                for ob in range(nob):
                    i = tc * nob + ob
                    sync_loads(sp, tc, ob, i)
                    if i >= 2:
                        sp.wait_ge(S("dvey"), i - 1)
                    ctx.inc(sp.dma_start(out=xt[i % 2][:],
                                         in_=xres[ob * 128:(ob + 1) * 128, tc * 512:(tc + 1) * 512]),
                            pfx + "xld%d" % (i % 2), 16)

        @blk.tensor
        def _(t):
            if first_tensor_wait:
                first_tensor_wait(t)
            for tc in range(NTC):
                if pre_tensor:
                    pre_tensor(t, tc)
                for ob in range(nob):
                    i = tc * nob + ob
                    if i >= 2:
                        t.wait_ge(S("dvey"), i - 1)
                    ctx.inc(pe_mms(t, tc, ob, i, psc[i % 2]), pfx + "pe2", 1)
                    t.wait_ge(S("dvey"), i + 1)
                    if ob == 0 and tc >= 1:
                        t.wait_ge(S("dvesm"), 2 * tc)
                    t.matmul(pz[0][:], onescol[:], ysb[:, ob, :],
                             start=(ob == 0), stop=(ob == nob - 1))
                    t.wait_ge(S("act2"), i + 1)
                    ctx.inc(t.matmul(pz[1][:], onescol[:], y2[i % 2][:],
                                     start=(ob == 0), stop=(ob == nob - 1)), pfx + "pest", 1)
                t.wait_ge(S("dvesf"), tc + 1)
                if tc >= 1:
                    t.wait_ge(S("dveb"), tc)
                t.matmul(pb[:], ones32[0:1, :], r_row[0:1, :], start=True, stop=True)
                ctx.inc(t.matmul(pbm[:], ones32[0:1, :], mr_row[0:1, :],
                                 start=True, stop=True), pfx + "peb2", 1)

        @blk.vector
        def _(v):
            for tc in range(NTC):
                if pre_vector:
                    pre_vector(v, tc)
                for ob in range(nob):
                    i = tc * nob + ob
                    v.wait_ge(S("pe2"), i + 1)
                    v.wait_ge(S("xld%d" % (i % 2)), 16 * (i // 2 + 1))
                    ctx.inc(v.tensor_tensor(out=ysb[:, ob, :], in0=psc[i % 2][:],
                                            in1=xt[i % 2][:], op=ALU.add), pfx + "dvey", 1)
                v.wait_ge(S("pest"), (tc + 1) * nob)
                v.tensor_scalar(out=m_row[:], in0=pz[0][:], scalar1=1.0 / dmodel, scalar2=None, op0=ALU.mult)
                ctx.inc(v.tensor_scalar(out=v_row[:], in0=pz[1][:], scalar1=1.0 / dmodel,
                                        scalar2=None, op0=ALU.mult), pfx + "dvesm", 2)
                v.drain()
                v.wait_ge(S("actm"), tc + 1)
                v.tensor_tensor(out=ve_row[:], in0=v_row[:], in1=m2_row[:], op=ALU.subtract)
                v.drain()
                ctx.inc(v.tensor_scalar(out=ve_row[:], in0=ve_row[:], scalar1=eps,
                                        scalar2=None, op0=ALU.add), pfx + "dvev", 1)
                v.wait_ge(S("acts2"), tc + 1)
                v.reciprocal(t_row[:], s_row[:])
                v.drain()
                v.tensor_tensor(out=t_row[:], in0=ve_row[:], in1=t_row[:], op=ALU.mult)
                v.drain()
                v.tensor_tensor(out=t_row[:], in0=t_row[:], in1=s_row[:], op=ALU.add)
                v.drain()
                v.tensor_scalar(out=t_row[:], in0=t_row[:], scalar1=0.5, scalar2=None, op0=ALU.mult)
                v.drain()
                v.reciprocal(r_row[:], t_row[:])
                v.drain()
                ctx.inc(v.scalar_tensor_tensor(out=mr_row[:], in0=m_row[:], scalar=-1.0,
                                               in1=r_row[:], op0=ALU.mult, op1=ALU.mult), pfx + "dvesf", 1)
                v.wait_ge(S("peb2"), tc + 1)
                v.tensor_copy(b1_sb[:], pb[:])
                ctx.inc(v.tensor_copy(b2_sb[:], pbm[:]), pfx + "dveb", 1)
                v.drain()
                for ob in range(nob):
                    i = tc * nob + ob
                    if i >= 2:
                        v.wait_ge(S("st2%d" % (i % 2)), 16 * (i // 2))
                    v.tensor_tensor(out=xoA1[:], in0=ysb[:, ob, :], in1=b1_sb[:], op=ALU.mult)
                    v.drain()
                    ctx.inc(v.tensor_tensor(out=xoB[ob % 2][:], in0=xoA1[:],
                                            in1=b2_sb[:], op=ALU.add), pfx + "dvex", 1)
                    v.drain()

        @blk.scalar
        def _(s):
            for tc in range(NTC):
                if pre_scalar:
                    pre_scalar(s, tc)
                for ob in range(nob):
                    i = tc * nob + ob
                    if sc_ob:
                        sc_ob(s, tc, ob, i)
                    s.wait_ge(S("dvey"), i + 1)
                    if i >= 2:
                        s.wait_ge(S("pest"), i - 1)
                    ctx.inc(s.activation(out=y2[i % 2][:], in_=ysb[:, ob, :], func=AF.Square), pfx + "act2", 1)
                s.wait_ge(S("dvesm"), 2 * (tc + 1))
                ctx.inc(s.activation(out=m2_row[:], in_=m_row[:], func=AF.Square), pfx + "actm", 1)
                s.wait_ge(S("dvev"), tc + 1)
                ctx.inc(s.activation(out=s_row[:], in_=ve_row[:], func=AF.Sqrt), pfx + "acts2", 1)

        @blk.gpsimd
        def _(g):
            for tc in range(NTC):
                if pre_gpsimd:
                    pre_gpsimd(g, tc)
                for ob in range(nob):
                    i = tc * nob + ob
                    g.wait_ge(S("dvex"), i + 1)
                    ctx.inc(g.dma_start(out=xout[ob * 128:(ob + 1) * 128, tc * 512:(tc + 1) * 512],
                                        in_=xoB[ob % 2][:]), pfx + "st2%d" % (i % 2), 16)
            g.wait_ge(S("st20"), 16 * NTC * nob // 2)
            g.wait_ge(S("st21"), 16 * NTC * nob // 2)



def emit_b(nc, ctx, es, io):
    x1 = io["x1"]
    w_base = io["w_base"]
    b_base = io["b_base"]
    w_A = io["w_A"]
    w_B = io["w_B"]
    w_router = io["w_router"]
    x2 = io["x2"]

    wbf = ctx.dram("wbf_base", [D, D], BF16)
    bbf = ctx.dram("bbf_base", [D], BF16)
    btbf = ctx.dram("btbf", [8, 128, D], BF16)
    x1bf = ctx.dram("x1bf", [D, T], BF16)

    ones = es.enter_context(ctx.sbuf("ones", [1, 512], BF16))
    onescol = es.enter_context(ctx.sbuf("onescol", [128, 1], BF16))
    ones32 = es.enter_context(ctx.sbuf("ones32", [1, 128], F32))
    ident = es.enter_context(ctx.sbuf("ident", [128, 128], F32))
    abf_sb = es.enter_context(ctx.sbuf("abf_sb", [128, NIB, 64], BF16))
    rbf_sb = es.enter_context(ctx.sbuf("rbf_sb", [128, NIB, 16], BF16))
    rh_sb = es.enter_context(ctx.sbuf("rh_sb", [64, T], BF16))
    ctT_sb = es.enter_context(ctx.sbuf("ctT_sb", [16, T], F32))
    crow = [es.enter_context(ctx.sbuf(f"crow{i}", [1, 512], BF16)) for i in range(2)]

    ident_c = ctx.inline(np.eye(128, dtype=np.float32), "ident_c")

    psc = [es.enter_context(ctx.psum(f"psc{i}", [128, 512], F32)) for i in range(2)]
    pz = [es.enter_context(ctx.psum(f"pz{i}", [1, 512], F32)) for i in range(2)]
    pb = es.enter_context(ctx.psum("pb", [128, 512], F32))
    pbm = es.enter_context(ctx.psum("pbm", [128, 512], F32))
    pbc = [es.enter_context(ctx.psum(f"pbc{i}", [64, 512], F32)) for i in range(2)]
    with nc.Block(no_gpsimd_drain=True) as blk:
        @blk.vector
        def _(v):
            v.memset(ones[:], 1.0)
            v.memset(ones32[:], 1.0)
            ctx.inc(v.memset(onescol[:], 1.0), "init", 1)

        @blk.gpsimd
        def _(g):
            ctx.inc(g.dma_start(out=bbf[:], in_=b_base), "wcast", 16)
            for j in range(4):
                cs = slice(j * 1024, (j + 1) * 1024)
                ctx.inc(g.dma_start(out=wbf[:, cs], in_=w_base[:, cs]), "wcast", 16)
            ctx.inc(g.dma_start(out=btbf.rearrange("k p o -> p k o"),
                                in_=w_B.rearrange("k p o -> p k o")), "wcast", 16)
            ctx.inc(g.dma_start(out=abf_sb[:],
                                in_=w_A.rearrange("(ib p) r -> p ib r", p=128)), "wcast", 16)
            ctx.inc(g.dma_start(out=rbf_sb[:],
                                in_=w_router.rearrange("(ib p) e -> p ib e", p=128)), "wcast", 16)
            ctx.inc(g.dma_start(out=ident[:], in_=ident_c), "wcast", 16)
            for j in range(4):
                rs = slice(j * 1024, (j + 1) * 1024)
                ctx.inc(g.dma_start(out=x1bf[rs, :], in_=x1[rs, :]), "wcast", 16)

    st1 = ExitStack()
    xb = [st1.enter_context(ctx.sbuf(f"xb{i}", [128, NIB, 512], BF16)) for i in range(2)]
    l_sb = st1.enter_context(ctx.sbuf("l_sb", [128, 16], F32))
    pe_sb = st1.enter_context(ctx.sbuf("pe_sb", [128, 16], F32))
    pn_sb = st1.enter_context(ctx.sbuf("pn_sb", [128, 16], F32))
    pm_sb = st1.enter_context(ctx.sbuf("pm_sb", [128, 16], F32))
    mk1_sb = st1.enter_context(ctx.sbuf("mk1_sb", [128, 16], F32))
    mk2_sb = st1.enter_context(ctx.sbuf("mk2_sb", [128, 16], F32))
    comb_sb = st1.enter_context(ctx.sbuf("comb_sb", [128, 16], F32))
    cstat = st1.enter_context(ctx.sbuf("cstat", [128, 6], F32))

    with nc.Block(no_gpsimd_drain=True) as blk:
        @blk.sync
        def _(sp):
            sp.wait_ge(ctx.sem("wcast"), 16 * 13)
            for tc in range(NTC):
                if tc >= 2:
                    sp.wait_ge(ctx.sem("pep"), (tc - 1) * 5)
                ctx.inc(sp.dma_start(
                    out=xb[tc % 2][:],
                    in_=x1bf.rearrange("(ib p) t -> p ib t", p=128)[:, :, tc * 512:(tc + 1) * 512]),
                    "xbld%d" % (tc % 2), 16)

        @blk.tensor
        def _(t):
            t.wait_ge(ctx.sem("init"), 1)
            for tc in range(NTC):
                t.wait_ge(ctx.sem("xbld%d" % (tc % 2)), 16 * (tc // 2 + 1))
                if tc >= 1:
                    t.wait_ge(ctx.sem("dvep"), tc)
                ins = None
                for ib in range(NIB):
                    ins = t.matmul(psc[0][0:64, :], abf_sb[:, ib, :], xb[tc % 2][:, ib, :],
                                   start=(ib == 0), stop=(ib == NIB - 1))
                ctx.inc(ins, "pep", 1)
                for tt in range(4):
                    g = 4 * tc + tt
                    if g >= 1:
                        t.wait_ge(ctx.sem("dvel"), g)
                    ins = None
                    for ib in range(NIB):
                        ins = t.matmul(psc[1][:, 0:16], xb[tc % 2][:, ib, tt * 128:(tt + 1) * 128],
                                       rbf_sb[:, ib, :], start=(ib == 0), stop=(ib == NIB - 1))
                    ctx.inc(ins, "pep", 1)
                    t.wait_ge(ctx.sem("dvec2"), g + 1)
                    if g >= 1:
                        t.wait_ge(ctx.sem("dvet"), g)
                    ctx.inc(t.transpose(pb[0:16, 0:128], comb_sb[:], ident[:]), "pept", 1)

        @blk.vector
        def _(v):
            for tc in range(NTC):
                v.wait_ge(ctx.sem("pep"), tc * 5 + 1)
                ctx.inc(v.tensor_copy(rh_sb[:, tc * 512:(tc + 1) * 512], psc[0][0:64, :]), "dvep", 1)
                for tt in range(4):
                    g = 4 * tc + tt
                    v.wait_ge(ctx.sem("pep"), tc * 5 + tt + 2)
                    v.tensor_reduce(cstat[:, 0:1], psc[1][:, 0:16], axis=AX.X, op=ALU.max)
                    v.drain()
                    v.tensor_scalar(out=cstat[:, 1:2], in0=cstat[:, 0:1], scalar1=-1.0, scalar2=None, op0=ALU.mult)
                    v.drain()
                    ctx.inc(v.tensor_copy(l_sb[:], psc[1][:, 0:16]), "dvel", 1)
                    v.wait_ge(ctx.sem("actp"), g + 1)
                    v.reciprocal(cstat[:, 2:3], cstat[:, 3:4])
                    v.drain()
                    v.tensor_scalar(out=pn_sb[:], in0=pe_sb[:], scalar1=cstat[:, 2:3], scalar2=None, op0=ALU.mult)
                    v.drain()
                    v.tensor_reduce(cstat[:, 0:1], pn_sb[:], axis=AX.X, op=ALU.max)
                    v.drain()
                    v.tensor_scalar(out=mk1_sb[:], in0=pn_sb[:], scalar1=cstat[:, 0:1], scalar2=None, op0=ALU.is_equal)
                    v.drain()
                    v.scalar_tensor_tensor(out=pm_sb[:], in0=mk1_sb[:], scalar=-1e30, in1=pn_sb[:],
                                           op0=ALU.mult, op1=ALU.add)
                    v.drain()
                    v.tensor_reduce(cstat[:, 1:2], pm_sb[:], axis=AX.X, op=ALU.max)
                    v.drain()
                    v.tensor_scalar(out=mk2_sb[:], in0=pm_sb[:], scalar1=cstat[:, 1:2], scalar2=None, op0=ALU.is_equal)
                    v.drain()
                    v.tensor_tensor(out=cstat[:, 2:3], in0=cstat[:, 0:1], in1=cstat[:, 1:2], op=ALU.add)
                    v.drain()
                    v.tensor_scalar(out=cstat[:, 2:3], in0=cstat[:, 2:3], scalar1=1e-6, scalar2=None, op0=ALU.add)
                    v.drain()
                    v.reciprocal(cstat[:, 2:3], cstat[:, 2:3])
                    v.drain()
                    v.tensor_scalar(out=cstat[:, 4:5], in0=cstat[:, 0:1], scalar1=cstat[:, 2:3],
                                    scalar2=0.5, op0=ALU.mult, op1=ALU.mult)
                    v.drain()
                    v.tensor_scalar(out=cstat[:, 5:6], in0=cstat[:, 1:2], scalar1=cstat[:, 2:3],
                                    scalar2=0.5, op0=ALU.mult, op1=ALU.mult)
                    v.drain()
                    v.tensor_scalar(out=comb_sb[:], in0=mk1_sb[:], scalar1=cstat[:, 4:5], scalar2=None, op0=ALU.mult)
                    v.drain()
                    ctx.inc(v.scalar_tensor_tensor(out=comb_sb[:], in0=mk2_sb[:], scalar=cstat[:, 5:6],
                                                   in1=comb_sb[:], op0=ALU.mult, op1=ALU.add), "dvec2", 1)
                    v.wait_ge(ctx.sem("pept"), g + 1)
                    ctx.inc(v.tensor_copy(ctT_sb[:, tc * 512 + tt * 128: tc * 512 + (tt + 1) * 128],
                                          pb[0:16, 0:128]), "dvet", 1)

        @blk.scalar
        def _(s):
            for tc in range(NTC):
                for tt in range(4):
                    g = 4 * tc + tt
                    s.wait_ge(ctx.sem("dvel"), g + 1)
                    ctx.inc(s.activation(out=pe_sb[:], in_=l_sb[:], func=AF.Exp,
                                         bias=cstat[:, 1:2], scale=1.0,
                                         accum_out=cstat[:, 3:4]), "actp", 1)

    st1.close()
    wsl = [es.enter_context(ctx.sbuf(f"wsl{i}", [128, NIB + 9, 128], BF16)) for i in range(2)]
    xb2 = [es.enter_context(ctx.sbuf(f"xb2_{i}", [128, NIB, 512], BF16)) for i in range(2)]
    h_sb = [es.enter_context(ctx.sbuf(f"h_sb{i}", [128, 8, 512], BF16)) for i in range(2)]

    def sync_loads(sp, tc, ob, i):
        if i >= 2:
            sp.wait_ge(ctx.sem("Bpe2"), i - 1)
        sem = "Bwld%d" % (i % 2)
        ctx.inc(sp.dma_start(out=wsl[i % 2][:, 0:NIB, :],
                             in_=wbf.rearrange("(ib p) o -> p ib o", p=128)[:, :, ob * 128:(ob + 1) * 128]),
                sem, 16)
        ctx.inc(sp.dma_start(out=wsl[i % 2][:, NIB:NIB + 8, :],
                             in_=btbf.rearrange("k p o -> p k o")[:, :, ob * 128:(ob + 1) * 128]),
                sem, 16)
        ctx.inc(sp.dma_start(out=wsl[i % 2][0:1, NIB + 8, :],
                             in_=bbf[None, ob * 128:(ob + 1) * 128]), sem, 16)

    def pre_sync(sp, tc):
        if tc == 0:
            sp.wait_ge(ctx.sem("dvet"), 16)
        if tc >= 2:
            sp.wait_ge(ctx.sem("Bpe2"), (tc - 1) * NIB)
        ctx.inc(sp.dma_start(
            out=xb2[tc % 2][:],
            in_=x1bf.rearrange("(ib p) t -> p ib t", p=128)[:, :, tc * 512:(tc + 1) * 512]),
            "Bxbld%d" % (tc % 2), 16)

    def pre_gpsimd(g, tc):
        for e in range(16):
            ge = tc * 16 + e
            if ge == 0:
                g.wait_ge(ctx.sem("dvet"), 16)
            if ge >= 2:
                g.wait_ge(ctx.sem("Bpeh"), ge - 1)
            ctx.inc(g.dma_start(out=crow[e % 2][:],
                                in_=ctT_sb[e:e + 1, tc * 512:(tc + 1) * 512]), "crw%d" % (ge % 2), 16)

    def pre_tensor(t, tc):
        for e in range(16):
            ge = tc * 16 + e
            t.wait_ge(ctx.sem("crw%d" % (ge % 2)), 16 * (ge // 2 + 1))
            if ge >= 2:
                t.wait_ge(ctx.sem("Bdveh"), ge - 1)
            ctx.inc(t.matmul(pbc[ge % 2][:], ones[0:1, 0:64], crow[e % 2][0:1, :],
                             start=True, stop=True), "Bpeh", 1)

    def pre_vector(v, tc):
        for e in range(16):
            ge = tc * 16 + e
            v.wait_ge(ctx.sem("Bpeh"), ge + 1)
            if tc >= 2 and e == 0:
                v.wait_ge(ctx.sem("Bpe2"), (tc - 1) * NIB)
            ctx.inc(v.tensor_tensor(out=h_sb[tc % 2][(e % 2) * 64:(e % 2) * 64 + 64, e // 2, :],
                                    in0=rh_sb[0:64, tc * 512:(tc + 1) * 512],
                                    in1=pbc[ge % 2][:],
                                    op=ALU.mult), "Bdveh", 1)

    def pe_mms(t, tc, ob, i, psum):
        t.wait_ge(ctx.sem("Bwld%d" % (i % 2)), 48 * (i // 2 + 1))
        t.wait_ge(ctx.sem("Bxbld%d" % (tc % 2)), 16 * (tc // 2 + 1))
        t.wait_ge(ctx.sem("Bdveh"), (tc + 1) * 16)
        for ib in range(NIB):
            t.matmul(psum[:], wsl[i % 2][:, ib, :], xb2[tc % 2][:, ib, :],
                     start=(ib == 0), stop=False)
        for kb in range(8):
            t.matmul(psum[:], wsl[i % 2][:, NIB + kb, :], h_sb[tc % 2][:, kb, :],
                     start=False, stop=False)
        return t.matmul(psum[:], wsl[i % 2][0:1, NIB + 8, :], ones[0:1, :],
                        start=False, stop=True)

    def first_wait(t):
        t.wait_ge(ctx.sem("dvel"), 16)
        t.wait_ge(ctx.sem("dvet"), 16)
        t.wait_ge(ctx.sem("dvep"), 4)

    ln_out_phase(nc, ctx, es, "B", x1, x2, pe_mms, sync_loads,
                 psc, pz, pb, pbm, ones32, onescol,
                 pre_sync=pre_sync, pre_tensor=pre_tensor, pre_vector=pre_vector,
                 pre_gpsimd=pre_gpsimd, first_tensor_wait=first_wait)


def build_b():
    nc = bass.Bass("TRN2", target_bir_lowering=False)
    io = {}
    io["x1"] = nc.dram_tensor("x1", [D, T], F32, kind="ExternalInput").ap()
    io["w_base"] = nc.dram_tensor("w_base", [D, D], F32, kind="ExternalInput").ap()
    io["b_base"] = nc.dram_tensor("b_base", [D], F32, kind="ExternalInput").ap()
    io["w_A"] = nc.dram_tensor("w_A", [D, 64], F32, kind="ExternalInput").ap()
    io["w_B"] = nc.dram_tensor("w_B", [8, 128, D], F32, kind="ExternalInput").ap()
    io["w_router"] = nc.dram_tensor("w_router", [D, 16], F32, kind="ExternalInput").ap()
    io["x2"] = nc.dram_tensor("x2", [D, T], F32, kind="ExternalOutput").ap()
    es = ExitStack()
    ctx = Ctx(nc, es, "B_")
    emit_b(nc, ctx, es, io)
    return nc




def emit_c(nc, ctx, es, io):
    xin = io["x2"]
    w = io["w_qkv1"]
    q_fm = io["q1_fm"]
    k_fm = io["k1_fm"]
    v_tok = io["v1_tok"]
    g2 = io.get("gctx")
    wbf = io.get("wbf_qkv1") or ctx.dram("wbf_qkv1", [D, 3 * D], BF16)
    if g2 is not None:
        wcw = lambda sp, j: sp.wait_ge(g2.sem("Cwc%d" % j), 16)
    else:
        wcw = lambda sp, j: sp.wait_ge(ctx.sem("wc%d" % j), 16)

    x_sb = es.enter_context(ctx.sbuf("x_sb", [128, NIB, T], BF16))
    with nc.Block(no_gpsimd_drain=True) as blk:
        @blk.gpsimd
        def _(g):
            ctx.inc(g.dma_start(out=x_sb[:], in_=xin.rearrange("(ib p) t -> p ib t", p=128)), "xld", 16)
            if g2 is None:
                for j in range(8):
                    cs = slice(j * 1536, (j + 1) * 1536)
                    ctx.inc(g.dma_start(out=wbf[:, cs], in_=w[:, cs]), "wc%d" % j, 16)

    wt = [es.enter_context(ctx.sbuf(f"wt{i}", [128, NIB, 128], BF16)) for i in range(2)]
    po = [es.enter_context(ctx.psum(f"po{i}", [128, 512], F32)) for i in range(2)]
    ob_out = [es.enter_context(ctx.sbuf(f"qk_out{i}", [128, T], BF16)) for i in range(2)]

    NOB_QK = 64
    with nc.Block(no_gpsimd_drain=True) as blk:
        @blk.sync
        def _(sp):
            for ob in range(NOB_QK):
                h, which = ob // 2, ob % 2
                off = h * 384 + which * 128
                if ob >= 2:
                    sp.wait_ge(ctx.sem("pe"), (ob - 1) * NTC)
                wcw(sp, (off + 127) // 1536)
                ctx.inc(sp.dma_start(out=wt[ob % 2][:],
                                     in_=wbf.rearrange("(ib p) o -> p ib o", p=128)[:, :, off:off + 128]),
                        "wld%d" % (ob % 2), 16)

        @blk.tensor
        def _(t):
            t.wait_ge(ctx.sem("xld"), 16)
            for ob in range(NOB_QK):
                t.wait_ge(ctx.sem("wld%d" % (ob % 2)), 16 * (ob // 2 + 1))
                for tc in range(NTC):
                    grp = ob * NTC + tc
                    bank = grp % 2
                    if grp >= 2:
                        t.wait_ge(ctx.sem("ev"), grp - 1)
                    ins = None
                    for ib in range(NIB):
                        ins = t.matmul(po[bank][:], wt[ob % 2][:, ib, :],
                                       x_sb[:, ib, tc * 512:(tc + 1) * 512],
                                       start=(ib == 0), stop=(ib == NIB - 1))
                    ctx.inc(ins, "pe", 1)

        @blk.scalar
        def _(s):
            for ob in range(NOB_QK):
                if ob >= 2:
                    s.wait_ge(ctx.sem("st%d" % (ob % 2)), 16 * (ob // 2))
                for tc in range(NTC):
                    grp = ob * NTC + tc
                    bank = grp % 2
                    s.wait_ge(ctx.sem("pe"), grp + 1)
                    ctx.inc(s.activation(out=ob_out[ob % 2][:, tc * 512:(tc + 1) * 512],
                                         in_=po[bank][:], func=AF.Silu), "ev", 1)

        @blk.gpsimd
        def _(g):
            for ob in range(NOB_QK):
                g.wait_ge(ctx.sem("ev"), (ob + 1) * NTC)
                h, which = ob // 2, ob % 2
                dst = q_fm if which == 0 else k_fm
                row = h * 128
                ctx.inc(g.dma_start(out=dst[row:row + 128, :], in_=ob_out[ob % 2][:]),
                        "st%d" % (ob % 2), 16)

    wv = es.enter_context(ctx.sbuf("wv", [128, NIB, 4, 128], BF16))
    pv = [es.enter_context(ctx.psum(f"pv{i}", [128, 512], F32)) for i in range(2)]
    v_out = es.enter_context(ctx.sbuf("v_out", [128, 16, 512], BF16))

    NOC = 8
    wr = wbf.rearrange("i (h x) -> i h x", h=32)
    with nc.Block(no_gpsimd_drain=True) as blk:
        @blk.sync
        def _(sp):
            for oc in range(NOC):
                if oc >= 1:
                    sp.wait_ge(ctx.sem("pev"), oc * 16)
                wcw(sp, 7)
                wcw(sp, (4 * oc * 384 + 256) // 1536)
                for hh in range(4):
                    ctx.inc(sp.dma_start(
                        out=wv[:, :, hh, :],
                        in_=wr.rearrange("(ib p) h x -> p ib h x", p=128)[:, :, 4 * oc + hh, 256:384]),
                        "wvld", 16)

        @blk.tensor
        def _(t):
            for oc in range(NOC):
                t.wait_ge(ctx.sem("wvld"), 64 * (oc + 1))
                for tb in range(16):
                    grp = oc * 16 + tb
                    bank = grp % 2
                    if grp >= 2:
                        t.wait_ge(ctx.sem("evv"), grp - 1)
                    ins = None
                    for ib in range(NIB):
                        ins = t.matmul(pv[bank][:], x_sb[:, ib, tb * 128:(tb + 1) * 128],
                                       wv[:, ib, :, :], start=(ib == 0), stop=(ib == NIB - 1))
                    ctx.inc(ins, "pev", 1)

        @blk.scalar
        def _(s):
            for oc in range(NOC):
                if oc >= 1:
                    s.wait_ge(ctx.sem("stv"), 64 * oc)
                for tb in range(16):
                    grp = oc * 16 + tb
                    bank = grp % 2
                    s.wait_ge(ctx.sem("pev"), grp + 1)
                    ctx.inc(s.activation(out=v_out[:, tb, :], in_=pv[bank][:], func=AF.Silu), "evv", 1)

        @blk.gpsimd
        def _(g):
            for oc in range(NOC):
                g.wait_ge(ctx.sem("evv"), (oc + 1) * 16)
                for hh in range(4):
                    ctx.inc(g.dma_start(
                        out=v_tok.rearrange("(tb p) o -> p tb o", p=128)[:, :, (4 * oc + hh) * 128:(4 * oc + hh + 1) * 128],
                        in_=v_out[:, :, hh * 128:(hh + 1) * 128]), "stv", 16)
            g.wait_ge(ctx.sem("st0"), 16 * (NOB_QK // 2))
            g.wait_ge(ctx.sem("st1"), 16 * (NOB_QK // 2))
            g.wait_ge(ctx.sem("stv"), 64 * NOC)



def build_c():
    nc = bass.Bass("TRN2", target_bir_lowering=False)
    io = {}
    io["x2"] = nc.dram_tensor("x2", [D, T], F32, kind="ExternalInput").ap()
    io["w_qkv1"] = nc.dram_tensor("w_qkv1", [D, 3 * D], F32, kind="ExternalInput").ap()
    io["q1_fm"] = nc.dram_tensor("q1_fm", [D, T], BF16, kind="ExternalOutput").ap()
    io["k1_fm"] = nc.dram_tensor("k1_fm", [D, T], BF16, kind="ExternalOutput").ap()
    io["v1_tok"] = nc.dram_tensor("v1_tok", [T, D], BF16, kind="ExternalOutput").ap()
    es = ExitStack()
    ctx = Ctx(nc, es, "C_")
    emit_c(nc, ctx, es, io)
    return nc



def _l1_decays():
    HEADS, BLOCK = 32, 256
    LAYER_IDX, N_LAYERS = 1, 32
    base = 1.0 / 2.0 ** (8.0 / HEADS)
    exponent = np.arange(HEADS, dtype=np.float64) + 1.0
    factor = 1.0 - LAYER_IDX / (N_LAYERS - 1 + 1e-05) + 1e-05
    slope = (base ** exponent * factor)
    r = np.arange(BLOCK, dtype=np.float64) + 1.0
    q_decay = np.exp(-slope[:, None] * r[None, :])
    k_decay = np.exp(-slope[:, None] * (BLOCK - r)[None, :])
    diff = r[:, None] - r[None, :]
    diag = slope[:, None, None] * diff[None, :, :]
    dd = np.where(diag >= 0, np.exp(-np.maximum(diag, 0.0)), 0.0)
    block_decay = np.exp(-slope * BLOCK)
    return (q_decay.astype(np.float32), k_decay.astype(np.float32),
            dd.astype(np.float32), block_decay.astype(np.float32))



def emit_d(nc, ctx, es, io):
    q_fm = io["q1_fm"]
    k_fm = io["k1_fm"]
    v_tok = io["v1_tok"]
    kv_init = io["kv_init"]
    attn_out = io["attn_d"]
    kv8_out = io["kv8_out"]

    qd, kd, dd, bd = _l1_decays()
    dct = np.ascontiguousarray(np.transpose(dd, (0, 2, 1)).reshape(32, 2, 128, 256))
    dct_c = ctx.inline(dct, "dct_c")
    qd_t = np.ascontiguousarray(np.tile(qd, (1, 8)))
    qd_c = ctx.inline(qd_t, "qd_c")
    kdT = np.ascontiguousarray(np.transpose(kd.reshape(32, 2, 128), (2, 1, 0)))
    kd_c = ctx.inline(kdT, "kd_c")
    ident_c = ctx.inline(np.eye(128, dtype=ml_dtypes.bfloat16).astype(ml_dtypes.bfloat16), "ident_cb")

    NB = 8

    attn_sb = es.enter_context(ctx.sbuf("attn_sb", [128, 32, T], BF16))
    ones32 = es.enter_context(ctx.sbuf("ones32", [1, 128], F32))
    ident = es.enter_context(ctx.sbuf("ident", [128, 128], BF16))
    kd_sb = es.enter_context(ctx.sbuf("kd_sb", [128, 2, 32], F32))
    qh = [es.enter_context(ctx.sbuf(f"qh{i}", [128, T], BF16)) for i in range(2)]
    kh = [es.enter_context(ctx.sbuf(f"kh{i}", [128, T], BF16)) for i in range(2)]
    vh = [es.enter_context(ctx.sbuf(f"vh{i}", [128, 16, 128], BF16)) for i in range(2)]
    dct_sb = [es.enter_context(ctx.sbuf(f"dct_sb{i}", [128, 2, 256], F32)) for i in range(2)]
    qdr = [es.enter_context(ctx.sbuf(f"qdr{i}", [1, T], F32)) for i in range(2)]
    kvi = [es.enter_context(ctx.sbuf(f"kvi{i}", [128, 128], F32)) for i in range(2)]
    qp_sb = es.enter_context(ctx.sbuf("qp_sb", [128, T], BF16))
    msk = es.enter_context(ctx.sbuf("msk", [128, 2, 256], BF16))
    ckp = es.enter_context(ctx.sbuf("ckp", [128, 2, 128], BF16))
    kvf = [es.enter_context(ctx.sbuf(f"kvf{i}", [128, 128], F32)) for i in range(2)]
    kvb = es.enter_context(ctx.sbuf("kvb", [128, 128], BF16))

    pqp = es.enter_context(ctx.psum("pqp", [128, 512], F32))
    ps2 = [es.enter_context(ctx.psum(f"ps2_{i}", [128, 2, 256], F32)) for i in range(2)]
    ptr = es.enter_context(ctx.psum("ptr", [128, 2, 128], BF16))
    po = [es.enter_context(ctx.psum(f"po{i}", [128, 256], F32)) for i in range(2)]
    pkv = es.enter_context(ctx.psum("pkv", [128, 128], F32))
    with nc.Block(no_gpsimd_drain=True) as blk:
        @blk.vector
        def _(v):
            ctx.inc(v.memset(ones32[:], 1.0), "init", 1)

        @blk.gpsimd
        def _(g):
            ctx.inc(g.dma_start(out=ident[:], in_=ident_c), "cld", 16)
            ctx.inc(g.dma_start(out=kd_sb[:], in_=kd_c), "cld", 16)

    with nc.Block(no_gpsimd_drain=True) as blk:
        @blk.sync
        def _(sp):
            for h in range(32):
                if h >= 2:
                    sp.wait_ge(ctx.sem("pek"), (h - 1) * NB)
                sem = "hld%d" % (h % 2)
                r = slice(h * 128, (h + 1) * 128)
                ctx.inc(sp.dma_start(out=qh[h % 2][:], in_=q_fm[r, :]), sem, 16)
                ctx.inc(sp.dma_start(out=kh[h % 2][:], in_=k_fm[r, :]), sem, 16)
                ctx.inc(sp.dma_start(out=vh[h % 2][:],
                                     in_=v_tok.rearrange("(tb p) o -> p tb o", p=128)[:, :, r]), sem, 16)
                ctx.inc(sp.dma_start(out=dct_sb[h % 2][:], in_=dct_c[h].rearrange("kb p q -> p kb q")), sem, 16)
                ctx.inc(sp.dma_start(out=qdr[h % 2][:], in_=qd_c[h][None, :]), sem, 16)
                ctx.inc(sp.dma_start(out=kvi[h % 2][:], in_=kv_init[h]), sem, 16)

        @blk.tensor
        def _(t):
            t.wait_ge(ctx.sem("init"), 1)
            t.wait_ge(ctx.sem("cld"), 32)
            for h in range(32):
                t.wait_ge(ctx.sem("hld%d" % (h % 2)), 96 * (h // 2 + 1))
                for c in range(NTC):
                    if h * NTC + c >= 1:
                        t.wait_ge(ctx.sem("dveq"), h * NTC + c)
                    ctx.inc(t.matmul(pqp[:], ones32[0:1, :], qdr[h % 2][0:1, c * 512:(c + 1) * 512],
                                     start=True, stop=True), "peq", 1)
                for n in range(NB):
                    gi = h * NB + n
                    if gi >= 2:
                        t.wait_ge(ctx.sem("dvem"), 2 * (gi - 1))
                    t.wait_ge(ctx.sem("dveq"), h * NTC + NTC)
                    t.matmul(ps2[gi % 2][:, 0, :], kh[h % 2][:, n * 256:n * 256 + 128],
                             qh[h % 2][:, n * 256:(n + 1) * 256], start=True, stop=False)
                    ctx.inc(t.matmul(ps2[gi % 2][:, 1, :], kh[h % 2][:, n * 256 + 128:(n + 1) * 256],
                                     qh[h % 2][:, n * 256:(n + 1) * 256], start=False, stop=True), "pes", 1)
                    if gi >= 1:
                        t.wait_ge(ctx.sem("dvec"), 2 * gi)
                    t.transpose(ptr[:, 0, :], kh[h % 2][:, n * 256:n * 256 + 128], ident[:])
                    ctx.inc(t.transpose(ptr[:, 1, :], kh[h % 2][:, n * 256 + 128:(n + 1) * 256],
                                        ident[:]), "pet", 1)
                    t.wait_ge(ctx.sem("dvem"), 2 * gi + 2)
                    if gi >= 2:
                        t.wait_ge(ctx.sem("dveo"), gi - 1)
                    t.matmul(po[gi % 2][:], vh[h % 2][:, 2 * n, :], msk[:, 0, :],
                             start=True, stop=False)
                    t.matmul(po[gi % 2][:], vh[h % 2][:, 2 * n + 1, :], msk[:, 1, :],
                             start=False, stop=False)
                    t.wait_ge(ctx.sem("dvek"), h * (NB + 1) + n + 1)
                    ctx.inc(t.matmul(po[gi % 2][:], kvb[:], qp_sb[:, n * 256:(n + 1) * 256],
                                     start=False, stop=True), "peo", 1)
                    t.wait_ge(ctx.sem("dvec"), 2 * gi + 2)
                    t.matmul(pkv[:], ckp[:, 0, :], vh[h % 2][:, 2 * n, :], start=True, stop=False)
                    ctx.inc(t.matmul(pkv[:], ckp[:, 1, :], vh[h % 2][:, 2 * n + 1, :],
                                     start=False, stop=True), "pek", 1)

        @blk.vector
        def _(v):
            for h in range(32):
                for c in range(NTC):
                    v.wait_ge(ctx.sem("peq"), h * NTC + c + 1)
                    ctx.inc(v.tensor_tensor(out=qp_sb[:, c * 512:(c + 1) * 512],
                                            in0=qh[h % 2][:, c * 512:(c + 1) * 512],
                                            in1=pqp[:], op=ALU.mult), "dveq", 1)
                if h >= 1:
                    v.wait_ge(ctx.sem("pek"), h * NB)
                    v.wait_ge(ctx.sem("stk%d" % ((h - 1) % 2)), 16 * ((h - 1) // 2 + 1))
                v.tensor_copy(kvf[0][:], kvi[h % 2][:])
                v.drain()
                ctx.inc(v.tensor_copy(kvb[:], kvf[0][:]), "dvek", 1)
                for n in range(NB):
                    gi = h * NB + n
                    v.wait_ge(ctx.sem("pes"), gi + 1)
                    v.tensor_tensor(out=msk[:, 0, :], in0=ps2[gi % 2][:, 0, :],
                                    in1=dct_sb[h % 2][:, 0, :], op=ALU.mult)
                    ctx.inc(v.tensor_tensor(out=msk[:, 1, :], in0=ps2[gi % 2][:, 1, :],
                                            in1=dct_sb[h % 2][:, 1, :], op=ALU.mult), "dvem", 2)
                    v.wait_ge(ctx.sem("pet"), gi + 1)
                    v.tensor_scalar(out=ckp[:, 0, :], in0=ptr[:, 0, :],
                                    scalar1=kd_sb[:, 0, h:h + 1], scalar2=None, op0=ALU.mult)
                    ctx.inc(v.tensor_scalar(out=ckp[:, 1, :], in0=ptr[:, 1, :],
                                            scalar1=kd_sb[:, 1, h:h + 1], scalar2=None, op0=ALU.mult),
                            "dvec", 2)
                    v.wait_ge(ctx.sem("peo"), gi + 1)
                    ctx.inc(v.tensor_copy(attn_sb[:, h, n * 256:(n + 1) * 256], po[gi % 2][:]),
                            "dveo", 1)
                    v.wait_ge(ctx.sem("pek"), gi + 1)
                    v.scalar_tensor_tensor(out=kvf[(n + 1) % 2][:], in0=kvf[n % 2][:],
                                           scalar=float(bd[h]), in1=pkv[:],
                                           op0=ALU.mult, op1=ALU.add)
                    v.drain()
                    ctx.inc(v.tensor_copy(kvb[:], kvf[(n + 1) % 2][:]), "dvek", 1)
                    v.drain()

        @blk.gpsimd
        def _(g):
            for h in range(32):
                g.wait_ge(ctx.sem("dvek"), h * (NB + 1) + NB + 1)
                ctx.inc(g.dma_start(out=kv8_out[h], in_=kvf[NB % 2][:]), "stk%d" % (h % 2), 16)
            g.wait_ge(ctx.sem("dveo"), 32 * NB)
            for ib in range(32):
                ctx.inc(g.dma_start(out=attn_out[ib * 128:(ib + 1) * 128, :],
                                    in_=attn_sb[:, ib, :]), "sta", 16)
            g.wait_ge(ctx.sem("sta"), 16 * 32)
            g.wait_ge(ctx.sem("stk0"), 16 * 16)
            g.wait_ge(ctx.sem("stk1"), 16 * 16)



def build_d():
    nc = bass.Bass("TRN2", target_bir_lowering=False)
    io = {}
    io["q1_fm"] = nc.dram_tensor("q1_fm", [D, T], BF16, kind="ExternalInput").ap()
    io["k1_fm"] = nc.dram_tensor("k1_fm", [D, T], BF16, kind="ExternalInput").ap()
    io["v1_tok"] = nc.dram_tensor("v1_tok", [T, D], BF16, kind="ExternalInput").ap()
    io["kv_init"] = nc.dram_tensor("kv_init", [32, 128, 128], F32, kind="ExternalInput").ap()
    io["attn_d"] = nc.dram_tensor("attn_d", [D, T], BF16, kind="ExternalOutput").ap()
    io["kv8_out"] = nc.dram_tensor("kv8_out", [32, 128, 128], F32, kind="ExternalOutput").ap()
    es = ExitStack()
    ctx = Ctx(nc, es, "D_")
    emit_d(nc, ctx, es, io)
    return nc





def emit_e(nc, ctx, es, io):
    flag = io.get("flag")
    kv8ag = io.get("kv8_ag")
    sel = io.get("sel")
    attn_d = io["attn_d"]
    kv8c = io["kv8c"]
    q1_fm = io["q1_fm"]
    x2 = io["x2"]
    w_gate = io["w_gate"]
    w_out1 = io["w_out1"]
    norm_w = io["norm_w"]
    x3 = io["x3"]

    g2 = io.get("gctx")
    wgbf = io.get("wgbf") or ctx.dram("wgbf", [D, D], BF16)
    wobf = io.get("wobf") or ctx.dram("wobf", [D, D], BF16)
    x2bf = ctx.dram("x2bf_e", [D, T], BF16)

    qd, kd, dd, bd = _l1_decays()
    qcorr = np.empty((32, T), np.float32)
    for h in range(32):
        for m in range(8):
            qcorr[h, m * 256:(m + 1) * 256] = qd[h] * (bd[h] ** m)
    qcorr_c = ctx.inline(qcorr.astype(ml_dtypes.bfloat16), "qcorr_c")

    onescol = es.enter_context(ctx.sbuf("onescol", [128, 1], BF16))
    ones32 = es.enter_context(ctx.sbuf("ones32", [1, 128], F32))
    onesb = es.enter_context(ctx.sbuf("onesb", [1, 128], BF16))
    normw_sb = es.enter_context(ctx.sbuf("normw_sb", [128, NIB], F32))
    kv8b = es.enter_context(ctx.sbuf("kv8b", [128, 32, 128], BF16))
    flag_sb = es.enter_context(ctx.sbuf("flag_sb", [128, 1], F32))
    if kv8ag is not None:
        kvr = [es.enter_context(ctx.sbuf(f"kvr{i}", [128, 8, 128], BF16)) for i in range(2)]
        sel_sb = es.enter_context(ctx.sbuf("sel_sb", [128, 8], F32))
    at_sb = es.enter_context(ctx.sbuf("at_sb", [128, NIB, 512], BF16))
    xg_sb = es.enter_context(ctx.sbuf("xg_sb", [128, NIB, 512], BF16))
    q1t = [es.enter_context(ctx.sbuf(f"q1t{i}", [128, 512], BF16)) for i in range(2)]
    qcr = [es.enter_context(ctx.sbuf(f"qcr{i}", [1, 512], BF16)) for i in range(2)]
    cqs = es.enter_context(ctx.sbuf("cqs", [128, 512], BF16))
    rms2 = [es.enter_context(ctx.sbuf(f"rms2_{i}", [128, 512], BF16)) for i in range(2)]
    gt_sb = [es.enter_context(ctx.sbuf(f"gt_sb{i}", [128, 512], BF16)) for i in range(2)]
    sE = es.enter_context(ctx.sbuf("sE", [1, 512], F32))
    veE = es.enter_context(ctx.sbuf("veE", [1, 512], F32))
    tE = es.enter_context(ctx.sbuf("tE", [1, 512], F32))
    rE = es.enter_context(ctx.sbuf("rE", [1, 512], F32))
    wg = [es.enter_context(ctx.sbuf(f"wg{i}", [128, NIB, 128], BF16)) for i in range(2)]
    wo = [es.enter_context(ctx.sbuf(f"wo{i}", [128, NIB, 128], BF16)) for i in range(2)]

    psc = [es.enter_context(ctx.psum(f"psc{i}", [128, 512], F32)) for i in range(2)]
    pz = [es.enter_context(ctx.psum(f"pz{i}", [1, 512], F32)) for i in range(2)]
    pb = es.enter_context(ctx.psum("pb", [128, 512], F32))
    pbm = es.enter_context(ctx.psum("pbm", [128, 512], F32))
    pq = es.enter_context(ctx.psum("pq", [128, 512], F32))
    pc = es.enter_context(ctx.psum("pc", [128, 512], F32))
    with nc.Block(no_gpsimd_drain=True) as blk:
        @blk.vector
        def _(v):
            v.memset(ones32[:], 1.0)
            v.memset(onesb[:], 1.0)
            ctx.inc(v.memset(onescol[:], 1.0), "init", 1)

        @blk.gpsimd
        def _(g):
            for j in range(4):
                cs = slice(j * 1024, (j + 1) * 1024)
                if g2 is None:
                    ctx.inc(g.dma_start(out=wgbf[:, cs], in_=w_gate[:, cs]), "wcg", 16)
                    ctx.inc(g.dma_start(out=wobf[:, cs], in_=w_out1[:, cs]), "wco", 16)
                ctx.inc(g.dma_start(out=x2bf[cs, :], in_=x2[cs, :]), "wcx", 16)
            if kv8ag is None:
                ctx.inc(g.dma_start(out=kv8b[:], in_=kv8c.rearrange("h p e -> p h e")), "wck", 16)
            else:
                ctx.inc(g.dma_start(out=flag_sb[0:1, 0:1], in_=kv8c[0][0:1, 0:1]), "wck", 16)
            ctx.inc(g.dma_start(out=normw_sb[:], in_=norm_w.rearrange("(ib p) -> p ib", p=128), allow_slow_non_contiguous=True), "wck", 16)
            if flag is not None:
                ctx.inc(g.dma_start(out=flag_sb[:], in_=flag), "wck", 16)

    if kv8ag is not None:
        with nc.Block(no_gpsimd_drain=True) as blk:
            @blk.gpsimd
            def _(g):
                ctx.inc(g.dma_start(out=sel_sb[:], in_=sel), "Esel", 16)
                for h in range(32):
                    if h >= 2:
                        g.wait_ge(ctx.sem("Ekv"), h - 1)
                    for r in range(8):
                        ctx.inc(g.dma_start(out=kvr[h % 2][:, r, :],
                                            in_=kv8ag[(r * 32 + h) * 128:(r * 32 + h + 1) * 128, :]),
                                "Ekvl%d" % (h % 2), 16)

            @blk.vector
            def _(v):
                v.wait_ge(ctx.sem("Esel"), 16)
                for h in range(32):
                    v.wait_ge(ctx.sem("Ekvl%d" % (h % 2)), 128 * (h // 2 + 1))
                    v.tensor_scalar(out=kv8b[:, h, :], in0=kvr[h % 2][:, 0, :],
                                    scalar1=sel_sb[:, 0:1], scalar2=None, op0=ALU.mult)
                    v.drain()
                    for r in range(1, 8):
                        v.scalar_tensor_tensor(out=kv8b[:, h, :], in0=kvr[h % 2][:, r, :],
                                               scalar=sel_sb[:, r:r + 1], in1=kv8b[:, h, :],
                                               op0=ALU.mult, op1=ALU.add)
                        v.drain()
                    ctx.inc(v.nop(), "Ekv", 1)

    if flag is not None:
        with nc.Block(no_gpsimd_drain=True) as blk:
            @blk.vector
            def _(v):
                v.wait_ge(ctx.sem("wck"), 48)
                ctx.inc(v.tensor_scalar(out=kv8b[:].rearrange("p h e -> p (h e)"),
                                        in0=kv8b[:].rearrange("p h e -> p (h e)"),
                                        scalar1=flag_sb[:], scalar2=None, op0=ALU.mult), "Efl", 1)

    def pre_sync(sp, tc):
        if tc == 0:
            sp.wait_ge(ctx.sem("wcx"), 64)
            if g2 is None:
                sp.wait_ge(ctx.sem("wcg"), 64)
                sp.wait_ge(ctx.sem("wco"), 64)
            else:
                sp.wait_ge(g2.sem("Ewcg"), 64)
                sp.wait_ge(g2.sem("Ewco"), 64)
        if tc >= 1:
            sp.wait_ge(ctx.sem("Lpe2"), tc * NIB)
        ctx.inc(sp.dma_start(out=at_sb[:],
                             in_=attn_d.rearrange("(ib p) t -> p ib t", p=128)[:, :, tc * 512:(tc + 1) * 512]),
                "Eat%d" % (tc % 2), 16)
        ctx.inc(sp.dma_start(out=xg_sb[:],
                             in_=x2bf.rearrange("(ib p) t -> p ib t", p=128)[:, :, tc * 512:(tc + 1) * 512]),
                "Exg%d" % (tc % 2), 16)
        for h in range(32):
            i = tc * 32 + h
            if i >= 2:
                sp.wait_ge(ctx.sem("Edvq"), i - 1)
            ctx.inc(sp.dma_start(out=q1t[h % 2][:],
                                 in_=q1_fm[h * 128:(h + 1) * 128, tc * 512:(tc + 1) * 512]),
                    "Eq1%d" % (i % 2), 16)
            ctx.inc(sp.dma_start(out=qcr[h % 2][:],
                                 in_=qcorr_c[h][None, tc * 512:(tc + 1) * 512]),
                    "Eq1%d" % (i % 2), 16)
        for ob in range(NIB):
            i = tc * NIB + ob
            if i >= 2:
                sp.wait_ge(ctx.sem("Epeg"), i - 1)
            ctx.inc(sp.dma_start(out=wg[i % 2][:],
                                 in_=wgbf.rearrange("(ib p) o -> p ib o", p=128)[:, :, ob * 128:(ob + 1) * 128]),
                    "Ewg%d" % (i % 2), 16)


    def pre_tensor(t, tc):
        if tc == 0:
            t.wait_ge(ctx.sem("init"), 1)
            t.wait_ge(ctx.sem("wck"), 32)
            if flag is not None:
                t.wait_ge(ctx.sem("Efl"), 1)
            if kv8ag is not None:
                t.wait_ge(ctx.sem("Ekv"), 32)
        for h in range(32):
            i = tc * 32 + h
            t.wait_ge(ctx.sem("Eq1%d" % (i % 2)), 32 * (i // 2 + 1))
            if i >= 1:
                t.wait_ge(ctx.sem("Edvq"), i)
            if h == 0 and tc >= 1:
                t.wait_ge(ctx.sem("Eacg"), tc * NIB)
            ctx.inc(t.matmul(pq[:], onesb[0:1, :], qcr[h % 2][0:1, :],
                             start=True, stop=True), "Epq", 1)
            t.wait_ge(ctx.sem("Edvq"), i + 1)
            if i >= 1:
                t.wait_ge(ctx.sem("Edvc"), i)
            ctx.inc(t.matmul(pc[:], kv8b[:, h, :], cqs[:], start=True, stop=True), "Epc", 1)
        for ib in range(NIB):
            i = tc * NIB + ib
            t.wait_ge(ctx.sem("Eact2"), i + 1)
            ctx.inc(t.matmul(pb[0:1, :], onescol[:], rms2[i % 2][:],
                             start=(ib == 0), stop=(ib == NIB - 1)), "Epst", 1)
        t.wait_ge(ctx.sem("Edvr"), tc + 1)
        ctx.inc(t.matmul(pbm[:], ones32[0:1, :], rE[0:1, :], start=True, stop=True), "Epb1", 1)
        for ob in range(NIB):
            i = tc * NIB + ob
            t.wait_ge(ctx.sem("Ewg%d" % (i % 2)), 16 * (i // 2 + 1))
            t.wait_ge(ctx.sem("Exg%d" % (tc % 2)), 16 * (tc // 2 + 1))
            if i >= 2:
                t.wait_ge(ctx.sem("Eacg"), i - 1)
            bank = pq if ob % 2 == 0 else pc
            ins = None
            for ib in range(NIB):
                ins = t.matmul(bank[:], wg[i % 2][:, ib, :], xg_sb[:, ib, :],
                               start=(ib == 0), stop=(ib == NIB - 1))
            ctx.inc(ins, "Epeg", 1)

    def pre_vector(v, tc):
        for h in range(32):
            i = tc * 32 + h
            v.wait_ge(ctx.sem("Epq"), i + 1)
            if tc >= 1 and h == 0:
                v.wait_ge(ctx.sem("Lpe2"), tc * NIB)
            v.wait_ge(ctx.sem("Eat%d" % (tc % 2)), 16 * (tc // 2 + 1))
            ctx.inc(v.tensor_tensor(out=cqs[:], in0=q1t[h % 2][:], in1=pq[:], op=ALU.mult), "Edvq", 1)
            v.drain()
            v.wait_ge(ctx.sem("Epc"), i + 1)
            ctx.inc(v.tensor_tensor(out=at_sb[:, h, :], in0=at_sb[:, h, :], in1=pc[:], op=ALU.add), "Edvc", 1)
            v.drain()
        v.wait_ge(ctx.sem("Epst"), (tc + 1) * NIB)
        ctx.inc(v.tensor_scalar(out=veE[:], in0=pb[0:1, :], scalar1=1.0 / D, scalar2=1e-5,
                                op0=ALU.mult, op1=ALU.add), "Edvv", 1)
        v.drain()
        v.wait_ge(ctx.sem("Eacs"), tc + 1)
        v.reciprocal(tE[:], sE[:])
        v.drain()
        v.tensor_tensor(out=tE[:], in0=veE[:], in1=tE[:], op=ALU.mult)
        v.drain()
        v.tensor_tensor(out=tE[:], in0=tE[:], in1=sE[:], op=ALU.add)
        v.drain()
        v.tensor_scalar(out=tE[:], in0=tE[:], scalar1=0.5, scalar2=None, op0=ALU.mult)
        v.drain()
        ctx.inc(v.reciprocal(rE[:], tE[:]), "Edvr", 1)
        v.drain()
        v.wait_ge(ctx.sem("Epb1"), tc + 1)
        for ib in range(NIB):
            i = tc * NIB + ib
            v.wait_ge(ctx.sem("Eact2"), i + 1)
            ctx.inc(v.scalar_tensor_tensor(out=at_sb[:, ib, :], in0=at_sb[:, ib, :],
                                           scalar=normw_sb[:, ib:ib + 1], in1=pbm[:],
                                           op0=ALU.mult, op1=ALU.mult), "Edvn", 1)
            v.drain()
        for ob in range(NIB):
            i = tc * NIB + ob
            v.wait_ge(ctx.sem("Eacg"), i + 1)
            ctx.inc(v.tensor_tensor(out=at_sb[:, ob, :], in0=at_sb[:, ob, :],
                                    in1=gt_sb[ob % 2][:], op=ALU.mult), "Edgn", 1)
            v.drain()

    def pre_scalar(s, tc):
        for ib in range(NIB):
            i = tc * NIB + ib
            s.wait_ge(ctx.sem("Edvc"), tc * 32 + ib + 1)
            if i >= 2:
                s.wait_ge(ctx.sem("Epst"), i - 1)
            ctx.inc(s.activation(out=rms2[i % 2][:], in_=at_sb[:, ib, :], func=AF.Square), "Eact2", 1)
        s.wait_ge(ctx.sem("Edvv"), tc + 1)
        ctx.inc(s.activation(out=sE[:], in_=veE[:], func=AF.Sqrt), "Eacs", 1)
        for ob in range(NIB):
            i = tc * NIB + ob
            s.wait_ge(ctx.sem("Epeg"), i + 1)
            s.wait_ge(ctx.sem("Edgn"), tc * NIB + max(0, ob - 1))
            bank = pq if ob % 2 == 0 else pc
            ctx.inc(s.activation(out=gt_sb[ob % 2][:], in_=bank[:], func=AF.Sigmoid), "Eacg", 1)

    def sync_loads(sp, tc, ob, i):
        if i >= 2:
            sp.wait_ge(ctx.sem("Lpe2"), i - 1)
        ctx.inc(sp.dma_start(out=wo[i % 2][:],
                             in_=wobf.rearrange("(ib p) o -> p ib o", p=128)[:, :, ob * 128:(ob + 1) * 128]),
                "Ewo%d" % (i % 2), 16)

    def pe_mms(t, tc, ob, i, psum):
        t.wait_ge(ctx.sem("Ewo%d" % (i % 2)), 16 * (i // 2 + 1))
        t.wait_ge(ctx.sem("Edgn"), (tc + 1) * NIB)
        ins = None
        for ib in range(NIB):
            ins = t.matmul(psum[:], wo[i % 2][:, ib, :], at_sb[:, ib, :],
                           start=(ib == 0), stop=(ib == NIB - 1))
        return ins

    ln_out_phase(nc, ctx, es, "L", x2, x3, pe_mms, sync_loads,
                 psc, pz, pb, pbm, ones32, onescol,
                 pre_sync=pre_sync, pre_tensor=pre_tensor, pre_vector=pre_vector,
                 pre_scalar=pre_scalar)


def build_e():
    nc = bass.Bass("TRN2", target_bir_lowering=False)
    io = {}
    io["attn_d"] = nc.dram_tensor("attn_d", [D, T], BF16, kind="ExternalInput").ap()
    io["kv8c"] = nc.dram_tensor("kv8c", [32, 128, 128], F32, kind="ExternalInput").ap()
    io["q1_fm"] = nc.dram_tensor("q1_fm", [D, T], BF16, kind="ExternalInput").ap()
    io["x2"] = nc.dram_tensor("x2", [D, T], F32, kind="ExternalInput").ap()
    io["w_gate"] = nc.dram_tensor("w_gate", [D, D], F32, kind="ExternalInput").ap()
    io["w_out1"] = nc.dram_tensor("w_out1", [D, D], F32, kind="ExternalInput").ap()
    io["norm_w"] = nc.dram_tensor("norm_w", [D], F32, kind="ExternalInput").ap()
    io["x3"] = nc.dram_tensor("x3", [D, T], F32, kind="ExternalOutput").ap()
    es = ExitStack()
    ctx = Ctx(nc, es, "E_")
    emit_e(nc, ctx, es, io)
    return nc





def emit_f(nc, ctx, es, io):
    x3 = io["x3"]
    up = io["up"]
    down = io["down"]
    x4 = io["x4"]

    upbf = ctx.dram("upbf", [16, 256, 1024], BF16)
    dnbf = ctx.dram("dnbf", [16, 1024, 256], BF16)
    x3bf = ctx.dram("x3bf", [D, T], BF16)

    onescol = es.enter_context(ctx.sbuf("onescol", [128, 1], BF16))
    ones32 = es.enter_context(ctx.sbuf("ones32", [1, 128], F32))
    xf_sb = [es.enter_context(ctx.sbuf(f"xf_sb{i}", [128, NIB, 512], BF16)) for i in range(2)]
    up_sb = [es.enter_context(ctx.sbuf(f"up_sb{i}", [128, 2, 1024], BF16)) for i in range(2)]
    dn_sb = [es.enter_context(ctx.sbuf(f"dn_sb{i}", [128, 8, 256], BF16)) for i in range(2)]
    mid_sb = [es.enter_context(ctx.sbuf(f"mid_sb{i}", [128, 8, 512], BF16)) for i in range(2)]

    psc = [es.enter_context(ctx.psum(f"psc{i}", [128, 512], F32)) for i in range(2)]
    pz = [es.enter_context(ctx.psum(f"pz{i}", [1, 512], F32)) for i in range(2)]
    pb = es.enter_context(ctx.psum("pb", [128, 512], F32))
    pbm = es.enter_context(ctx.psum("pbm", [128, 512], F32))
    pmid = [es.enter_context(ctx.psum(f"pmid{i}", [128, 512], F32)) for i in range(2)]
    with nc.Block(no_gpsimd_drain=True) as blk:
        @blk.vector
        def _(v):
            v.memset(ones32[:], 1.0)
            ctx.inc(v.memset(onescol[:], 1.0), "init", 1)

        @blk.gpsimd
        def _(g):
            ctx.inc(g.dma_start(out=upbf.rearrange("h d m -> (h d) m"),
                                in_=up.rearrange("h d m -> (h d) m")), "wcu", 16)
            ctx.inc(g.dma_start(out=dnbf.rearrange("h m d -> (h m) d"),
                                in_=down.rearrange("h m d -> (h m) d")), "wcu", 16)
            for j in range(4):
                cs = slice(j * 1024, (j + 1) * 1024)
                ctx.inc(g.dma_start(out=x3bf[cs, :], in_=x3[cs, :]), "wcu", 16)

    def pre_sync(sp, tc):
        if tc >= 2:
            sp.wait_ge(ctx.sem("Fpu"), 8 * ((tc - 1) * 16))
        if tc == 0:
            sp.wait_ge(ctx.sem("wcu"), 96)
        ctx.inc(sp.dma_start(out=xf_sb[tc % 2][:],
                             in_=x3bf.rearrange("(ib p) t -> p ib t", p=128)[:, :, tc * 512:(tc + 1) * 512]),
                "Fxf%d" % (tc % 2), 16)

    def sync_loads(sp, tc, ob, i):
        if ob % 2 == 1:
            return
        h = ob // 2
        g = tc * 16 + h
        if g >= 2:
            sp.wait_ge(ctx.sem("Fpu"), 8 * (g - 1))
        ctx.inc(sp.dma_start(out=up_sb[g % 2][:],
                             in_=upbf[h].rearrange("(ib p) m -> p ib m", p=128)), "Fup%d" % (g % 2), 16)
        if g >= 2:
            sp.wait_ge(ctx.sem("Fpe2"), tc * NIB + ob - 2)
        ctx.inc(sp.dma_start(out=dn_sb[g % 2][:],
                             in_=dnbf[h].rearrange("(mb p) d -> p mb d", p=128)), "Fup%d" % (g % 2), 16)

    def pe_mms(t, tc, ob, i, psum):
        h = ob // 2
        g = tc * 16 + h
        db = ob % 2
        if db == 0:
            t.wait_ge(ctx.sem("Fup%d" % (g % 2)), 32 * (g // 2 + 1))
            t.wait_ge(ctx.sem("Fxf%d" % (tc % 2)), 16 * (tc // 2 + 1))
            for mb in range(8):
                gm = g * 8 + mb
                if gm >= 2:
                    t.wait_ge(ctx.sem("Fgel"), gm - 1)
                t.matmul(pmid[gm % 2][:], up_sb[g % 2][:, 0, mb * 128:(mb + 1) * 128],
                         xf_sb[tc % 2][:, 2 * h, :], start=True, stop=False)
                ctx.inc(t.matmul(pmid[gm % 2][:], up_sb[g % 2][:, 1, mb * 128:(mb + 1) * 128],
                                 xf_sb[tc % 2][:, 2 * h + 1, :], start=False, stop=True), "Fpu", 1)
            t.wait_ge(ctx.sem("Fgel"), (g + 1) * 8)
        ins = None
        for mb in range(8):
            ins = t.matmul(psum[:], dn_sb[g % 2][:, mb, db * 128:(db + 1) * 128],
                           mid_sb[g % 2][:, mb, :], start=(mb == 0), stop=(mb == 7))
        return ins

    def sc_ob(s, tc, ob, i):
        if ob % 2 == 1:
            return
        h = ob // 2
        g = tc * 16 + h
        for mb in range(8):
            gm = g * 8 + mb
            s.wait_ge(ctx.sem("Fpu"), gm + 1)
            if g >= 2 and mb == 0:
                s.wait_ge(ctx.sem("Fpe2"), tc * NIB + ob - 1)
            ctx.inc(s.activation(out=mid_sb[g % 2][:, mb, :], in_=pmid[gm % 2][:],
                                 func=AF.Gelu), "Fgel", 1)

    def first_wait(t):
        t.wait_ge(ctx.sem("init"), 1)

    ln_out_phase(nc, ctx, es, "F", x3, x4, pe_mms, sync_loads,
                 psc, pz, pb, pbm, ones32, onescol,
                 pre_sync=pre_sync, sc_ob=sc_ob, first_tensor_wait=first_wait)


def build_f():
    nc = bass.Bass("TRN2", target_bir_lowering=False)
    io = {}
    io["x3"] = nc.dram_tensor("x3", [D, T], F32, kind="ExternalInput").ap()
    io["up"] = nc.dram_tensor("up", [16, 256, 1024], F32, kind="ExternalInput").ap()
    io["down"] = nc.dram_tensor("down", [16, 1024, 256], F32, kind="ExternalInput").ap()
    io["x4"] = nc.dram_tensor("x4", [D, T], F32, kind="ExternalOutput").ap()
    es = ExitStack()
    ctx = Ctx(nc, es, "F_")
    emit_f(nc, ctx, es, io)
    return nc





def emit_a1f(nc, ctx, es, io):
    xin = io["xin_f"]
    w = io["w_qkv0"]
    bias = io["b_qkv0"]
    q_fm, k_fm, v_tok = io["q_fm"], io["k_fm"], io["v_tok"]
    wbf = ctx.dram("wbf_qkv0", [D, 3 * D], BF16)
    bbf = ctx.dram("bbf_qkv0", [3 * D], BF16)

    x_sb = es.enter_context(ctx.sbuf("x_sb", [128, NIB, T], BF16))
    ones = es.enter_context(ctx.sbuf("ones", [1, 512], BF16))
    wt = [es.enter_context(ctx.sbuf(f"wt{i}", [128, NIB + 1, 128], BF16)) for i in range(2)]
    po = [es.enter_context(ctx.psum(f"po{i}", [128, 512], F32)) for i in range(2)]
    ob_out = [es.enter_context(ctx.sbuf(f"qk_out{i}", [128, T], BF16)) for i in range(2)]
    wv = es.enter_context(ctx.sbuf("wv", [128, NIB + 1, 512], BF16))
    pv = [es.enter_context(ctx.psum(f"pv{i}", [128, 512], F32)) for i in range(2)]
    v_out = es.enter_context(ctx.sbuf("v_out", [128, 16, 512], BF16))

    with nc.Block(no_gpsimd_drain=True) as blk:
        @blk.vector
        def _(v):
            ctx.inc(v.memset(ones[:], 1.0), "xld0", 1)

        @blk.gpsimd
        def _(g):
            ctx.inc(g.dma_start(out=bbf[:], in_=bias), "wcb", 16)
            ctx.inc(g.dma_start(out=wbf[:, 0:1536], in_=w[:, 0:1536]), "wc0", 16)
            for tcx in range(NTC):
                ctx.inc(g.dma_start(out=x_sb[:, :, tcx * 512:(tcx + 1) * 512],
                                    in_=xin.rearrange("(ib p) t -> p ib t", p=128)[:, :, tcx * 512:(tcx + 1) * 512]),
                        "xc%d" % tcx, 16)
            for j in range(1, 8):
                cs = slice(j * 1536, (j + 1) * 1536)
                ctx.inc(g.dma_start(out=wbf[:, cs], in_=w[:, cs]), "wc%d" % j, 16)

    oi = 0
    gi = 0
    for hf in range(2):
        obs = list(range(64)) if hf == 0 else list(range(32, 64))
        if hf == 1:
            pe_h0 = ctx.val("pe")
            pev_h0 = ctx.val("pev")
            ev_h0 = ctx.val("ev")
            evv_h0 = ctx.val("evv")
            with nc.Block(no_gpsimd_drain=True) as blk:
                @blk.gpsimd
                def _(g):
                    g.wait_ge(ctx.sem("pe"), pe_h0)
                    g.wait_ge(ctx.sem("pev"), pev_h0)
                    ctx.inc(g.dma_start(out=x_sb[:],
                                        in_=xin.rearrange("(ib p) t -> p ib t", p=128)[:, :, T:2 * T]),
                            "xldb", 16)

        oi_base = oi
        with nc.Block(no_gpsimd_drain=True) as blk:
            @blk.sync
            def _(sp):
                for j, ob in enumerate(obs):
                    o = oi_base + j
                    if o >= 2:
                        sp.wait_ge(ctx.sem("pe"), (o - 1) * NTC)
                    sp.wait_ge(ctx.sem("wc%d" % ((ob * 128 + 127) // 1536)), 16)
                    sp.wait_ge(ctx.sem("wcb"), 16)
                    sem = "wld%d" % (o % 2)
                    ctx.inc(sp.dma_start(out=wt[o % 2][:, 0:NIB, :],
                                         in_=wbf.rearrange("(ib p) o -> p ib o", p=128)[:, :, ob * 128:(ob + 1) * 128]),
                            sem, 16)
                    ctx.inc(sp.dma_start(out=wt[o % 2][0:1, NIB, :],
                                         in_=bbf[None, ob * 128:(ob + 1) * 128]),
                            sem, 16)

            @blk.tensor
            def _(t):
                if hf == 0:
                    t.wait_ge(ctx.sem("xld0"), 1)
                else:
                    t.wait_ge(ctx.sem("xldb"), 16)
                for j, ob in enumerate(obs):
                    o = oi_base + j
                    t.wait_ge(ctx.sem("wld%d" % (o % 2)), 32 * (o // 2 + 1))
                    for tc in range(NTC):
                        if hf == 0:
                            t.wait_ge(ctx.sem("xc%d" % tc), 16)
                        grp = o * NTC + tc
                        bank = grp % 2
                        if grp >= 2:
                            t.wait_ge(ctx.sem("ev"), grp - 1)
                        for ib in range(NIB):
                            t.matmul(po[bank][:], wt[o % 2][:, ib, :],
                                     x_sb[:, ib, tc * 512:(tc + 1) * 512],
                                     start=(ib == 0), stop=False)
                        ctx.inc(t.matmul(po[bank][:], wt[o % 2][0:1, NIB, :],
                                         ones[0:1, :], start=False, stop=True), "pe", 1)

            @blk.scalar
            def _(s):
                for j, ob in enumerate(obs):
                    o = oi_base + j
                    if o >= 2:
                        s.wait_ge(ctx.sem("st%d" % (o % 2)), 16 * (o // 2))
                    for tc in range(NTC):
                        grp = o * NTC + tc
                        bank = grp % 2
                        s.wait_ge(ctx.sem("pe"), grp + 1)
                        ctx.inc(s.copy(out=ob_out[o % 2][:, tc * 512:(tc + 1) * 512],
                                       in_=po[bank][:]), "ev", 1)

            @blk.gpsimd
            def _(g):
                for j, ob in enumerate(obs):
                    o = oi_base + j
                    g.wait_ge(ctx.sem("ev"), (o + 1) * NTC)
                    if ob < 32:
                        dst = q_fm[ob * 128:(ob + 1) * 128, :]
                    else:
                        dst = k_fm[(ob - 32) * 128:(ob - 31) * 128, hf * T:(hf + 1) * T]
                    ctx.inc(g.dma_start(out=dst, in_=ob_out[o % 2][:]),
                            "st%d" % (o % 2), 16)
        oi += len(obs)

        gi_base = gi
        with nc.Block(no_gpsimd_drain=True) as blk:
            @blk.sync
            def _(sp):
                for oc in range(8):
                    g2 = gi_base + oc
                    if g2 >= 1:
                        sp.wait_ge(ctx.sem("pev"), g2 * 16)
                    sp.wait_ge(ctx.sem("wc%d" % ((2 * D + oc * 512 + 511) // 1536)), 16)
                    sp.wait_ge(ctx.sem("wcb"), 16)
                    ctx.inc(sp.dma_start(out=wv[:, 0:NIB, :],
                                         in_=wbf.rearrange("(ib p) o -> p ib o", p=128)[:, :, 2 * D + oc * 512: 2 * D + (oc + 1) * 512]),
                            "wvld", 16)
                    ctx.inc(sp.dma_start(out=wv[0:1, NIB, :],
                                         in_=bbf[None, 2 * D + oc * 512: 2 * D + (oc + 1) * 512]),
                            "wvld", 16)

            @blk.tensor
            def _(t):
                for oc in range(8):
                    g2 = gi_base + oc
                    t.wait_ge(ctx.sem("wvld"), 32 * (g2 + 1))
                    for tb in range(16):
                        grp = g2 * 16 + tb
                        bank = grp % 2
                        if grp >= 2:
                            t.wait_ge(ctx.sem("evv"), grp - 1)
                        for ib in range(NIB):
                            t.matmul(pv[bank][:], x_sb[:, ib, tb * 128:(tb + 1) * 128],
                                     wv[:, ib, :], start=(ib == 0), stop=False)
                        ctx.inc(t.matmul(pv[bank][:], ones[0:1, 0:128],
                                         wv[0:1, NIB, :], start=False, stop=True), "pev", 1)

            @blk.scalar
            def _(s):
                for oc in range(8):
                    g2 = gi_base + oc
                    if g2 >= 1:
                        s.wait_ge(ctx.sem("stv"), 16 * g2)
                    for tb in range(16):
                        grp = g2 * 16 + tb
                        bank = grp % 2
                        s.wait_ge(ctx.sem("pev"), grp + 1)
                        ctx.inc(s.copy(out=v_out[:, tb, :], in_=pv[bank][:]), "evv", 1)

            @blk.gpsimd
            def _(g):
                for oc in range(8):
                    g2 = gi_base + oc
                    g.wait_ge(ctx.sem("evv"), (g2 + 1) * 16)
                    ctx.inc(g.dma_start(
                        out=v_tok.rearrange("(tb p) o -> p tb o", p=128)[:, 16 * hf:16 * hf + 16, oc * 512:(oc + 1) * 512],
                        in_=v_out[:]), "stv", 16)
        gi += 8

    with nc.Block(no_gpsimd_drain=True) as blk:
        @blk.gpsimd
        def _(g):
            g.wait_ge(ctx.sem("st0"), 16 * 48)
            g.wait_ge(ctx.sem("st1"), 16 * 48)
            g.wait_ge(ctx.sem("stv"), 16 * 16)

    return nc



def _phase_reset(nc, snap):
    nc.all_engine_barrier()
    nc.clear_and_free_semaphores(nc.allocated_since(snap))
    nc.all_engine_barrier()
    nc.restore_sems(snap)


def build_fused():
    nc = bass.Bass("TRN2", target_bir_lowering=False, num_devices=NCORES)
    io = {}
    EI = lambda n, s, d=F32: nc.dram_tensor(n, s, d, kind="ExternalInput").ap()
    io["xin_f"] = EI("xin_f", [D, 2 * T])
    io["w_qkv0"] = EI("w_qkv0", [D, 3 * D]); io["b_qkv0"] = EI("b_qkv0", [3 * D])
    io["w_o0"] = EI("w_o0", [D, D]); io["b_o0"] = EI("b_o0", [D])
    io["w_base"] = EI("w_base", [D, D]); io["b_base"] = EI("b_base", [D])
    io["w_A"] = EI("w_A", [D, 64]); io["w_B"] = EI("w_B", [8, 128, D])
    io["w_router"] = EI("w_router", [D, 16])
    io["w_qkv1"] = EI("w_qkv1", [D, 3 * D])
    io["w_gate"] = EI("w_gate", [D, D]); io["w_out1"] = EI("w_out1", [D, D])
    io["norm_w"] = EI("norm_w", [D])
    io["up"] = EI("up", [16, 256, 1024]); io["down"] = EI("down", [16, 1024, 256])
    io["sel"] = EI("sel", [128, 8])
    io["kv_init"] = EI("kv_init", [32, 128, 128])
    io["x4"] = nc.dram_tensor("x4", [D, T], F32, kind="ExternalOutput").ap()

    INT = lambda n, s, d: nc.dram_tensor(n, s, d, kind="Internal").ap()
    io["q_fm"] = INT("q_fm", [D, T], BF16)
    io["k_fm"] = INT("k_fm", [D, 2 * T], BF16)
    io["v_tok"] = INT("v_tok", [2 * T, D], BF16)
    io["x1"] = INT("x1", [D, T], F32)
    io["x2"] = INT("x2", [D, T], F32)
    io["q1_fm"] = INT("q1_fm", [D, T], BF16)
    io["k1_fm"] = INT("k1_fm", [D, T], BF16)
    io["v1_tok"] = INT("v1_tok", [T, D], BF16)
    io["attn_d"] = INT("attn_d", [D, T], BF16)
    io["kv8_out"] = INT("kv8_out", [32, 128, 128], F32)
    io["kv8_ag"] = INT("kv8_ag", [8 * 32 * 128, 128], F32)
    io["xin"] = io["xin_f"][:, 0:T]
    io["kv8c"] = io["kv8_ag"][0:32 * 128, :].rearrange("(h p) e -> h p e", p=128)
    io["x3"] = INT("x3", [D, T], F32)
    io["wbf_qkv1"] = INT("wbf_qkv1", [D, 3 * D], BF16)
    io["wgbf"] = INT("wgbf", [D, D], BF16)
    io["wobf"] = INT("wobf", [D, D], BF16)

    ges = ExitStack()
    gctx = Ctx(nc, ges, "G_")
    io["gctx"] = gctx
    for j in range(8):
        gctx.sem("Cwc%d" % j)
    gctx.sem("Ewcg")
    gctx.sem("Ewco")

    def prefetch_c():
        with nc.Block(no_gpsimd_drain=True) as blk:
            @blk.gpsimd
            def _(g):
                for j in range(8):
                    cs = slice(j * 1536, (j + 1) * 1536)
                    gctx.inc(g.dma_start(out=io["wbf_qkv1"][:, cs], in_=io["w_qkv1"][:, cs]),
                             "Cwc%d" % j, 16)

    def prefetch_e():
        with nc.Block(no_gpsimd_drain=True) as blk:
            @blk.gpsimd
            def _(g):
                for j in range(4):
                    cs = slice(j * 1024, (j + 1) * 1024)
                    gctx.inc(g.dma_start(out=io["wgbf"][:, cs], in_=io["w_gate"][:, cs]), "Ewcg", 16)
                    gctx.inc(g.dma_start(out=io["wobf"][:, cs], in_=io["w_out1"][:, cs]), "Ewco", 16)

    def soft_barrier(prev):
        waits = prev.finals()
        with nc.Block(no_gpsimd_drain=True) as blk:
            for eng in ("sync", "tensor", "vector", "scalar", "gpsimd"):
                def body(e, waits=waits):
                    for h, v in waits:
                        e.wait_ge(h, v)
                getattr(blk, eng)(body)

    groups = [
        [("P1_", emit_a1f), ("P2_", emit_a2)],
        [("P3_", emit_b), ("P4_", emit_c), ("P5_", emit_d), ("CC_", None)],
        [("P6_", emit_e), ("P7_", emit_f)],
    ]
    for gi, group in enumerate(groups):
        snap = nc.snapshot_sems()
        prev = None
        for pfx, emit in group:
            if pfx == "P2_":
                prefetch_c()
            if pfx == "P3_":
                prefetch_e()
            es = ExitStack()
            ctx = Ctx(nc, es, pfx)
            if prev is not None:
                soft_barrier(prev)
            if emit is None:
                with nc.Block(no_gpsimd_drain=True) as blk:
                    @blk.gpsimd
                    def _(g):
                        ins = g.collective_compute(
                            "AllGather", ALU.bypass,
                            replica_groups=[[0, 1, 2, 3, 4, 5, 6, 7]],
                            ins=[io["kv8_out"].rearrange("h p e -> (h p) e")],
                            outs=[io["kv8_ag"]])
                        ctx.inc(ins, "cc", 1)
                        g.wait_ge(ctx.sem("cc"), 1)
            else:
                emit(nc, ctx, es, io)
            es.close()
            prev = ctx
        _phase_reset(nc, snap)
    return nc




def _permuted_inputs(inp):
    P = np.asarray(inp["L1_ffn_perm"]).astype(np.int64)
    out = {}
    x = np.asarray(inp["L0_attn_x"])
    out["x"] = x[:, :, P]
    out["w_qkv0"] = np.ascontiguousarray(np.asarray(inp["L0_attn_attn_qkvw"])[P, :])
    out["b_qkv0"] = np.asarray(inp["L0_attn_attn_qkvb"])
    out["w_o0"] = np.ascontiguousarray(np.asarray(inp["L0_attn_attn_ow"])[:, P])
    out["b_o0"] = np.asarray(inp["L0_attn_attn_ob"])[P]
    bw = np.asarray(inp["L0_ffn_base_w"])
    out["w_base"] = np.ascontiguousarray(bw[P][:, P].T)
    out["b_base"] = np.asarray(inp["L0_ffn_base_b"])[P]
    out["w_A"] = np.ascontiguousarray(np.asarray(inp["L0_ffn_A"])[:, P].T)
    B = np.asarray(inp["L0_ffn_B"])
    Bp = B[:, P, :]
    BT = np.empty((8, 128, D), dtype=B.dtype)
    for kb in range(8):
        BT[kb, 0:64, :] = Bp[2 * kb].T
        BT[kb, 64:128, :] = Bp[2 * kb + 1].T
    out["w_B"] = np.ascontiguousarray(BT)
    out["w_router"] = np.ascontiguousarray(np.asarray(inp["L0_ffn_router_w"])[:, P].T)
    out["w_qkv1"] = np.ascontiguousarray(np.asarray(inp["L1_attn_qkv_w"])[:, P].T)
    out["w_gate"] = np.ascontiguousarray(np.asarray(inp["L1_attn_gate_w"])[:, P].T)
    out["w_out1"] = np.ascontiguousarray(np.asarray(inp["L1_attn_out_w"])[P, :].T)
    out["norm_w"] = np.asarray(inp["L1_attn_norm_w"])
    out["up"] = np.asarray(inp["L1_ffn_up"])
    out["down"] = np.asarray(inp["L1_ffn_down"])
    out["P"] = P
    return out


def _core_x_fm(xp, c):
    b, h = c // 2, c % 2
    return np.ascontiguousarray(xp[b, h * T:(h + 1) * T, :].T.astype(np.float32))


def _weight_feed(pre):
    return {
        "w_qkv0": pre["w_qkv0"], "b_qkv0": pre["b_qkv0"],
        "w_o0": pre["w_o0"], "b_o0": pre["b_o0"],
        "w_base": pre["w_base"], "b_base": pre["b_base"],
        "w_A": pre["w_A"], "w_B": pre["w_B"], "w_router": pre["w_router"],
        "w_qkv1": pre["w_qkv1"], "w_gate": pre["w_gate"], "w_out1": pre["w_out1"],
        "norm_w": pre["norm_w"], "up": pre["up"], "down": pre["down"],
    }


def kernel(**inputs):
    pre = _permuted_inputs(inputs)
    P = pre["P"]
    inv = np.argsort(P)
    xs = [_core_x_fm(pre["x"], c) for c in range(NCORES)]
    zeros_kv = np.zeros((32, 128, 128), np.float32)
    wfeed = _weight_feed(pre)

    nc = build_fused()
    in_maps = []
    for c in range(NCORES):
        own, partner = c, c ^ 1
        xin_f = np.ascontiguousarray(np.concatenate([xs[own], xs[partner]], axis=1))
        sel = np.zeros((128, 8), np.float32)
        if c % 2 == 1:
            sel[:, c - 1] = 1.0
        in_maps.append(dict(wfeed, xin_f=xin_f, sel=sel, kv_init=zeros_kv))
    res = run_bass_kernel_spmd(nc, in_maps, core_ids=list(range(NCORES))).results

    out = np.empty((4, 4096, 4096), np.float32)
    for c in range(NCORES):
        b, half = c // 2, c % 2
        out[b, half * T:(half + 1) * T, :] = np.asarray(res[c]["x4"]).T[:, inv]
    return out


def kernel_unfused(**inputs):
    pre = _permuted_inputs(inputs)
    P = pre["P"]
    inv = np.argsort(P)
    xs = [_core_x_fm(pre["x"], c) for c in range(NCORES)]
    zeros_kv = np.zeros((32, 128, 128), np.float32)

    nc = build_a1()
    r1 = run_bass_kernel_spmd(nc, [
        {"xin": xs[c], "w_qkv0": pre["w_qkv0"], "b_qkv0": pre["b_qkv0"]}
        for c in range(NCORES)], core_ids=list(range(NCORES))).results

    nc = build_a2()
    in2 = []
    for c in range(NCORES):
        b = c // 2
        kf = np.ascontiguousarray(np.concatenate(
            [np.asarray(r1[2 * b]["k_fm"]), np.asarray(r1[2 * b + 1]["k_fm"])], axis=1))
        vf = np.ascontiguousarray(np.concatenate(
            [np.asarray(r1[2 * b]["v_tok"]), np.asarray(r1[2 * b + 1]["v_tok"])], axis=0))
        in2.append({"q_fm": np.asarray(r1[c]["q_fm"]), "k_fm": kf, "v_tok": vf,
                    "xin": xs[c], "w_o0": pre["w_o0"], "b_o0": pre["b_o0"]})
    r2 = run_bass_kernel_spmd(nc, in2, core_ids=list(range(NCORES))).results

    nc = build_b()
    r3 = run_bass_kernel_spmd(nc, [
        {"x1": np.asarray(r2[c]["x1"]), "w_base": pre["w_base"], "b_base": pre["b_base"],
         "w_A": pre["w_A"], "w_B": pre["w_B"], "w_router": pre["w_router"]}
        for c in range(NCORES)], core_ids=list(range(NCORES))).results

    nc = build_c()
    r4 = run_bass_kernel_spmd(nc, [
        {"x2": np.asarray(r3[c]["x2"]), "w_qkv1": pre["w_qkv1"]}
        for c in range(NCORES)], core_ids=list(range(NCORES))).results

    nc = build_d()
    r5 = run_bass_kernel_spmd(nc, [
        {"q1_fm": np.asarray(r4[c]["q1_fm"]), "k1_fm": np.asarray(r4[c]["k1_fm"]),
         "v1_tok": np.asarray(r4[c]["v1_tok"]), "kv_init": zeros_kv}
        for c in range(NCORES)], core_ids=list(range(NCORES))).results

    nc = build_e()
    in6 = []
    for c in range(NCORES):
        kv8c = np.asarray(r5[c - 1]["kv8_out"]) if c % 2 == 1 else zeros_kv
        in6.append({"attn_d": np.asarray(r5[c]["attn_d"]), "kv8c": kv8c,
                    "q1_fm": np.asarray(r4[c]["q1_fm"]), "x2": np.asarray(r3[c]["x2"]),
                    "w_gate": pre["w_gate"], "w_out1": pre["w_out1"],
                    "norm_w": pre["norm_w"]})
    r6 = run_bass_kernel_spmd(nc, in6, core_ids=list(range(NCORES))).results

    nc = build_f()
    r7 = run_bass_kernel_spmd(nc, [
        {"x3": np.asarray(r6[c]["x3"]), "up": pre["up"], "down": pre["down"]}
        for c in range(NCORES)], core_ids=list(range(NCORES))).results

    out = np.empty((4, 4096, 4096), np.float32)
    for c in range(NCORES):
        b, half = c // 2, c % 2
        out[b, half * T:(half + 1) * T, :] = np.asarray(r7[c]["x4"]).T[:, inv]
    return out


if __name__ == "__main__":
    pass


# version 09 (final)
# speedup vs baseline: 1.0208x; 1.0208x over previous
import numpy as np
import ml_dtypes
from contextlib import ExitStack

import concourse.bass as bass
import concourse.mybir as mybir
from concourse.bass_utils import run_bass_kernel_spmd

F32 = mybir.dt.float32
BF16 = mybir.dt.bfloat16
AF = mybir.ActivationFunctionType
ALU = mybir.AluOpType
AX = mybir.AxisListType
BF16NP = ml_dtypes.bfloat16

D = 4096
T = 2048
NIB = D // 128
NTC = T // 512
NCORES = 8



class Ctx:

    def __init__(self, nc, es, pfx=""):
        self.nc = nc
        self.es = es
        self.pfx = pfx
        self.h = {}
        self.n = {}

    def sem(self, name):
        if name not in self.h:
            self.h[name] = self.nc.alloc_semaphore(self.pfx + name)
            self.n[name] = 0
        return self.h[name]

    def sbuf(self, name, shape, dt):
        return self.nc.sbuf_tensor(self.pfx + name, shape, dt)

    def psum(self, name, shape, dt):
        return self.nc.psum_tensor(self.pfx + name, shape, dt)

    def inline(self, data, name):
        return self.nc.inline_tensor(data, name=self.pfx + name).ap()

    def dram(self, name, shape, dt):
        return self.nc.dram_tensor(self.pfx + name, shape, dt, kind="Internal").ap()

    def finals(self):
        return [(self.h[n], self.n[n]) for n in self.h if self.n[n] > 0]

    def inc(self, ins, name, k=1):
        ins.then_inc(self.sem(name), k)
        self.n[name] += k
        return self.n[name]

    def val(self, name):
        self.sem(name)
        return self.n[name]


def cast_load(g, ctx, sem, dst, src):
    return ctx.inc(g.dma_start(out=dst, in_=src), sem, 16)



def emit_a1(nc, ctx, es, io):
    xin = io["xin"]
    w = io["w_qkv0"]
    bias = io["b_qkv0"]
    q_fm = io["q_fm"]
    k_fm = io["k_fm"]
    v_tok = io["v_tok"]
    wbf = ctx.dram("wbf_qkv0", [D, 3 * D], BF16)
    bbf = ctx.dram("bbf_qkv0", [3 * D], BF16)

    x_sb = es.enter_context(ctx.sbuf("x_sb", [128, NIB, T], BF16))
    ones = es.enter_context(ctx.sbuf("ones", [1, 512], BF16))
    with nc.Block(no_gpsimd_drain=True) as blk:
        @blk.gpsimd
        def _(g):
            cast_load(g, ctx, "xld", x_sb[:], xin.rearrange("(ib p) t -> p ib t", p=128))
            ctx.inc(g.dma_start(out=bbf[:], in_=bias), "wcb", 16)
            for j in range(8):
                cs = slice(j * 1536, (j + 1) * 1536)
                ctx.inc(g.dma_start(out=wbf[:, cs], in_=w[:, cs]), "wc%d" % j, 16)

        @blk.vector
        def _(v):
            ctx.inc(v.memset(ones[:], 1.0), "xld0", 1)

    sub = es
    wt = [sub.enter_context(ctx.sbuf(f"wt{i}", [128, NIB + 1, 128], BF16)) for i in range(2)]
    po = [sub.enter_context(ctx.psum(f"po{i}", [128, 512], F32)) for i in range(2)]
    ob_out = [sub.enter_context(ctx.sbuf(f"qk_out{i}", [128, T], BF16)) for i in range(2)]

    NOB_QK = 64
    with nc.Block(no_gpsimd_drain=True) as blk:
        @blk.sync
        def _(sp):
            for ob in range(NOB_QK):
                if ob >= 2:
                    sp.wait_ge(ctx.sem("pe"), (ob - 1) * NTC)
                sp.wait_ge(ctx.sem("wc%d" % ((ob * 128 + 127) // 1536)), 16)
                sp.wait_ge(ctx.sem("wcb"), 16)
                sem = "wld%d" % (ob % 2)
                ctx.inc(sp.dma_start(out=wt[ob % 2][:, 0:NIB, :],
                                     in_=wbf.rearrange("(ib p) o -> p ib o", p=128)[:, :, ob * 128:(ob + 1) * 128]),
                        sem, 16)
                ctx.inc(sp.dma_start(out=wt[ob % 2][0:1, NIB, :],
                                     in_=bbf[None, ob * 128:(ob + 1) * 128]),
                        sem, 16)

        @blk.tensor
        def _(t):
            t.wait_ge(ctx.sem("xld"), 16)
            t.wait_ge(ctx.sem("xld0"), 1)
            for ob in range(NOB_QK):
                t.wait_ge(ctx.sem("wld%d" % (ob % 2)), 32 * (ob // 2 + 1))
                for tc in range(NTC):
                    bank = (ob * NTC + tc) % 2
                    grp = ob * NTC + tc
                    if grp >= 2:
                        t.wait_ge(ctx.sem("ev"), grp - 1)
                    for ib in range(NIB):
                        t.matmul(po[bank][:], wt[ob % 2][:, ib, :],
                                 x_sb[:, ib, tc * 512:(tc + 1) * 512],
                                 start=(ib == 0), stop=False)
                    ins = t.matmul(po[bank][:], wt[ob % 2][0:1, NIB, :],
                                   ones[0:1, :], start=False, stop=True)
                    ctx.inc(ins, "pe", 1)

        @blk.scalar
        def _(s):
            for ob in range(NOB_QK):
                if ob >= 2:
                    s.wait_ge(ctx.sem("st%d" % (ob % 2)), 16 * (ob // 2))
                for tc in range(NTC):
                    grp = ob * NTC + tc
                    bank = grp % 2
                    s.wait_ge(ctx.sem("pe"), grp + 1)
                    ctx.inc(s.copy(out=ob_out[ob % 2][:, tc * 512:(tc + 1) * 512],
                                   in_=po[bank][:]), "ev", 1)

        @blk.gpsimd
        def _(g):
            for ob in range(NOB_QK):
                g.wait_ge(ctx.sem("ev"), (ob + 1) * NTC)
                dst = q_fm if ob < 32 else k_fm
                row = (ob % 32) * 128
                ctx.inc(g.dma_start(out=dst[row:row + 128, :], in_=ob_out[ob % 2][:]),
                        "st%d" % (ob % 2), 16)


    sub = es
    wv = sub.enter_context(ctx.sbuf("wv", [128, NIB + 1, 512], BF16))
    pv = [sub.enter_context(ctx.psum(f"pv{i}", [128, 512], F32)) for i in range(2)]
    v_out = sub.enter_context(ctx.sbuf("v_out", [128, 16, 512], BF16))

    NOC = 8
    with nc.Block(no_gpsimd_drain=True) as blk:
        @blk.sync
        def _(sp):
            for oc in range(NOC):
                if oc >= 1:
                    sp.wait_ge(ctx.sem("pev"), oc * 16)
                sp.wait_ge(ctx.sem("wc%d" % ((2 * D + oc * 512 + 511) // 1536)), 16)
                sp.wait_ge(ctx.sem("wcb"), 16)
                ctx.inc(sp.dma_start(out=wv[:, 0:NIB, :],
                                     in_=wbf.rearrange("(ib p) o -> p ib o", p=128)[:, :, 2 * D + oc * 512: 2 * D + (oc + 1) * 512]),
                        "wvld", 16)
                ctx.inc(sp.dma_start(out=wv[0:1, NIB, :],
                                     in_=bbf[None, 2 * D + oc * 512: 2 * D + (oc + 1) * 512]),
                        "wvld", 16)

        @blk.tensor
        def _(t):
            for oc in range(NOC):
                t.wait_ge(ctx.sem("wvld"), 32 * (oc + 1))
                for tb in range(16):
                    grp = oc * 16 + tb
                    bank = grp % 2
                    if grp >= 2:
                        t.wait_ge(ctx.sem("evv"), grp - 1)
                    for ib in range(NIB):
                        t.matmul(pv[bank][:], x_sb[:, ib, tb * 128:(tb + 1) * 128],
                                 wv[:, ib, :], start=(ib == 0), stop=False)
                    ins = t.matmul(pv[bank][:], ones[0:1, 0:128],
                                   wv[0:1, NIB, :], start=False, stop=True)
                    ctx.inc(ins, "pev", 1)

        @blk.scalar
        def _(s):
            for oc in range(NOC):
                if oc >= 1:
                    s.wait_ge(ctx.sem("stv"), 16 * oc)
                for tb in range(16):
                    grp = oc * 16 + tb
                    bank = grp % 2
                    s.wait_ge(ctx.sem("pev"), grp + 1)
                    ctx.inc(s.copy(out=v_out[:, tb, :], in_=pv[bank][:]), "evv", 1)

        @blk.gpsimd
        def _(g):
            for oc in range(NOC):
                g.wait_ge(ctx.sem("evv"), (oc + 1) * 16)
                ctx.inc(g.dma_start(
                    out=v_tok.rearrange("(tb p) o -> p tb o", p=128)[:, :, oc * 512:(oc + 1) * 512],
                    in_=v_out[:]), "stv", 16)
            g.wait_ge(ctx.sem("st0"), 16 * (NOB_QK // 2))
            g.wait_ge(ctx.sem("st1"), 16 * (NOB_QK // 2))
            g.wait_ge(ctx.sem("stv"), 16 * NOC)



def build_a1():
    nc = bass.Bass("TRN2", target_bir_lowering=False)
    io = {}
    io["xin"] = nc.dram_tensor("xin", [D, T], F32, kind="ExternalInput").ap()
    io["w_qkv0"] = nc.dram_tensor("w_qkv0", [D, 3 * D], F32, kind="ExternalInput").ap()
    io["b_qkv0"] = nc.dram_tensor("b_qkv0", [3 * D], F32, kind="ExternalInput").ap()
    io["q_fm"] = nc.dram_tensor("q_fm", [D, T], BF16, kind="ExternalOutput").ap()
    io["k_fm"] = nc.dram_tensor("k_fm", [D, T], BF16, kind="ExternalOutput").ap()
    io["v_tok"] = nc.dram_tensor("v_tok", [T, D], BF16, kind="ExternalOutput").ap()
    es = ExitStack()
    ctx = Ctx(nc, es, "1_")
    emit_a1(nc, ctx, es, io)
    return nc




SQ128 = float(1.0 / np.sqrt(128.0))


def emit_a2(nc, ctx, es, io):
    q_fm = io["q_fm"]
    k_fm = io["k_fm"]
    v_tok = io["v_tok"]
    xin = io["xin"]
    w_o = io["w_o0"]
    b_o = io["b_o0"]
    x1 = io["x1"]

    wbf = ctx.dram("wbf_o0", [D, D], BF16)
    bbf = ctx.dram("bbf_o0", [D], BF16)

    NH = 32
    NKB = 32
    NQC = NTC
    NNC = 12
    NEXP = NQC * NKB

    ctx_sb = es.enter_context(ctx.sbuf("ctx_sb", [128, NH, T], BF16))
    ones = es.enter_context(ctx.sbuf("ones", [1, 512], BF16))
    onescol = es.enter_context(ctx.sbuf("onescol", [128, 1], BF16))
    ones32 = es.enter_context(ctx.sbuf("ones32", [1, 128], F32))
    u_mx = es.enter_context(ctx.sbuf("u_mx", [1, 4], F32))
    st1 = ExitStack()
    qh = [st1.enter_context(ctx.sbuf(f"qh{i}", [128, T], BF16)) for i in range(2)]
    kh = [st1.enter_context(ctx.sbuf(f"kh{i}", [128, 2 * T], BF16)) for i in range(2)]
    vh = [st1.enter_context(ctx.sbuf(f"vh{i}", [128, NKB, 128], BF16)) for i in range(2)]
    e_sb = [st1.enter_context(ctx.sbuf(f"e_sb{i}", [128, 1024], BF16)) for i in range(3)]
    sq = [st1.enter_context(ctx.sbuf(f"sq{i}", [128, 512], BF16)) for i in range(2)]
    bias_u = [st1.enter_context(ctx.sbuf(f"bias_u{i}", [128, 1], F32)) for i in range(2)]
    zrow = st1.enter_context(ctx.sbuf("zrow", [1, 512], F32))
    bz_sb = st1.enter_context(ctx.sbuf("bz_sb", [128, 512], F32))

    psc = [es.enter_context(ctx.psum(f"psc{i}", [128, 2, 512], F32)) for i in range(2)]
    pctx = [es.enter_context(ctx.psum(f"pctx{i}", [128, 512], F32)) for i in range(2)]
    pz = [es.enter_context(ctx.psum("pz0", [1, 512], F32))]
    pz.append(pz[0])
    pb = es.enter_context(ctx.psum("pb", [128, 512], F32))
    pbu = pb[:, 0:1]

    NPR = NKB // 2
    NEXP2 = NQC * NPR

    with nc.Block(no_gpsimd_drain=True) as blk:
        @blk.vector
        def _(v):
            v.memset(ones[:], 1.0)
            v.memset(ones32[:], 1.0)
            ctx.inc(v.memset(onescol[:], 1.0), "init", 1)

    with nc.Block(no_gpsimd_drain=True) as blk:
        @blk.gpsimd
        def _(g):
            ctx.inc(g.dma_start(out=bbf[:], in_=b_o), "wcast", 16)
            for j in range(4):
                cs = slice(j * 1024, (j + 1) * 1024)
                ctx.inc(g.dma_start(out=wbf[:, cs], in_=w_o[:, cs]), "wcast", 16)

        @blk.sync
        def _(sp):
            for h in range(NH):
                if h >= 2:
                    sp.wait_ge(ctx.sem("peb"), NQC * (h - 1))
                sem = "hld%d" % (h % 2)
                r = slice(h * 128, (h + 1) * 128)
                ctx.inc(sp.dma_start(out=qh[h % 2][:], in_=q_fm[r, :]), sem, 16)
                ctx.inc(sp.dma_start(out=kh[h % 2][:], in_=k_fm[r, :]), sem, 16)
                ctx.inc(sp.dma_start(
                    out=vh[h % 2][:],
                    in_=v_tok.rearrange("(tb p) o -> p tb o", p=128)[:, :, r]), sem, 16)

        @blk.scalar
        def _(s):
            for h in range(NH):
                s.wait_ge(ctx.sem("hld%d" % (h % 2)), 48 * (h // 2 + 1))
                for j in range(NNC):
                    src, c = (0, j) if j < NQC else (1, j - NQC)
                    buf = qh[h % 2] if src == 0 else kh[h % 2]
                    if h * NNC + j >= 2:
                        s.wait_ge(ctx.sem("pen"), h * NNC + j - 1)
                    ctx.inc(s.activation(out=sq[j % 2][:], in_=buf[:, c * 512:(c + 1) * 512],
                                         func=AF.Square), "actn", 1)
                s.wait_ge(ctx.sem("dven"), h + 1)
                for qc in range(NQC):
                    for pr in range(NPR):
                        i = (h * NQC + qc) * NPR + pr
                        s.wait_ge(ctx.sem("pes"), i + 1)
                        if i >= 3:
                            s.wait_ge(ctx.sem("pez"), i - 2)
                        ctx.inc(s.activation(out=e_sb[i % 3][:], in_=psc[i % 2][:],
                                             func=AF.Exp, bias=bias_u[h % 2][:],
                                             scale=SQ128), "acte", 1)

        @blk.tensor
        def _(t):
            t.wait_ge(ctx.sem("init"), 1)
            for h in range(NH):
                for j in range(NNC):
                    t.wait_ge(ctx.sem("actn"), h * NNC + j + 1)
                    if h * NNC + j >= 2:
                        t.wait_ge(ctx.sem("dvemx"), h * NNC + j - 1)
                    if h >= 1 and j < 2:
                        t.wait_ge(ctx.sem("acte"), h * NEXP2)
                    ctx.inc(t.matmul(psc[j % 2][0:1, 0, :], onescol[:], sq[j % 2][:],
                                     start=True, stop=True), "pen", 1)
                t.wait_ge(ctx.sem("dveu"), h + 1)
                if h >= 1:
                    t.wait_ge(ctx.sem("dven"), h)
                    t.wait_ge(ctx.sem("dvec"), h * NQC)
                ctx.inc(t.matmul(pbu, ones32[0:1, :], u_mx[0:1, 2:3],
                                 start=True, stop=True), "pebu", 1)
                for qc in range(NQC):
                    g = h * NQC + qc
                    qsl = slice(qc * 512, (qc + 1) * 512)
                    for pr in range(NPR + 1):
                        if pr < NPR:
                            i = g * NPR + pr
                            if i >= 2:
                                t.wait_ge(ctx.sem("acte"), i - 1)
                            if pr < 2:
                                t.wait_ge(ctx.sem("dvemx"), (h + 1) * NNC)
                            t.matmul(psc[i % 2][:, 0, :], kh[h % 2][:, (2 * pr) * 128:(2 * pr + 1) * 128],
                                     qh[h % 2][:, qsl], start=True, stop=True)
                            ctx.inc(t.matmul(psc[i % 2][:, 1, :], kh[h % 2][:, (2 * pr + 1) * 128:(2 * pr + 2) * 128],
                                             qh[h % 2][:, qsl], start=True, stop=True), "pes", 1)
                        if pr >= 1:
                            pc_ = pr - 1
                            i = g * NPR + pc_
                            t.wait_ge(ctx.sem("acte"), i + 1)
                            if pc_ == 0 and g >= 2:
                                t.wait_ge(ctx.sem("dvec"), g - 1)
                            if pc_ == 0 and g >= 1:
                                t.wait_ge(ctx.sem("dvez"), 2 * g)
                            t.matmul(pctx[g % 2][:], vh[h % 2][:, 2 * pc_, :], e_sb[i % 3][:, 0:512],
                                     start=(pc_ == 0), stop=False)
                            t.matmul(pctx[g % 2][:], vh[h % 2][:, 2 * pc_ + 1, :], e_sb[i % 3][:, 512:1024],
                                     start=False, stop=(pc_ == NPR - 1))
                            t.matmul(pz[0][:], onescol[:], e_sb[i % 3][:, 0:512],
                                     start=(pc_ == 0), stop=False)
                            ctx.inc(t.matmul(pz[0][:], onescol[:], e_sb[i % 3][:, 512:1024],
                                             start=False, stop=(pc_ == NPR - 1)), "pez", 1)
                    t.wait_ge(ctx.sem("dvez"), 2 * g + 2)
                    if g >= 1:
                        t.wait_ge(ctx.sem("dvec"), g)
                    if qc == 0:
                        t.wait_ge(ctx.sem("dven"), h + 1)
                    ctx.inc(t.matmul(pb[:], ones32[0:1, :], zrow[0:1, :],
                                     start=True, stop=True), "peb", 1)

        @blk.vector
        def _(v):
            for h in range(NH):
                v.memset(u_mx[0:1, 0:1], -1e30)
                for j in range(NNC):
                    v.wait_ge(ctx.sem("pen"), h * NNC + j + 1)
                    v.tensor_reduce(u_mx[0:1, 1:2], psc[j % 2][0:1, 0, :], axis=AX.X, op=ALU.max)
                    v.drain()
                    ctx.inc(v.tensor_tensor(out=u_mx[0:1, 0:1], in0=u_mx[0:1, 0:1],
                                            in1=u_mx[0:1, 1:2], op=ALU.max), "dvemx", 1)
                    v.drain()
                if h >= 1:
                    v.wait_ge(ctx.sem("pebu"), h)
                v.drain()
                ctx.inc(v.tensor_scalar(out=u_mx[0:1, 2:3], in0=u_mx[0:1, 0:1],
                                        scalar1=-SQ128, scalar2=None, op0=ALU.mult), "dveu", 1)
                v.wait_ge(ctx.sem("pebu"), h + 1)
                if h >= 2:
                    v.wait_ge(ctx.sem("acte"), (h - 1) * NEXP2)
                ctx.inc(v.tensor_copy(bias_u[h % 2][:], pbu), "dven", 1)
                for qc in range(NQC):
                    g = h * NQC + qc
                    qsl = slice(qc * 512, (qc + 1) * 512)
                    v.wait_ge(ctx.sem("pez"), (g + 1) * NPR)
                    ctx.inc(v.tensor_scalar(out=zrow[:], in0=pz[0][:],
                                            scalar1=1e-30, scalar2=None, op0=ALU.add), "dvez", 1)
                    v.drain()
                    ctx.inc(v.reciprocal(zrow[:], zrow[:]), "dvez", 1)
                    v.wait_ge(ctx.sem("peb"), g + 1)
                    v.tensor_copy(bz_sb[:], pb[:])
                    v.drain()
                    ctx.inc(v.tensor_tensor(out=ctx_sb[:, h, qsl], in0=pctx[g % 2][:],
                                            in1=bz_sb[:], op=ALU.mult), "dvec", 1)

    NG1 = NH * NQC

    st1.close()
    wo = [es.enter_context(ctx.sbuf(f"wo{i}", [128, NIB + 1, 128], BF16)) for i in range(2)]

    def sync_loads(sp, tc, ob, i):
        if i == 0:
            sp.wait_ge(ctx.sem("wcast"), 16 * 5)
            sp.wait_ge(ctx.sem("dvec"), NG1)
        if i >= 2:
            sp.wait_ge(ctx.sem("Ape2"), i - 1)
        sem = "wld2%d" % (i % 2)
        ctx.inc(sp.dma_start(out=wo[i % 2][:, 0:NIB, :],
                             in_=wbf.rearrange("(ib p) o -> p ib o", p=128)[:, :, ob * 128:(ob + 1) * 128]),
                sem, 16)
        ctx.inc(sp.dma_start(out=wo[i % 2][0:1, NIB, :],
                             in_=bbf[None, ob * 128:(ob + 1) * 128]), sem, 16)

    def pe_mms(t, tc, ob, i, psum):
        if i == 0:
            t.wait_ge(ctx.sem("dvec"), NG1)
        t.wait_ge(ctx.sem("wld2%d" % (i % 2)), 32 * (i // 2 + 1))
        for ib in range(NIB):
            t.matmul(psum[:], wo[i % 2][:, ib, :],
                     ctx_sb[:, ib, tc * 512:(tc + 1) * 512],
                     start=(ib == 0), stop=False)
        return t.matmul(psum[:], wo[i % 2][0:1, NIB, :], ones[0:1, :],
                        start=False, stop=True)

    ln_out_phase(nc, ctx, es, "A", xin, x1, pe_mms, sync_loads,
                 [psc[0][:, 0, :], psc[1][:, 0, :]],
                 [psc[0][0:1, 1, :], psc[1][0:1, 1, :]],
                 pb, pctx[0], ones32, onescol)



def build_a2():
    nc = bass.Bass("TRN2", target_bir_lowering=False)
    io = {}
    io["q_fm"] = nc.dram_tensor("q_fm", [D, T], BF16, kind="ExternalInput").ap()
    io["k_fm"] = nc.dram_tensor("k_fm", [D, 2 * T], BF16, kind="ExternalInput").ap()
    io["v_tok"] = nc.dram_tensor("v_tok", [2 * T, D], BF16, kind="ExternalInput").ap()
    io["xin"] = nc.dram_tensor("xin", [D, T], F32, kind="ExternalInput").ap()
    io["w_o0"] = nc.dram_tensor("w_o0", [D, D], F32, kind="ExternalInput").ap()
    io["b_o0"] = nc.dram_tensor("b_o0", [D], F32, kind="ExternalInput").ap()
    io["x1"] = nc.dram_tensor("x1", [D, T], F32, kind="ExternalOutput").ap()
    es = ExitStack()
    ctx = Ctx(nc, es, "2_")
    emit_a2(nc, ctx, es, io)
    return nc




def ln_out_phase(nc, ctx, es, pfx, xres, xout, pe_mms, sync_loads,
                 psc, pz, pb, pbm, ones32, onescol,
                 pre_sync=None, pre_tensor=None, pre_vector=None,
                 pre_scalar=None, pre_gpsimd=None, first_tensor_wait=None,
                 sc_ob=None, nob=NIB, eps=1e-5, dmodel=None):
    S = lambda n: ctx.sem(pfx + n)
    V = lambda n: ctx.val(pfx + n)
    dmodel = dmodel or (128 * nob)

    xt = [es.enter_context(ctx.sbuf(pfx + f"xt{i}", [128, 512], F32)) for i in range(2)]
    ysb = es.enter_context(ctx.sbuf(pfx + "ysb", [128, nob, 512], BF16))
    y2 = [es.enter_context(ctx.sbuf(pfx + f"y2_{i}", [128, 512], BF16)) for i in range(2)]
    m_row = es.enter_context(ctx.sbuf(pfx + "m_row", [1, 512], F32))
    v_row = es.enter_context(ctx.sbuf(pfx + "v_row", [1, 512], F32))
    m2_row = es.enter_context(ctx.sbuf(pfx + "m2_row", [1, 512], F32))
    ve_row = es.enter_context(ctx.sbuf(pfx + "ve_row", [1, 512], F32))
    s_row = es.enter_context(ctx.sbuf(pfx + "s_row", [1, 512], F32))
    t_row = m2_row
    r_row = es.enter_context(ctx.sbuf(pfx + "r_row", [1, 512], F32))
    mr_row = ve_row
    b1_sb = es.enter_context(ctx.sbuf(pfx + "b1_sb", [128, 512], F32))
    b2_sb = es.enter_context(ctx.sbuf(pfx + "b2_sb", [128, 512], F32))
    xoA1 = es.enter_context(ctx.sbuf(pfx + "xoA", [128, 512], F32))
    xoB = [es.enter_context(ctx.sbuf(pfx + f"xoB{i}", [128, 512], F32)) for i in range(2)]

    with nc.Block(no_gpsimd_drain=True) as blk:
        @blk.sync
        def _(sp):
            for tc in range(NTC):
                if pre_sync:
                    pre_sync(sp, tc)
                for ob in range(nob):
                    i = tc * nob + ob
                    sync_loads(sp, tc, ob, i)
                    if i >= 2:
                        sp.wait_ge(S("dvey"), i - 1)
                    ctx.inc(sp.dma_start(out=xt[i % 2][:],
                                         in_=xres[ob * 128:(ob + 1) * 128, tc * 512:(tc + 1) * 512]),
                            pfx + "xld%d" % (i % 2), 16)

        @blk.tensor
        def _(t):
            if first_tensor_wait:
                first_tensor_wait(t)
            for tc in range(NTC):
                if pre_tensor:
                    pre_tensor(t, tc)
                for ob in range(nob):
                    i = tc * nob + ob
                    if i >= 2:
                        t.wait_ge(S("dvey"), i - 1)
                    ctx.inc(pe_mms(t, tc, ob, i, psc[i % 2]), pfx + "pe2", 1)
                    t.wait_ge(S("dvey"), i + 1)
                    if ob == 0 and tc >= 1:
                        t.wait_ge(S("dvesm"), 2 * tc)
                    t.matmul(pz[0][:], onescol[:], ysb[:, ob, :],
                             start=(ob == 0), stop=(ob == nob - 1))
                    t.wait_ge(S("act2"), i + 1)
                    ctx.inc(t.matmul(pz[1][:], onescol[:], y2[i % 2][:],
                                     start=(ob == 0), stop=(ob == nob - 1)), pfx + "pest", 1)
                t.wait_ge(S("dvesf"), tc + 1)
                if tc >= 1:
                    t.wait_ge(S("dveb"), tc)
                t.matmul(pb[:], ones32[0:1, :], r_row[0:1, :], start=True, stop=True)
                ctx.inc(t.matmul(pbm[:], ones32[0:1, :], mr_row[0:1, :],
                                 start=True, stop=True), pfx + "peb2", 1)

        @blk.vector
        def _(v):
            for tc in range(NTC):
                if pre_vector:
                    pre_vector(v, tc)
                for ob in range(nob):
                    i = tc * nob + ob
                    v.wait_ge(S("pe2"), i + 1)
                    v.wait_ge(S("xld%d" % (i % 2)), 16 * (i // 2 + 1))
                    ctx.inc(v.tensor_tensor(out=ysb[:, ob, :], in0=psc[i % 2][:],
                                            in1=xt[i % 2][:], op=ALU.add), pfx + "dvey", 1)
                v.wait_ge(S("pest"), (tc + 1) * nob)
                v.tensor_scalar(out=m_row[:], in0=pz[0][:], scalar1=1.0 / dmodel, scalar2=None, op0=ALU.mult)
                ctx.inc(v.tensor_scalar(out=v_row[:], in0=pz[1][:], scalar1=1.0 / dmodel,
                                        scalar2=None, op0=ALU.mult), pfx + "dvesm", 2)
                v.drain()
                v.wait_ge(S("actm"), tc + 1)
                v.tensor_tensor(out=ve_row[:], in0=v_row[:], in1=m2_row[:], op=ALU.subtract)
                v.drain()
                ctx.inc(v.tensor_scalar(out=ve_row[:], in0=ve_row[:], scalar1=eps,
                                        scalar2=None, op0=ALU.add), pfx + "dvev", 1)
                v.wait_ge(S("acts2"), tc + 1)
                v.reciprocal(t_row[:], s_row[:])
                v.drain()
                v.tensor_tensor(out=t_row[:], in0=ve_row[:], in1=t_row[:], op=ALU.mult)
                v.drain()
                v.tensor_tensor(out=t_row[:], in0=t_row[:], in1=s_row[:], op=ALU.add)
                v.drain()
                v.tensor_scalar(out=t_row[:], in0=t_row[:], scalar1=0.5, scalar2=None, op0=ALU.mult)
                v.drain()
                v.reciprocal(r_row[:], t_row[:])
                v.drain()
                ctx.inc(v.scalar_tensor_tensor(out=mr_row[:], in0=m_row[:], scalar=-1.0,
                                               in1=r_row[:], op0=ALU.mult, op1=ALU.mult), pfx + "dvesf", 1)
                v.wait_ge(S("peb2"), tc + 1)
                v.tensor_copy(b1_sb[:], pb[:])
                ctx.inc(v.tensor_copy(b2_sb[:], pbm[:]), pfx + "dveb", 1)
                v.drain()
                for ob in range(nob):
                    i = tc * nob + ob
                    if i >= 2:
                        v.wait_ge(S("st2%d" % (i % 2)), 16 * (i // 2))
                    v.tensor_tensor(out=xoA1[:], in0=ysb[:, ob, :], in1=b1_sb[:], op=ALU.mult)
                    v.drain()
                    ctx.inc(v.tensor_tensor(out=xoB[ob % 2][:], in0=xoA1[:],
                                            in1=b2_sb[:], op=ALU.add), pfx + "dvex", 1)
                    v.drain()

        @blk.scalar
        def _(s):
            for tc in range(NTC):
                if pre_scalar:
                    pre_scalar(s, tc)
                for ob in range(nob):
                    i = tc * nob + ob
                    if sc_ob:
                        sc_ob(s, tc, ob, i)
                    s.wait_ge(S("dvey"), i + 1)
                    if i >= 2:
                        s.wait_ge(S("pest"), i - 1)
                    ctx.inc(s.activation(out=y2[i % 2][:], in_=ysb[:, ob, :], func=AF.Square), pfx + "act2", 1)
                s.wait_ge(S("dvesm"), 2 * (tc + 1))
                ctx.inc(s.activation(out=m2_row[:], in_=m_row[:], func=AF.Square), pfx + "actm", 1)
                s.wait_ge(S("dvev"), tc + 1)
                ctx.inc(s.activation(out=s_row[:], in_=ve_row[:], func=AF.Sqrt), pfx + "acts2", 1)

        @blk.gpsimd
        def _(g):
            for tc in range(NTC):
                if pre_gpsimd:
                    pre_gpsimd(g, tc)
                for ob in range(nob):
                    i = tc * nob + ob
                    g.wait_ge(S("dvex"), i + 1)
                    ctx.inc(g.dma_start(out=xout[ob * 128:(ob + 1) * 128, tc * 512:(tc + 1) * 512],
                                        in_=xoB[ob % 2][:]), pfx + "st2%d" % (i % 2), 16)
            g.wait_ge(S("st20"), 16 * NTC * nob // 2)
            g.wait_ge(S("st21"), 16 * NTC * nob // 2)



def emit_b(nc, ctx, es, io):
    x1 = io["x1"]
    w_base = io["w_base"]
    b_base = io["b_base"]
    w_A = io["w_A"]
    w_B = io["w_B"]
    w_router = io["w_router"]
    x2 = io["x2"]

    wbf = ctx.dram("wbf_base", [D, D], BF16)
    bbf = ctx.dram("bbf_base", [D], BF16)
    btbf = ctx.dram("btbf", [8, 128, D], BF16)
    x1bf = ctx.dram("x1bf", [D, T], BF16)

    ones = es.enter_context(ctx.sbuf("ones", [1, 512], BF16))
    onescol = es.enter_context(ctx.sbuf("onescol", [128, 1], BF16))
    ones32 = es.enter_context(ctx.sbuf("ones32", [1, 128], F32))
    ident = es.enter_context(ctx.sbuf("ident", [128, 128], F32))
    abf_sb = es.enter_context(ctx.sbuf("abf_sb", [128, NIB, 64], BF16))
    rbf_sb = es.enter_context(ctx.sbuf("rbf_sb", [128, NIB, 16], BF16))
    rh_sb = es.enter_context(ctx.sbuf("rh_sb", [64, T], BF16))
    ctT_sb = es.enter_context(ctx.sbuf("ctT_sb", [16, T], F32))
    crow = [es.enter_context(ctx.sbuf(f"crow{i}", [1, 512], BF16)) for i in range(2)]

    ident_c = ctx.inline(np.eye(128, dtype=np.float32), "ident_c")

    psc = [es.enter_context(ctx.psum(f"psc{i}", [128, 512], F32)) for i in range(2)]
    pz = [es.enter_context(ctx.psum(f"pz{i}", [1, 512], F32)) for i in range(2)]
    pb = es.enter_context(ctx.psum("pb", [128, 512], F32))
    pbm = es.enter_context(ctx.psum("pbm", [128, 512], F32))
    pbc = [es.enter_context(ctx.psum(f"pbc{i}", [64, 512], F32)) for i in range(2)]
    with nc.Block(no_gpsimd_drain=True) as blk:
        @blk.vector
        def _(v):
            v.memset(ones[:], 1.0)
            v.memset(ones32[:], 1.0)
            ctx.inc(v.memset(onescol[:], 1.0), "init", 1)

        @blk.gpsimd
        def _(g):
            ctx.inc(g.dma_start(out=abf_sb[:],
                                in_=w_A.rearrange("(ib p) r -> p ib r", p=128)), "wcast", 16)
            ctx.inc(g.dma_start(out=rbf_sb[:],
                                in_=w_router.rearrange("(ib p) e -> p ib e", p=128)), "wcast", 16)
            ctx.inc(g.dma_start(out=ident[:], in_=ident_c), "wcast", 16)
            for j in range(4):
                rs = slice(j * 1024, (j + 1) * 1024)
                ctx.inc(g.dma_start(out=x1bf[rs, :], in_=x1[rs, :]), "wcast", 16)

    st1 = ExitStack()
    xb = [st1.enter_context(ctx.sbuf(f"xb{i}", [128, NIB, 512], BF16)) for i in range(2)]
    l_sb = st1.enter_context(ctx.sbuf("l_sb", [128, 16], F32))
    pe_sb = st1.enter_context(ctx.sbuf("pe_sb", [128, 16], F32))
    pn_sb = st1.enter_context(ctx.sbuf("pn_sb", [128, 16], F32))
    pm_sb = st1.enter_context(ctx.sbuf("pm_sb", [128, 16], F32))
    mk1_sb = st1.enter_context(ctx.sbuf("mk1_sb", [128, 16], F32))
    mk2_sb = st1.enter_context(ctx.sbuf("mk2_sb", [128, 16], F32))
    comb_sb = st1.enter_context(ctx.sbuf("comb_sb", [128, 16], F32))
    cstat = st1.enter_context(ctx.sbuf("cstat", [128, 6], F32))

    with nc.Block(no_gpsimd_drain=True) as blk:
        @blk.gpsimd
        def _(g):
            ctx.inc(g.dma_start(out=bbf[:], in_=b_base), "wcb2", 16)
            for j in range(4):
                cs = slice(j * 1024, (j + 1) * 1024)
                ctx.inc(g.dma_start(out=wbf[:, cs], in_=w_base[:, cs]), "wcb2", 16)
            ctx.inc(g.dma_start(out=btbf.rearrange("k p o -> p k o"),
                                in_=w_B.rearrange("k p o -> p k o")), "wcb2", 16)

        @blk.sync
        def _(sp):
            sp.wait_ge(ctx.sem("wcast"), 16 * 7)
            for tc in range(NTC):
                if tc >= 2:
                    sp.wait_ge(ctx.sem("pep"), (tc - 1) * 5)
                ctx.inc(sp.dma_start(
                    out=xb[tc % 2][:],
                    in_=x1bf.rearrange("(ib p) t -> p ib t", p=128)[:, :, tc * 512:(tc + 1) * 512]),
                    "xbld%d" % (tc % 2), 16)

        @blk.tensor
        def _(t):
            t.wait_ge(ctx.sem("init"), 1)
            for tc in range(NTC):
                t.wait_ge(ctx.sem("xbld%d" % (tc % 2)), 16 * (tc // 2 + 1))
                if tc >= 1:
                    t.wait_ge(ctx.sem("dvep"), tc)
                ins = None
                for ib in range(NIB):
                    ins = t.matmul(psc[0][0:64, :], abf_sb[:, ib, :], xb[tc % 2][:, ib, :],
                                   start=(ib == 0), stop=(ib == NIB - 1))
                ctx.inc(ins, "pep", 1)
                for tt in range(4):
                    g = 4 * tc + tt
                    if g >= 1:
                        t.wait_ge(ctx.sem("dvel"), g)
                    ins = None
                    for ib in range(NIB):
                        ins = t.matmul(psc[1][:, 0:16], xb[tc % 2][:, ib, tt * 128:(tt + 1) * 128],
                                       rbf_sb[:, ib, :], start=(ib == 0), stop=(ib == NIB - 1))
                    ctx.inc(ins, "pep", 1)
                    t.wait_ge(ctx.sem("dvec2"), g + 1)
                    if g >= 1:
                        t.wait_ge(ctx.sem("dvet"), g)
                    ctx.inc(t.transpose(pb[0:16, 0:128], comb_sb[:], ident[:]), "pept", 1)

        @blk.vector
        def _(v):
            for tc in range(NTC):
                v.wait_ge(ctx.sem("pep"), tc * 5 + 1)
                ctx.inc(v.tensor_copy(rh_sb[:, tc * 512:(tc + 1) * 512], psc[0][0:64, :]), "dvep", 1)
                for tt in range(4):
                    g = 4 * tc + tt
                    v.wait_ge(ctx.sem("pep"), tc * 5 + tt + 2)
                    v.tensor_reduce(cstat[:, 0:1], psc[1][:, 0:16], axis=AX.X, op=ALU.max)
                    v.drain()
                    v.tensor_scalar(out=cstat[:, 1:2], in0=cstat[:, 0:1], scalar1=-1.0, scalar2=None, op0=ALU.mult)
                    v.drain()
                    ctx.inc(v.tensor_copy(l_sb[:], psc[1][:, 0:16]), "dvel", 1)
                    v.wait_ge(ctx.sem("actp"), g + 1)
                    v.reciprocal(cstat[:, 2:3], cstat[:, 3:4])
                    v.drain()
                    v.tensor_scalar(out=pn_sb[:], in0=pe_sb[:], scalar1=cstat[:, 2:3], scalar2=None, op0=ALU.mult)
                    v.drain()
                    v.tensor_reduce(cstat[:, 0:1], pn_sb[:], axis=AX.X, op=ALU.max)
                    v.drain()
                    v.tensor_scalar(out=mk1_sb[:], in0=pn_sb[:], scalar1=cstat[:, 0:1], scalar2=None, op0=ALU.is_equal)
                    v.drain()
                    v.scalar_tensor_tensor(out=pm_sb[:], in0=mk1_sb[:], scalar=-1e30, in1=pn_sb[:],
                                           op0=ALU.mult, op1=ALU.add)
                    v.drain()
                    v.tensor_reduce(cstat[:, 1:2], pm_sb[:], axis=AX.X, op=ALU.max)
                    v.drain()
                    v.tensor_scalar(out=mk2_sb[:], in0=pm_sb[:], scalar1=cstat[:, 1:2], scalar2=None, op0=ALU.is_equal)
                    v.drain()
                    v.tensor_tensor(out=cstat[:, 2:3], in0=cstat[:, 0:1], in1=cstat[:, 1:2], op=ALU.add)
                    v.drain()
                    v.tensor_scalar(out=cstat[:, 2:3], in0=cstat[:, 2:3], scalar1=1e-6, scalar2=None, op0=ALU.add)
                    v.drain()
                    v.reciprocal(cstat[:, 2:3], cstat[:, 2:3])
                    v.drain()
                    v.tensor_scalar(out=cstat[:, 4:5], in0=cstat[:, 0:1], scalar1=cstat[:, 2:3],
                                    scalar2=0.5, op0=ALU.mult, op1=ALU.mult)
                    v.drain()
                    v.tensor_scalar(out=cstat[:, 5:6], in0=cstat[:, 1:2], scalar1=cstat[:, 2:3],
                                    scalar2=0.5, op0=ALU.mult, op1=ALU.mult)
                    v.drain()
                    v.tensor_scalar(out=comb_sb[:], in0=mk1_sb[:], scalar1=cstat[:, 4:5], scalar2=None, op0=ALU.mult)
                    v.drain()
                    ctx.inc(v.scalar_tensor_tensor(out=comb_sb[:], in0=mk2_sb[:], scalar=cstat[:, 5:6],
                                                   in1=comb_sb[:], op0=ALU.mult, op1=ALU.add), "dvec2", 1)
                    v.wait_ge(ctx.sem("pept"), g + 1)
                    ctx.inc(v.tensor_copy(ctT_sb[:, tc * 512 + tt * 128: tc * 512 + (tt + 1) * 128],
                                          pb[0:16, 0:128]), "dvet", 1)

        @blk.scalar
        def _(s):
            for tc in range(NTC):
                for tt in range(4):
                    g = 4 * tc + tt
                    s.wait_ge(ctx.sem("dvel"), g + 1)
                    ctx.inc(s.activation(out=pe_sb[:], in_=l_sb[:], func=AF.Exp,
                                         bias=cstat[:, 1:2], scale=1.0,
                                         accum_out=cstat[:, 3:4]), "actp", 1)

    st1.close()
    wsl = [es.enter_context(ctx.sbuf(f"wsl{i}", [128, NIB + 9, 128], BF16)) for i in range(2)]
    xb2 = [es.enter_context(ctx.sbuf(f"xb2_{i}", [128, NIB, 512], BF16)) for i in range(2)]
    h_sb = [es.enter_context(ctx.sbuf(f"h_sb{i}", [128, 8, 512], BF16)) for i in range(2)]

    def sync_loads(sp, tc, ob, i):
        if i == 0:
            sp.wait_ge(ctx.sem("wcb2"), 16 * 6)
        if i >= 2:
            sp.wait_ge(ctx.sem("Bpe2"), i - 1)
        sem = "Bwld%d" % (i % 2)
        ctx.inc(sp.dma_start(out=wsl[i % 2][:, 0:NIB, :],
                             in_=wbf.rearrange("(ib p) o -> p ib o", p=128)[:, :, ob * 128:(ob + 1) * 128]),
                sem, 16)
        ctx.inc(sp.dma_start(out=wsl[i % 2][:, NIB:NIB + 8, :],
                             in_=btbf.rearrange("k p o -> p k o")[:, :, ob * 128:(ob + 1) * 128]),
                sem, 16)
        ctx.inc(sp.dma_start(out=wsl[i % 2][0:1, NIB + 8, :],
                             in_=bbf[None, ob * 128:(ob + 1) * 128]), sem, 16)

    def pre_sync(sp, tc):
        if tc == 0:
            sp.wait_ge(ctx.sem("dvet"), 16)
        if tc >= 2:
            sp.wait_ge(ctx.sem("Bpe2"), (tc - 1) * NIB)
        ctx.inc(sp.dma_start(
            out=xb2[tc % 2][:],
            in_=x1bf.rearrange("(ib p) t -> p ib t", p=128)[:, :, tc * 512:(tc + 1) * 512]),
            "Bxbld%d" % (tc % 2), 16)

    def pre_gpsimd(g, tc):
        for e in range(16):
            ge = tc * 16 + e
            if ge == 0:
                g.wait_ge(ctx.sem("dvet"), 16)
            if ge >= 2:
                g.wait_ge(ctx.sem("Bpeh"), ge - 1)
            ctx.inc(g.dma_start(out=crow[e % 2][:],
                                in_=ctT_sb[e:e + 1, tc * 512:(tc + 1) * 512]), "crw%d" % (ge % 2), 16)

    def pre_tensor(t, tc):
        for e in range(16):
            ge = tc * 16 + e
            t.wait_ge(ctx.sem("crw%d" % (ge % 2)), 16 * (ge // 2 + 1))
            if ge >= 2:
                t.wait_ge(ctx.sem("Bdveh"), ge - 1)
            ctx.inc(t.matmul(pbc[ge % 2][:], ones[0:1, 0:64], crow[e % 2][0:1, :],
                             start=True, stop=True), "Bpeh", 1)

    def pre_vector(v, tc):
        for e in range(16):
            ge = tc * 16 + e
            v.wait_ge(ctx.sem("Bpeh"), ge + 1)
            if tc >= 2 and e == 0:
                v.wait_ge(ctx.sem("Bpe2"), (tc - 1) * NIB)
            ctx.inc(v.tensor_tensor(out=h_sb[tc % 2][(e % 2) * 64:(e % 2) * 64 + 64, e // 2, :],
                                    in0=rh_sb[0:64, tc * 512:(tc + 1) * 512],
                                    in1=pbc[ge % 2][:],
                                    op=ALU.mult), "Bdveh", 1)

    def pe_mms(t, tc, ob, i, psum):
        t.wait_ge(ctx.sem("Bwld%d" % (i % 2)), 48 * (i // 2 + 1))
        t.wait_ge(ctx.sem("Bxbld%d" % (tc % 2)), 16 * (tc // 2 + 1))
        t.wait_ge(ctx.sem("Bdveh"), (tc + 1) * 16)
        for ib in range(NIB):
            t.matmul(psum[:], wsl[i % 2][:, ib, :], xb2[tc % 2][:, ib, :],
                     start=(ib == 0), stop=False)
        for kb in range(8):
            t.matmul(psum[:], wsl[i % 2][:, NIB + kb, :], h_sb[tc % 2][:, kb, :],
                     start=False, stop=False)
        return t.matmul(psum[:], wsl[i % 2][0:1, NIB + 8, :], ones[0:1, :],
                        start=False, stop=True)

    def first_wait(t):
        t.wait_ge(ctx.sem("dvel"), 16)
        t.wait_ge(ctx.sem("dvet"), 16)
        t.wait_ge(ctx.sem("dvep"), 4)

    ln_out_phase(nc, ctx, es, "B", x1, x2, pe_mms, sync_loads,
                 psc, pz, pb, pbm, ones32, onescol,
                 pre_sync=pre_sync, pre_tensor=pre_tensor, pre_vector=pre_vector,
                 pre_gpsimd=pre_gpsimd, first_tensor_wait=first_wait)


def build_b():
    nc = bass.Bass("TRN2", target_bir_lowering=False)
    io = {}
    io["x1"] = nc.dram_tensor("x1", [D, T], F32, kind="ExternalInput").ap()
    io["w_base"] = nc.dram_tensor("w_base", [D, D], F32, kind="ExternalInput").ap()
    io["b_base"] = nc.dram_tensor("b_base", [D], F32, kind="ExternalInput").ap()
    io["w_A"] = nc.dram_tensor("w_A", [D, 64], F32, kind="ExternalInput").ap()
    io["w_B"] = nc.dram_tensor("w_B", [8, 128, D], F32, kind="ExternalInput").ap()
    io["w_router"] = nc.dram_tensor("w_router", [D, 16], F32, kind="ExternalInput").ap()
    io["x2"] = nc.dram_tensor("x2", [D, T], F32, kind="ExternalOutput").ap()
    es = ExitStack()
    ctx = Ctx(nc, es, "B_")
    emit_b(nc, ctx, es, io)
    return nc




def emit_c(nc, ctx, es, io):
    xin = io["x2"]
    w = io["w_qkv1"]
    q_fm = io["q1_fm"]
    k_fm = io["k1_fm"]
    v_tok = io["v1_tok"]
    g2 = io.get("gctx")
    wbf = io.get("wbf_qkv1") or ctx.dram("wbf_qkv1", [D, 3 * D], BF16)
    if g2 is not None:
        wcw = lambda sp, j: sp.wait_ge(g2.sem("Cwc%d" % j), 16)
    else:
        wcw = lambda sp, j: sp.wait_ge(ctx.sem("wc%d" % j), 16)

    x_sb = es.enter_context(ctx.sbuf("x_sb", [128, NIB, T], BF16))
    with nc.Block(no_gpsimd_drain=True) as blk:
        @blk.gpsimd
        def _(g):
            ctx.inc(g.dma_start(out=x_sb[:], in_=xin.rearrange("(ib p) t -> p ib t", p=128)), "xld", 16)
            if g2 is None:
                for j in range(8):
                    cs = slice(j * 1536, (j + 1) * 1536)
                    ctx.inc(g.dma_start(out=wbf[:, cs], in_=w[:, cs]), "wc%d" % j, 16)

    wt = [es.enter_context(ctx.sbuf(f"wt{i}", [128, NIB, 128], BF16)) for i in range(2)]
    po = [es.enter_context(ctx.psum(f"po{i}", [128, 512], F32)) for i in range(2)]
    ob_out = [es.enter_context(ctx.sbuf(f"qk_out{i}", [128, T], BF16)) for i in range(2)]

    NOB_QK = 64
    with nc.Block(no_gpsimd_drain=True) as blk:
        @blk.sync
        def _(sp):
            for ob in range(NOB_QK):
                h, which = ob // 2, ob % 2
                off = h * 384 + which * 128
                if ob >= 2:
                    sp.wait_ge(ctx.sem("pe"), (ob - 1) * NTC)
                wcw(sp, (off + 127) // 1536)
                ctx.inc(sp.dma_start(out=wt[ob % 2][:],
                                     in_=wbf.rearrange("(ib p) o -> p ib o", p=128)[:, :, off:off + 128]),
                        "wld%d" % (ob % 2), 16)

        @blk.tensor
        def _(t):
            t.wait_ge(ctx.sem("xld"), 16)
            for ob in range(NOB_QK):
                t.wait_ge(ctx.sem("wld%d" % (ob % 2)), 16 * (ob // 2 + 1))
                for tc in range(NTC):
                    grp = ob * NTC + tc
                    bank = grp % 2
                    if grp >= 2:
                        t.wait_ge(ctx.sem("ev"), grp - 1)
                    ins = None
                    for ib in range(NIB):
                        ins = t.matmul(po[bank][:], wt[ob % 2][:, ib, :],
                                       x_sb[:, ib, tc * 512:(tc + 1) * 512],
                                       start=(ib == 0), stop=(ib == NIB - 1))
                    ctx.inc(ins, "pe", 1)

        @blk.scalar
        def _(s):
            for ob in range(NOB_QK):
                if ob >= 2:
                    s.wait_ge(ctx.sem("st%d" % (ob % 2)), 16 * (ob // 2))
                for tc in range(NTC):
                    grp = ob * NTC + tc
                    bank = grp % 2
                    s.wait_ge(ctx.sem("pe"), grp + 1)
                    ctx.inc(s.activation(out=ob_out[ob % 2][:, tc * 512:(tc + 1) * 512],
                                         in_=po[bank][:], func=AF.Silu), "ev", 1)

        @blk.gpsimd
        def _(g):
            for ob in range(NOB_QK):
                g.wait_ge(ctx.sem("ev"), (ob + 1) * NTC)
                h, which = ob // 2, ob % 2
                dst = q_fm if which == 0 else k_fm
                row = h * 128
                ctx.inc(g.dma_start(out=dst[row:row + 128, :], in_=ob_out[ob % 2][:]),
                        "st%d" % (ob % 2), 16)

    wv = es.enter_context(ctx.sbuf("wv", [128, NIB, 4, 128], BF16))
    pv = [es.enter_context(ctx.psum(f"pv{i}", [128, 512], F32)) for i in range(2)]
    v_out = es.enter_context(ctx.sbuf("v_out", [128, 16, 512], BF16))

    NOC = 8
    wr = wbf.rearrange("i (h x) -> i h x", h=32)
    with nc.Block(no_gpsimd_drain=True) as blk:
        @blk.sync
        def _(sp):
            for oc in range(NOC):
                if oc >= 1:
                    sp.wait_ge(ctx.sem("pev"), oc * 16)
                wcw(sp, 7)
                wcw(sp, (4 * oc * 384 + 256) // 1536)
                for hh in range(4):
                    ctx.inc(sp.dma_start(
                        out=wv[:, :, hh, :],
                        in_=wr.rearrange("(ib p) h x -> p ib h x", p=128)[:, :, 4 * oc + hh, 256:384]),
                        "wvld", 16)

        @blk.tensor
        def _(t):
            for oc in range(NOC):
                t.wait_ge(ctx.sem("wvld"), 64 * (oc + 1))
                for tb in range(16):
                    grp = oc * 16 + tb
                    bank = grp % 2
                    if grp >= 2:
                        t.wait_ge(ctx.sem("evv"), grp - 1)
                    ins = None
                    for ib in range(NIB):
                        ins = t.matmul(pv[bank][:], x_sb[:, ib, tb * 128:(tb + 1) * 128],
                                       wv[:, ib, :, :], start=(ib == 0), stop=(ib == NIB - 1))
                    ctx.inc(ins, "pev", 1)

        @blk.scalar
        def _(s):
            for oc in range(NOC):
                if oc >= 1:
                    s.wait_ge(ctx.sem("stv"), 64 * oc)
                for tb in range(16):
                    grp = oc * 16 + tb
                    bank = grp % 2
                    s.wait_ge(ctx.sem("pev"), grp + 1)
                    ctx.inc(s.activation(out=v_out[:, tb, :], in_=pv[bank][:], func=AF.Silu), "evv", 1)

        @blk.gpsimd
        def _(g):
            for oc in range(NOC):
                g.wait_ge(ctx.sem("evv"), (oc + 1) * 16)
                for hh in range(4):
                    ctx.inc(g.dma_start(
                        out=v_tok.rearrange("(tb p) o -> p tb o", p=128)[:, :, (4 * oc + hh) * 128:(4 * oc + hh + 1) * 128],
                        in_=v_out[:, :, hh * 128:(hh + 1) * 128]), "stv", 16)
            g.wait_ge(ctx.sem("st0"), 16 * (NOB_QK // 2))
            g.wait_ge(ctx.sem("st1"), 16 * (NOB_QK // 2))
            g.wait_ge(ctx.sem("stv"), 64 * NOC)



def build_c():
    nc = bass.Bass("TRN2", target_bir_lowering=False)
    io = {}
    io["x2"] = nc.dram_tensor("x2", [D, T], F32, kind="ExternalInput").ap()
    io["w_qkv1"] = nc.dram_tensor("w_qkv1", [D, 3 * D], F32, kind="ExternalInput").ap()
    io["q1_fm"] = nc.dram_tensor("q1_fm", [D, T], BF16, kind="ExternalOutput").ap()
    io["k1_fm"] = nc.dram_tensor("k1_fm", [D, T], BF16, kind="ExternalOutput").ap()
    io["v1_tok"] = nc.dram_tensor("v1_tok", [T, D], BF16, kind="ExternalOutput").ap()
    es = ExitStack()
    ctx = Ctx(nc, es, "C_")
    emit_c(nc, ctx, es, io)
    return nc



def _l1_decays():
    HEADS, BLOCK = 32, 256
    LAYER_IDX, N_LAYERS = 1, 32
    base = 1.0 / 2.0 ** (8.0 / HEADS)
    exponent = np.arange(HEADS, dtype=np.float64) + 1.0
    factor = 1.0 - LAYER_IDX / (N_LAYERS - 1 + 1e-05) + 1e-05
    slope = (base ** exponent * factor)
    r = np.arange(BLOCK, dtype=np.float64) + 1.0
    q_decay = np.exp(-slope[:, None] * r[None, :])
    k_decay = np.exp(-slope[:, None] * (BLOCK - r)[None, :])
    diff = r[:, None] - r[None, :]
    diag = slope[:, None, None] * diff[None, :, :]
    dd = np.where(diag >= 0, np.exp(-np.maximum(diag, 0.0)), 0.0)
    block_decay = np.exp(-slope * BLOCK)
    return (q_decay.astype(np.float32), k_decay.astype(np.float32),
            dd.astype(np.float32), block_decay.astype(np.float32))



def emit_d(nc, ctx, es, io):
    q_fm = io["q1_fm"]
    k_fm = io["k1_fm"]
    v_tok = io["v1_tok"]
    kv_init = io["kv_init"]
    attn_out = io["attn_d"]
    kv8_out = io["kv8_out"]

    qd, kd, dd, bd = _l1_decays()
    dct = np.ascontiguousarray(np.transpose(dd, (0, 2, 1)).reshape(32, 2, 128, 256))
    dct_c = ctx.inline(dct, "dct_c")
    qd_t = np.ascontiguousarray(np.tile(qd, (1, 8)))
    qd_c = ctx.inline(qd_t, "qd_c")
    kdT = np.ascontiguousarray(np.transpose(kd.reshape(32, 2, 128), (2, 1, 0)))
    kd_c = ctx.inline(kdT, "kd_c")
    ident_c = ctx.inline(np.eye(128, dtype=ml_dtypes.bfloat16).astype(ml_dtypes.bfloat16), "ident_cb")

    NB = 8

    attn_sb = es.enter_context(ctx.sbuf("attn_sb", [128, 32, T], BF16))
    ones32 = es.enter_context(ctx.sbuf("ones32", [1, 128], F32))
    ident = es.enter_context(ctx.sbuf("ident", [128, 128], BF16))
    kd_sb = es.enter_context(ctx.sbuf("kd_sb", [128, 2, 32], F32))
    qh = [es.enter_context(ctx.sbuf(f"qh{i}", [128, T], BF16)) for i in range(2)]
    kh = [es.enter_context(ctx.sbuf(f"kh{i}", [128, T], BF16)) for i in range(2)]
    vh = [es.enter_context(ctx.sbuf(f"vh{i}", [128, 16, 128], BF16)) for i in range(2)]
    dct_sb = [es.enter_context(ctx.sbuf(f"dct_sb{i}", [128, 2, 256], F32)) for i in range(2)]
    qdr = [es.enter_context(ctx.sbuf(f"qdr{i}", [1, T], F32)) for i in range(2)]
    kvi = [es.enter_context(ctx.sbuf(f"kvi{i}", [128, 128], F32)) for i in range(2)]
    qp_sb = es.enter_context(ctx.sbuf("qp_sb", [128, T], BF16))
    msk = es.enter_context(ctx.sbuf("msk", [128, 2, 256], BF16))
    ckp = es.enter_context(ctx.sbuf("ckp", [128, 2, 128], BF16))
    kvf = [es.enter_context(ctx.sbuf(f"kvf{i}", [128, 128], F32)) for i in range(2)]
    kvb = es.enter_context(ctx.sbuf("kvb", [128, 128], BF16))

    pqp = es.enter_context(ctx.psum("pqp", [128, 512], F32))
    ps2 = [es.enter_context(ctx.psum(f"ps2_{i}", [128, 2, 256], F32)) for i in range(2)]
    ptr = es.enter_context(ctx.psum("ptr", [128, 2, 128], BF16))
    po = [es.enter_context(ctx.psum(f"po{i}", [128, 256], F32)) for i in range(2)]
    pkv = es.enter_context(ctx.psum("pkv", [128, 128], F32))
    with nc.Block(no_gpsimd_drain=True) as blk:
        @blk.vector
        def _(v):
            ctx.inc(v.memset(ones32[:], 1.0), "init", 1)

        @blk.gpsimd
        def _(g):
            ctx.inc(g.dma_start(out=ident[:], in_=ident_c), "cld", 16)
            ctx.inc(g.dma_start(out=kd_sb[:], in_=kd_c), "cld", 16)

    with nc.Block(no_gpsimd_drain=True) as blk:
        @blk.sync
        def _(sp):
            for h in range(32):
                if h >= 2:
                    sp.wait_ge(ctx.sem("pek"), (h - 1) * NB)
                sem = "hld%d" % (h % 2)
                r = slice(h * 128, (h + 1) * 128)
                ctx.inc(sp.dma_start(out=qh[h % 2][:], in_=q_fm[r, :]), sem, 16)
                ctx.inc(sp.dma_start(out=kh[h % 2][:], in_=k_fm[r, :]), sem, 16)
                ctx.inc(sp.dma_start(out=vh[h % 2][:],
                                     in_=v_tok.rearrange("(tb p) o -> p tb o", p=128)[:, :, r]), sem, 16)
                ctx.inc(sp.dma_start(out=dct_sb[h % 2][:], in_=dct_c[h].rearrange("kb p q -> p kb q")), sem, 16)
                ctx.inc(sp.dma_start(out=qdr[h % 2][:], in_=qd_c[h][None, :]), sem, 16)
                ctx.inc(sp.dma_start(out=kvi[h % 2][:], in_=kv_init[h]), sem, 16)

        @blk.tensor
        def _(t):
            t.wait_ge(ctx.sem("init"), 1)
            t.wait_ge(ctx.sem("cld"), 32)
            for h in range(32):
                t.wait_ge(ctx.sem("hld%d" % (h % 2)), 96 * (h // 2 + 1))
                for c in range(NTC):
                    if h * NTC + c >= 1:
                        t.wait_ge(ctx.sem("dveq"), h * NTC + c)
                    ctx.inc(t.matmul(pqp[:], ones32[0:1, :], qdr[h % 2][0:1, c * 512:(c + 1) * 512],
                                     start=True, stop=True), "peq", 1)
                for n in range(NB):
                    gi = h * NB + n
                    if gi >= 2:
                        t.wait_ge(ctx.sem("dvem"), 2 * (gi - 1))
                    t.wait_ge(ctx.sem("dveq"), h * NTC + NTC)
                    t.matmul(ps2[gi % 2][:, 0, :], kh[h % 2][:, n * 256:n * 256 + 128],
                             qh[h % 2][:, n * 256:(n + 1) * 256], start=True, stop=False)
                    ctx.inc(t.matmul(ps2[gi % 2][:, 1, :], kh[h % 2][:, n * 256 + 128:(n + 1) * 256],
                                     qh[h % 2][:, n * 256:(n + 1) * 256], start=False, stop=True), "pes", 1)
                    if gi >= 1:
                        t.wait_ge(ctx.sem("dvec"), 2 * gi)
                    t.transpose(ptr[:, 0, :], kh[h % 2][:, n * 256:n * 256 + 128], ident[:])
                    ctx.inc(t.transpose(ptr[:, 1, :], kh[h % 2][:, n * 256 + 128:(n + 1) * 256],
                                        ident[:]), "pet", 1)
                    t.wait_ge(ctx.sem("dvem"), 2 * gi + 2)
                    if gi >= 2:
                        t.wait_ge(ctx.sem("dveo"), gi - 1)
                    t.matmul(po[gi % 2][:], vh[h % 2][:, 2 * n, :], msk[:, 0, :],
                             start=True, stop=False)
                    t.matmul(po[gi % 2][:], vh[h % 2][:, 2 * n + 1, :], msk[:, 1, :],
                             start=False, stop=False)
                    t.wait_ge(ctx.sem("dvek"), h * (NB + 1) + n + 1)
                    ctx.inc(t.matmul(po[gi % 2][:], kvb[:], qp_sb[:, n * 256:(n + 1) * 256],
                                     start=False, stop=True), "peo", 1)
                    t.wait_ge(ctx.sem("dvec"), 2 * gi + 2)
                    t.matmul(pkv[:], ckp[:, 0, :], vh[h % 2][:, 2 * n, :], start=True, stop=False)
                    ctx.inc(t.matmul(pkv[:], ckp[:, 1, :], vh[h % 2][:, 2 * n + 1, :],
                                     start=False, stop=True), "pek", 1)

        @blk.vector
        def _(v):
            for h in range(32):
                for c in range(NTC):
                    v.wait_ge(ctx.sem("peq"), h * NTC + c + 1)
                    ctx.inc(v.tensor_tensor(out=qp_sb[:, c * 512:(c + 1) * 512],
                                            in0=qh[h % 2][:, c * 512:(c + 1) * 512],
                                            in1=pqp[:], op=ALU.mult), "dveq", 1)
                if h >= 1:
                    v.wait_ge(ctx.sem("pek"), h * NB)
                    v.wait_ge(ctx.sem("stk%d" % ((h - 1) % 2)), 16 * ((h - 1) // 2 + 1))
                v.tensor_copy(kvf[0][:], kvi[h % 2][:])
                v.drain()
                ctx.inc(v.tensor_copy(kvb[:], kvf[0][:]), "dvek", 1)
                for n in range(NB):
                    gi = h * NB + n
                    v.wait_ge(ctx.sem("pes"), gi + 1)
                    v.tensor_tensor(out=msk[:, 0, :], in0=ps2[gi % 2][:, 0, :],
                                    in1=dct_sb[h % 2][:, 0, :], op=ALU.mult)
                    ctx.inc(v.tensor_tensor(out=msk[:, 1, :], in0=ps2[gi % 2][:, 1, :],
                                            in1=dct_sb[h % 2][:, 1, :], op=ALU.mult), "dvem", 2)
                    v.wait_ge(ctx.sem("pet"), gi + 1)
                    v.tensor_scalar(out=ckp[:, 0, :], in0=ptr[:, 0, :],
                                    scalar1=kd_sb[:, 0, h:h + 1], scalar2=None, op0=ALU.mult)
                    ctx.inc(v.tensor_scalar(out=ckp[:, 1, :], in0=ptr[:, 1, :],
                                            scalar1=kd_sb[:, 1, h:h + 1], scalar2=None, op0=ALU.mult),
                            "dvec", 2)
                    v.wait_ge(ctx.sem("peo"), gi + 1)
                    ctx.inc(v.tensor_copy(attn_sb[:, h, n * 256:(n + 1) * 256], po[gi % 2][:]),
                            "dveo", 1)
                    v.wait_ge(ctx.sem("pek"), gi + 1)
                    v.scalar_tensor_tensor(out=kvf[(n + 1) % 2][:], in0=kvf[n % 2][:],
                                           scalar=float(bd[h]), in1=pkv[:],
                                           op0=ALU.mult, op1=ALU.add)
                    v.drain()
                    ctx.inc(v.tensor_copy(kvb[:], kvf[(n + 1) % 2][:]), "dvek", 1)
                    v.drain()

        @blk.gpsimd
        def _(g):
            for h in range(32):
                g.wait_ge(ctx.sem("dvek"), h * (NB + 1) + NB + 1)
                ctx.inc(g.dma_start(out=kv8_out[h], in_=kvf[NB % 2][:]), "stk%d" % (h % 2), 16)
            g.wait_ge(ctx.sem("dveo"), 32 * NB)
            for ib in range(32):
                ctx.inc(g.dma_start(out=attn_out[ib * 128:(ib + 1) * 128, :],
                                    in_=attn_sb[:, ib, :]), "sta", 16)
            g.wait_ge(ctx.sem("sta"), 16 * 32)
            g.wait_ge(ctx.sem("stk0"), 16 * 16)
            g.wait_ge(ctx.sem("stk1"), 16 * 16)



def build_d():
    nc = bass.Bass("TRN2", target_bir_lowering=False)
    io = {}
    io["q1_fm"] = nc.dram_tensor("q1_fm", [D, T], BF16, kind="ExternalInput").ap()
    io["k1_fm"] = nc.dram_tensor("k1_fm", [D, T], BF16, kind="ExternalInput").ap()
    io["v1_tok"] = nc.dram_tensor("v1_tok", [T, D], BF16, kind="ExternalInput").ap()
    io["kv_init"] = nc.dram_tensor("kv_init", [32, 128, 128], F32, kind="ExternalInput").ap()
    io["attn_d"] = nc.dram_tensor("attn_d", [D, T], BF16, kind="ExternalOutput").ap()
    io["kv8_out"] = nc.dram_tensor("kv8_out", [32, 128, 128], F32, kind="ExternalOutput").ap()
    es = ExitStack()
    ctx = Ctx(nc, es, "D_")
    emit_d(nc, ctx, es, io)
    return nc





def emit_e(nc, ctx, es, io):
    flag = io.get("flag")
    kv8ag = io.get("kv8_ag")
    sel = io.get("sel")
    attn_d = io["attn_d"]
    kv8c = io["kv8c"]
    q1_fm = io["q1_fm"]
    x2 = io["x2"]
    w_gate = io["w_gate"]
    w_out1 = io["w_out1"]
    norm_w = io["norm_w"]
    x3 = io["x3"]

    g2 = io.get("gctx")
    wgbf = io.get("wgbf") or ctx.dram("wgbf", [D, D], BF16)
    wobf = io.get("wobf") or ctx.dram("wobf", [D, D], BF16)
    x2bf = ctx.dram("x2bf_e", [D, T], BF16)

    qd, kd, dd, bd = _l1_decays()
    qcorr = np.empty((32, T), np.float32)
    for h in range(32):
        for m in range(8):
            qcorr[h, m * 256:(m + 1) * 256] = qd[h] * (bd[h] ** m)
    qcorr_c = ctx.inline(qcorr.astype(ml_dtypes.bfloat16), "qcorr_c")

    onescol = es.enter_context(ctx.sbuf("onescol", [128, 1], BF16))
    ones32 = es.enter_context(ctx.sbuf("ones32", [1, 128], F32))
    onesb = es.enter_context(ctx.sbuf("onesb", [1, 128], BF16))
    normw_sb = es.enter_context(ctx.sbuf("normw_sb", [128, NIB], F32))
    kv8b = es.enter_context(ctx.sbuf("kv8b", [128, 32, 128], BF16))
    flag_sb = es.enter_context(ctx.sbuf("flag_sb", [128, 1], F32))
    if kv8ag is not None:
        kvr = [es.enter_context(ctx.sbuf(f"kvr{i}", [128, 8, 128], BF16)) for i in range(2)]
        sel_sb = es.enter_context(ctx.sbuf("sel_sb", [128, 8], F32))
    at_sb = es.enter_context(ctx.sbuf("at_sb", [128, NIB, 512], BF16))
    xg_sb = es.enter_context(ctx.sbuf("xg_sb", [128, NIB, 512], BF16))
    q1t = [es.enter_context(ctx.sbuf(f"q1t{i}", [128, 512], BF16)) for i in range(2)]
    qcr = [es.enter_context(ctx.sbuf(f"qcr{i}", [1, 512], BF16)) for i in range(2)]
    cqs = es.enter_context(ctx.sbuf("cqs", [128, 512], BF16))
    rms2 = [es.enter_context(ctx.sbuf(f"rms2_{i}", [128, 512], BF16)) for i in range(2)]
    gt_sb = [es.enter_context(ctx.sbuf(f"gt_sb{i}", [128, 512], BF16)) for i in range(2)]
    sE = es.enter_context(ctx.sbuf("sE", [1, 512], F32))
    veE = es.enter_context(ctx.sbuf("veE", [1, 512], F32))
    tE = es.enter_context(ctx.sbuf("tE", [1, 512], F32))
    rE = es.enter_context(ctx.sbuf("rE", [1, 512], F32))
    wg = [es.enter_context(ctx.sbuf(f"wg{i}", [128, NIB, 128], BF16)) for i in range(2)]
    wo = [es.enter_context(ctx.sbuf(f"wo{i}", [128, NIB, 128], BF16)) for i in range(2)]

    psc = [es.enter_context(ctx.psum(f"psc{i}", [128, 512], F32)) for i in range(2)]
    pz = [es.enter_context(ctx.psum(f"pz{i}", [1, 512], F32)) for i in range(2)]
    pb = es.enter_context(ctx.psum("pb", [128, 512], F32))
    pbm = es.enter_context(ctx.psum("pbm", [128, 512], F32))
    pq = es.enter_context(ctx.psum("pq", [128, 512], F32))
    pc = es.enter_context(ctx.psum("pc", [128, 512], F32))
    with nc.Block(no_gpsimd_drain=True) as blk:
        @blk.vector
        def _(v):
            v.memset(ones32[:], 1.0)
            v.memset(onesb[:], 1.0)
            ctx.inc(v.memset(onescol[:], 1.0), "init", 1)

        @blk.gpsimd
        def _(g):
            for j in range(4):
                cs = slice(j * 1024, (j + 1) * 1024)
                if g2 is None:
                    ctx.inc(g.dma_start(out=wgbf[:, cs], in_=w_gate[:, cs]), "wcg", 16)
                    ctx.inc(g.dma_start(out=wobf[:, cs], in_=w_out1[:, cs]), "wco", 16)
                ctx.inc(g.dma_start(out=x2bf[cs, :], in_=x2[cs, :]), "wcx", 16)
            if kv8ag is None:
                ctx.inc(g.dma_start(out=kv8b[:], in_=kv8c.rearrange("h p e -> p h e")), "wck", 16)
            else:
                ctx.inc(g.dma_start(out=flag_sb[0:1, 0:1], in_=kv8c[0][0:1, 0:1]), "wck", 16)
            ctx.inc(g.dma_start(out=normw_sb[:], in_=norm_w.rearrange("(ib p) -> p ib", p=128), allow_slow_non_contiguous=True), "wck", 16)
            if flag is not None:
                ctx.inc(g.dma_start(out=flag_sb[:], in_=flag), "wck", 16)

    if kv8ag is not None:
        with nc.Block(no_gpsimd_drain=True) as blk:
            @blk.gpsimd
            def _(g):
                ctx.inc(g.dma_start(out=sel_sb[:], in_=sel), "Esel", 16)
                for h in range(32):
                    if h >= 2:
                        g.wait_ge(ctx.sem("Ekv"), h - 1)
                    for r in range(8):
                        ctx.inc(g.dma_start(out=kvr[h % 2][:, r, :],
                                            in_=kv8ag[(r * 32 + h) * 128:(r * 32 + h + 1) * 128, :]),
                                "Ekvl%d" % (h % 2), 16)

            @blk.vector
            def _(v):
                v.wait_ge(ctx.sem("Esel"), 16)
                for h in range(32):
                    v.wait_ge(ctx.sem("Ekvl%d" % (h % 2)), 128 * (h // 2 + 1))
                    v.tensor_scalar(out=kv8b[:, h, :], in0=kvr[h % 2][:, 0, :],
                                    scalar1=sel_sb[:, 0:1], scalar2=None, op0=ALU.mult)
                    v.drain()
                    for r in range(1, 8):
                        v.scalar_tensor_tensor(out=kv8b[:, h, :], in0=kvr[h % 2][:, r, :],
                                               scalar=sel_sb[:, r:r + 1], in1=kv8b[:, h, :],
                                               op0=ALU.mult, op1=ALU.add)
                        v.drain()
                    ctx.inc(v.nop(), "Ekv", 1)

    if flag is not None:
        with nc.Block(no_gpsimd_drain=True) as blk:
            @blk.vector
            def _(v):
                v.wait_ge(ctx.sem("wck"), 48)
                ctx.inc(v.tensor_scalar(out=kv8b[:].rearrange("p h e -> p (h e)"),
                                        in0=kv8b[:].rearrange("p h e -> p (h e)"),
                                        scalar1=flag_sb[:], scalar2=None, op0=ALU.mult), "Efl", 1)

    def pre_sync(sp, tc):
        if tc == 0:
            sp.wait_ge(ctx.sem("wcx"), 64)
            if g2 is None:
                sp.wait_ge(ctx.sem("wcg"), 64)
                sp.wait_ge(ctx.sem("wco"), 64)
            else:
                sp.wait_ge(g2.sem("Ewcg"), 64)
                sp.wait_ge(g2.sem("Ewco"), 64)
        if tc >= 1:
            sp.wait_ge(ctx.sem("Lpe2"), tc * NIB)
        ctx.inc(sp.dma_start(out=at_sb[:],
                             in_=attn_d.rearrange("(ib p) t -> p ib t", p=128)[:, :, tc * 512:(tc + 1) * 512]),
                "Eat%d" % (tc % 2), 16)
        ctx.inc(sp.dma_start(out=xg_sb[:],
                             in_=x2bf.rearrange("(ib p) t -> p ib t", p=128)[:, :, tc * 512:(tc + 1) * 512]),
                "Exg%d" % (tc % 2), 16)
        for h in range(32):
            i = tc * 32 + h
            if i >= 2:
                sp.wait_ge(ctx.sem("Edvq"), i - 1)
            ctx.inc(sp.dma_start(out=q1t[h % 2][:],
                                 in_=q1_fm[h * 128:(h + 1) * 128, tc * 512:(tc + 1) * 512]),
                    "Eq1%d" % (i % 2), 16)
            ctx.inc(sp.dma_start(out=qcr[h % 2][:],
                                 in_=qcorr_c[h][None, tc * 512:(tc + 1) * 512]),
                    "Eq1%d" % (i % 2), 16)
        for ob in range(NIB):
            i = tc * NIB + ob
            if i >= 2:
                sp.wait_ge(ctx.sem("Epeg"), i - 1)
            ctx.inc(sp.dma_start(out=wg[i % 2][:],
                                 in_=wgbf.rearrange("(ib p) o -> p ib o", p=128)[:, :, ob * 128:(ob + 1) * 128]),
                    "Ewg%d" % (i % 2), 16)


    def pre_tensor(t, tc):
        if tc == 0:
            t.wait_ge(ctx.sem("init"), 1)
            t.wait_ge(ctx.sem("wck"), 32)
            if flag is not None:
                t.wait_ge(ctx.sem("Efl"), 1)
            if kv8ag is not None:
                t.wait_ge(ctx.sem("Ekv"), 32)
        for h in range(32):
            i = tc * 32 + h
            t.wait_ge(ctx.sem("Eq1%d" % (i % 2)), 32 * (i // 2 + 1))
            if i >= 1:
                t.wait_ge(ctx.sem("Edvq"), i)
            if h == 0 and tc >= 1:
                t.wait_ge(ctx.sem("Eacg"), tc * NIB)
            ctx.inc(t.matmul(pq[:], onesb[0:1, :], qcr[h % 2][0:1, :],
                             start=True, stop=True), "Epq", 1)
            t.wait_ge(ctx.sem("Edvq"), i + 1)
            if i >= 1:
                t.wait_ge(ctx.sem("Edvc"), i)
            ctx.inc(t.matmul(pc[:], kv8b[:, h, :], cqs[:], start=True, stop=True), "Epc", 1)
        for ib in range(NIB):
            i = tc * NIB + ib
            t.wait_ge(ctx.sem("Eact2"), i + 1)
            ctx.inc(t.matmul(pb[0:1, :], onescol[:], rms2[i % 2][:],
                             start=(ib == 0), stop=(ib == NIB - 1)), "Epst", 1)
        t.wait_ge(ctx.sem("Edvr"), tc + 1)
        ctx.inc(t.matmul(pbm[:], ones32[0:1, :], rE[0:1, :], start=True, stop=True), "Epb1", 1)
        for ob in range(NIB):
            i = tc * NIB + ob
            t.wait_ge(ctx.sem("Ewg%d" % (i % 2)), 16 * (i // 2 + 1))
            t.wait_ge(ctx.sem("Exg%d" % (tc % 2)), 16 * (tc // 2 + 1))
            if i >= 2:
                t.wait_ge(ctx.sem("Eacg"), i - 1)
            bank = pq if ob % 2 == 0 else pc
            ins = None
            for ib in range(NIB):
                ins = t.matmul(bank[:], wg[i % 2][:, ib, :], xg_sb[:, ib, :],
                               start=(ib == 0), stop=(ib == NIB - 1))
            ctx.inc(ins, "Epeg", 1)

    def pre_vector(v, tc):
        for h in range(32):
            i = tc * 32 + h
            v.wait_ge(ctx.sem("Epq"), i + 1)
            if tc >= 1 and h == 0:
                v.wait_ge(ctx.sem("Lpe2"), tc * NIB)
            v.wait_ge(ctx.sem("Eat%d" % (tc % 2)), 16 * (tc // 2 + 1))
            ctx.inc(v.tensor_tensor(out=cqs[:], in0=q1t[h % 2][:], in1=pq[:], op=ALU.mult), "Edvq", 1)
            v.drain()
            v.wait_ge(ctx.sem("Epc"), i + 1)
            ctx.inc(v.tensor_tensor(out=at_sb[:, h, :], in0=at_sb[:, h, :], in1=pc[:], op=ALU.add), "Edvc", 1)
            v.drain()
        v.wait_ge(ctx.sem("Epst"), (tc + 1) * NIB)
        ctx.inc(v.tensor_scalar(out=veE[:], in0=pb[0:1, :], scalar1=1.0 / D, scalar2=1e-5,
                                op0=ALU.mult, op1=ALU.add), "Edvv", 1)
        v.drain()
        v.wait_ge(ctx.sem("Eacs"), tc + 1)
        v.reciprocal(tE[:], sE[:])
        v.drain()
        v.tensor_tensor(out=tE[:], in0=veE[:], in1=tE[:], op=ALU.mult)
        v.drain()
        v.tensor_tensor(out=tE[:], in0=tE[:], in1=sE[:], op=ALU.add)
        v.drain()
        v.tensor_scalar(out=tE[:], in0=tE[:], scalar1=0.5, scalar2=None, op0=ALU.mult)
        v.drain()
        ctx.inc(v.reciprocal(rE[:], tE[:]), "Edvr", 1)
        v.drain()
        v.wait_ge(ctx.sem("Epb1"), tc + 1)
        for ib in range(NIB):
            i = tc * NIB + ib
            v.wait_ge(ctx.sem("Eact2"), i + 1)
            ctx.inc(v.scalar_tensor_tensor(out=at_sb[:, ib, :], in0=at_sb[:, ib, :],
                                           scalar=normw_sb[:, ib:ib + 1], in1=pbm[:],
                                           op0=ALU.mult, op1=ALU.mult), "Edvn", 1)
            v.drain()
        for ob in range(NIB):
            i = tc * NIB + ob
            v.wait_ge(ctx.sem("Eacg"), i + 1)
            ctx.inc(v.tensor_tensor(out=at_sb[:, ob, :], in0=at_sb[:, ob, :],
                                    in1=gt_sb[ob % 2][:], op=ALU.mult), "Edgn", 1)
            v.drain()

    def pre_scalar(s, tc):
        for ib in range(NIB):
            i = tc * NIB + ib
            s.wait_ge(ctx.sem("Edvc"), tc * 32 + ib + 1)
            if i >= 2:
                s.wait_ge(ctx.sem("Epst"), i - 1)
            ctx.inc(s.activation(out=rms2[i % 2][:], in_=at_sb[:, ib, :], func=AF.Square), "Eact2", 1)
        s.wait_ge(ctx.sem("Edvv"), tc + 1)
        ctx.inc(s.activation(out=sE[:], in_=veE[:], func=AF.Sqrt), "Eacs", 1)
        for ob in range(NIB):
            i = tc * NIB + ob
            s.wait_ge(ctx.sem("Epeg"), i + 1)
            s.wait_ge(ctx.sem("Edgn"), tc * NIB + max(0, ob - 1))
            bank = pq if ob % 2 == 0 else pc
            ctx.inc(s.activation(out=gt_sb[ob % 2][:], in_=bank[:], func=AF.Sigmoid), "Eacg", 1)

    def sync_loads(sp, tc, ob, i):
        if i >= 2:
            sp.wait_ge(ctx.sem("Lpe2"), i - 1)
        ctx.inc(sp.dma_start(out=wo[i % 2][:],
                             in_=wobf.rearrange("(ib p) o -> p ib o", p=128)[:, :, ob * 128:(ob + 1) * 128]),
                "Ewo%d" % (i % 2), 16)

    def pe_mms(t, tc, ob, i, psum):
        t.wait_ge(ctx.sem("Ewo%d" % (i % 2)), 16 * (i // 2 + 1))
        t.wait_ge(ctx.sem("Edgn"), (tc + 1) * NIB)
        ins = None
        for ib in range(NIB):
            ins = t.matmul(psum[:], wo[i % 2][:, ib, :], at_sb[:, ib, :],
                           start=(ib == 0), stop=(ib == NIB - 1))
        return ins

    ln_out_phase(nc, ctx, es, "L", x2, x3, pe_mms, sync_loads,
                 psc, pz, pb, pbm, ones32, onescol,
                 pre_sync=pre_sync, pre_tensor=pre_tensor, pre_vector=pre_vector,
                 pre_scalar=pre_scalar)


def build_e():
    nc = bass.Bass("TRN2", target_bir_lowering=False)
    io = {}
    io["attn_d"] = nc.dram_tensor("attn_d", [D, T], BF16, kind="ExternalInput").ap()
    io["kv8c"] = nc.dram_tensor("kv8c", [32, 128, 128], F32, kind="ExternalInput").ap()
    io["q1_fm"] = nc.dram_tensor("q1_fm", [D, T], BF16, kind="ExternalInput").ap()
    io["x2"] = nc.dram_tensor("x2", [D, T], F32, kind="ExternalInput").ap()
    io["w_gate"] = nc.dram_tensor("w_gate", [D, D], F32, kind="ExternalInput").ap()
    io["w_out1"] = nc.dram_tensor("w_out1", [D, D], F32, kind="ExternalInput").ap()
    io["norm_w"] = nc.dram_tensor("norm_w", [D], F32, kind="ExternalInput").ap()
    io["x3"] = nc.dram_tensor("x3", [D, T], F32, kind="ExternalOutput").ap()
    es = ExitStack()
    ctx = Ctx(nc, es, "E_")
    emit_e(nc, ctx, es, io)
    return nc





def emit_f(nc, ctx, es, io):
    x3 = io["x3"]
    up = io["up"]
    down = io["down"]
    x4 = io["x4"]

    upbf = ctx.dram("upbf", [16, 256, 1024], BF16)
    dnbf = ctx.dram("dnbf", [16, 1024, 256], BF16)
    x3bf = ctx.dram("x3bf", [D, T], BF16)

    onescol = es.enter_context(ctx.sbuf("onescol", [128, 1], BF16))
    ones32 = es.enter_context(ctx.sbuf("ones32", [1, 128], F32))
    xf_sb = [es.enter_context(ctx.sbuf(f"xf_sb{i}", [128, NIB, 512], BF16)) for i in range(2)]
    up_sb = [es.enter_context(ctx.sbuf(f"up_sb{i}", [128, 2, 1024], BF16)) for i in range(2)]
    dn_sb = [es.enter_context(ctx.sbuf(f"dn_sb{i}", [128, 8, 256], BF16)) for i in range(2)]
    mid_sb = [es.enter_context(ctx.sbuf(f"mid_sb{i}", [128, 8, 512], BF16)) for i in range(2)]

    psc = [es.enter_context(ctx.psum(f"psc{i}", [128, 512], F32)) for i in range(2)]
    pz = [es.enter_context(ctx.psum(f"pz{i}", [1, 512], F32)) for i in range(2)]
    pb = es.enter_context(ctx.psum("pb", [128, 512], F32))
    pbm = es.enter_context(ctx.psum("pbm", [128, 512], F32))
    pmid = [es.enter_context(ctx.psum(f"pmid{i}", [128, 512], F32)) for i in range(2)]
    with nc.Block(no_gpsimd_drain=True) as blk:
        @blk.vector
        def _(v):
            v.memset(ones32[:], 1.0)
            ctx.inc(v.memset(onescol[:], 1.0), "init", 1)

        @blk.gpsimd
        def _(g):
            ctx.inc(g.dma_start(out=upbf.rearrange("h d m -> (h d) m"),
                                in_=up.rearrange("h d m -> (h d) m")), "wcu", 16)
            ctx.inc(g.dma_start(out=dnbf.rearrange("h m d -> (h m) d"),
                                in_=down.rearrange("h m d -> (h m) d")), "wcu", 16)
            for j in range(4):
                cs = slice(j * 1024, (j + 1) * 1024)
                ctx.inc(g.dma_start(out=x3bf[cs, :], in_=x3[cs, :]), "wcu", 16)

    def pre_sync(sp, tc):
        if tc >= 2:
            sp.wait_ge(ctx.sem("Fpu"), 8 * ((tc - 1) * 16))
        if tc == 0:
            sp.wait_ge(ctx.sem("wcu"), 96)
        ctx.inc(sp.dma_start(out=xf_sb[tc % 2][:],
                             in_=x3bf.rearrange("(ib p) t -> p ib t", p=128)[:, :, tc * 512:(tc + 1) * 512]),
                "Fxf%d" % (tc % 2), 16)

    def sync_loads(sp, tc, ob, i):
        if ob % 2 == 1:
            return
        h = ob // 2
        g = tc * 16 + h
        if g >= 2:
            sp.wait_ge(ctx.sem("Fpu"), 8 * (g - 1))
        ctx.inc(sp.dma_start(out=up_sb[g % 2][:],
                             in_=upbf[h].rearrange("(ib p) m -> p ib m", p=128)), "Fup%d" % (g % 2), 16)
        if g >= 2:
            sp.wait_ge(ctx.sem("Fpe2"), tc * NIB + ob - 2)
        ctx.inc(sp.dma_start(out=dn_sb[g % 2][:],
                             in_=dnbf[h].rearrange("(mb p) d -> p mb d", p=128)), "Fup%d" % (g % 2), 16)

    def pe_mms(t, tc, ob, i, psum):
        h = ob // 2
        g = tc * 16 + h
        db = ob % 2
        if db == 0:
            t.wait_ge(ctx.sem("Fup%d" % (g % 2)), 32 * (g // 2 + 1))
            t.wait_ge(ctx.sem("Fxf%d" % (tc % 2)), 16 * (tc // 2 + 1))
            for mb in range(8):
                gm = g * 8 + mb
                if gm >= 2:
                    t.wait_ge(ctx.sem("Fgel"), gm - 1)
                t.matmul(pmid[gm % 2][:], up_sb[g % 2][:, 0, mb * 128:(mb + 1) * 128],
                         xf_sb[tc % 2][:, 2 * h, :], start=True, stop=False)
                ctx.inc(t.matmul(pmid[gm % 2][:], up_sb[g % 2][:, 1, mb * 128:(mb + 1) * 128],
                                 xf_sb[tc % 2][:, 2 * h + 1, :], start=False, stop=True), "Fpu", 1)
            t.wait_ge(ctx.sem("Fgel"), (g + 1) * 8)
        ins = None
        for mb in range(8):
            ins = t.matmul(psum[:], dn_sb[g % 2][:, mb, db * 128:(db + 1) * 128],
                           mid_sb[g % 2][:, mb, :], start=(mb == 0), stop=(mb == 7))
        return ins

    def sc_ob(s, tc, ob, i):
        if ob % 2 == 1:
            return
        h = ob // 2
        g = tc * 16 + h
        for mb in range(8):
            gm = g * 8 + mb
            s.wait_ge(ctx.sem("Fpu"), gm + 1)
            if g >= 2 and mb == 0:
                s.wait_ge(ctx.sem("Fpe2"), tc * NIB + ob - 1)
            ctx.inc(s.activation(out=mid_sb[g % 2][:, mb, :], in_=pmid[gm % 2][:],
                                 func=AF.Gelu), "Fgel", 1)

    def first_wait(t):
        t.wait_ge(ctx.sem("init"), 1)

    ln_out_phase(nc, ctx, es, "F", x3, x4, pe_mms, sync_loads,
                 psc, pz, pb, pbm, ones32, onescol,
                 pre_sync=pre_sync, sc_ob=sc_ob, first_tensor_wait=first_wait)


def build_f():
    nc = bass.Bass("TRN2", target_bir_lowering=False)
    io = {}
    io["x3"] = nc.dram_tensor("x3", [D, T], F32, kind="ExternalInput").ap()
    io["up"] = nc.dram_tensor("up", [16, 256, 1024], F32, kind="ExternalInput").ap()
    io["down"] = nc.dram_tensor("down", [16, 1024, 256], F32, kind="ExternalInput").ap()
    io["x4"] = nc.dram_tensor("x4", [D, T], F32, kind="ExternalOutput").ap()
    es = ExitStack()
    ctx = Ctx(nc, es, "F_")
    emit_f(nc, ctx, es, io)
    return nc





def emit_a1f(nc, ctx, es, io):
    xin = io["xin_f"]
    w = io["w_qkv0"]
    bias = io["b_qkv0"]
    q_fm, k_fm, v_tok = io["q_fm"], io["k_fm"], io["v_tok"]
    wbf = ctx.dram("wbf_qkv0", [D, 3 * D], BF16)
    bbf = ctx.dram("bbf_qkv0", [3 * D], BF16)

    x_sb = es.enter_context(ctx.sbuf("x_sb", [128, NIB, T], BF16))
    ones = es.enter_context(ctx.sbuf("ones", [1, 512], BF16))
    wt = [es.enter_context(ctx.sbuf(f"wt{i}", [128, NIB + 1, 128], BF16)) for i in range(2)]
    po = [es.enter_context(ctx.psum(f"po{i}", [128, 512], F32)) for i in range(2)]
    ob_out = [es.enter_context(ctx.sbuf(f"qk_out{i}", [128, T], BF16)) for i in range(2)]
    wv = es.enter_context(ctx.sbuf("wv", [128, NIB + 1, 512], BF16))
    pv = [es.enter_context(ctx.psum(f"pv{i}", [128, 512], F32)) for i in range(2)]
    v_out = es.enter_context(ctx.sbuf("v_out", [128, 16, 512], BF16))

    with nc.Block(no_gpsimd_drain=True) as blk:
        @blk.vector
        def _(v):
            ctx.inc(v.memset(ones[:], 1.0), "xld0", 1)

        @blk.gpsimd
        def _(g):
            ctx.inc(g.dma_start(out=bbf[:], in_=bias), "wcb", 16)
            ctx.inc(g.dma_start(out=wbf[:, 0:1536], in_=w[:, 0:1536]), "wc0", 16)
            for tcx in range(NTC):
                ctx.inc(g.dma_start(out=x_sb[:, :, tcx * 512:(tcx + 1) * 512],
                                    in_=xin.rearrange("(ib p) t -> p ib t", p=128)[:, :, tcx * 512:(tcx + 1) * 512]),
                        "xc%d" % tcx, 16)
            for j in range(1, 8):
                cs = slice(j * 1536, (j + 1) * 1536)
                ctx.inc(g.dma_start(out=wbf[:, cs], in_=w[:, cs]), "wc%d" % j, 16)

    oi = 0
    gi = 0
    for hf in range(2):
        obs = list(range(64)) if hf == 0 else list(range(32, 64))
        if hf == 1:
            pe_h0 = ctx.val("pe")
            pev_h0 = ctx.val("pev")
            ev_h0 = ctx.val("ev")
            evv_h0 = ctx.val("evv")
            with nc.Block(no_gpsimd_drain=True) as blk:
                @blk.gpsimd
                def _(g):
                    g.wait_ge(ctx.sem("pe"), pe_h0)
                    g.wait_ge(ctx.sem("pev"), pev_h0)
                    ctx.inc(g.dma_start(out=x_sb[:],
                                        in_=xin.rearrange("(ib p) t -> p ib t", p=128)[:, :, T:2 * T]),
                            "xldb", 16)

        oi_base = oi
        with nc.Block(no_gpsimd_drain=True) as blk:
            @blk.sync
            def _(sp):
                for j, ob in enumerate(obs):
                    o = oi_base + j
                    if o >= 2:
                        sp.wait_ge(ctx.sem("pe"), (o - 1) * NTC)
                    sp.wait_ge(ctx.sem("wc%d" % ((ob * 128 + 127) // 1536)), 16)
                    sp.wait_ge(ctx.sem("wcb"), 16)
                    sem = "wld%d" % (o % 2)
                    ctx.inc(sp.dma_start(out=wt[o % 2][:, 0:NIB, :],
                                         in_=wbf.rearrange("(ib p) o -> p ib o", p=128)[:, :, ob * 128:(ob + 1) * 128]),
                            sem, 16)
                    ctx.inc(sp.dma_start(out=wt[o % 2][0:1, NIB, :],
                                         in_=bbf[None, ob * 128:(ob + 1) * 128]),
                            sem, 16)

            @blk.tensor
            def _(t):
                if hf == 0:
                    t.wait_ge(ctx.sem("xld0"), 1)
                else:
                    t.wait_ge(ctx.sem("xldb"), 16)
                for j, ob in enumerate(obs):
                    o = oi_base + j
                    t.wait_ge(ctx.sem("wld%d" % (o % 2)), 32 * (o // 2 + 1))
                    for tc in range(NTC):
                        if hf == 0:
                            t.wait_ge(ctx.sem("xc%d" % tc), 16)
                        grp = o * NTC + tc
                        bank = grp % 2
                        if grp >= 2:
                            t.wait_ge(ctx.sem("ev"), grp - 1)
                        for ib in range(NIB):
                            t.matmul(po[bank][:], wt[o % 2][:, ib, :],
                                     x_sb[:, ib, tc * 512:(tc + 1) * 512],
                                     start=(ib == 0), stop=False)
                        ctx.inc(t.matmul(po[bank][:], wt[o % 2][0:1, NIB, :],
                                         ones[0:1, :], start=False, stop=True), "pe", 1)

            @blk.scalar
            def _(s):
                for j, ob in enumerate(obs):
                    o = oi_base + j
                    if o >= 2:
                        s.wait_ge(ctx.sem("st%d" % (o % 2)), 16 * (o // 2))
                    for tc in range(NTC):
                        grp = o * NTC + tc
                        bank = grp % 2
                        s.wait_ge(ctx.sem("pe"), grp + 1)
                        ctx.inc(s.copy(out=ob_out[o % 2][:, tc * 512:(tc + 1) * 512],
                                       in_=po[bank][:]), "ev", 1)

            @blk.gpsimd
            def _(g):
                for j, ob in enumerate(obs):
                    o = oi_base + j
                    g.wait_ge(ctx.sem("ev"), (o + 1) * NTC)
                    if ob < 32:
                        dst = q_fm[ob * 128:(ob + 1) * 128, :]
                    else:
                        dst = k_fm[(ob - 32) * 128:(ob - 31) * 128, hf * T:(hf + 1) * T]
                    ctx.inc(g.dma_start(out=dst, in_=ob_out[o % 2][:]),
                            "st%d" % (o % 2), 16)
        oi += len(obs)

        gi_base = gi
        with nc.Block(no_gpsimd_drain=True) as blk:
            @blk.sync
            def _(sp):
                for oc in range(8):
                    g2 = gi_base + oc
                    if g2 >= 1:
                        sp.wait_ge(ctx.sem("pev"), g2 * 16)
                    sp.wait_ge(ctx.sem("wc%d" % ((2 * D + oc * 512 + 511) // 1536)), 16)
                    sp.wait_ge(ctx.sem("wcb"), 16)
                    ctx.inc(sp.dma_start(out=wv[:, 0:NIB, :],
                                         in_=wbf.rearrange("(ib p) o -> p ib o", p=128)[:, :, 2 * D + oc * 512: 2 * D + (oc + 1) * 512]),
                            "wvld", 16)
                    ctx.inc(sp.dma_start(out=wv[0:1, NIB, :],
                                         in_=bbf[None, 2 * D + oc * 512: 2 * D + (oc + 1) * 512]),
                            "wvld", 16)

            @blk.tensor
            def _(t):
                for oc in range(8):
                    g2 = gi_base + oc
                    t.wait_ge(ctx.sem("wvld"), 32 * (g2 + 1))
                    for tb in range(16):
                        grp = g2 * 16 + tb
                        bank = grp % 2
                        if grp >= 2:
                            t.wait_ge(ctx.sem("evv"), grp - 1)
                        for ib in range(NIB):
                            t.matmul(pv[bank][:], x_sb[:, ib, tb * 128:(tb + 1) * 128],
                                     wv[:, ib, :], start=(ib == 0), stop=False)
                        ctx.inc(t.matmul(pv[bank][:], ones[0:1, 0:128],
                                         wv[0:1, NIB, :], start=False, stop=True), "pev", 1)

            @blk.scalar
            def _(s):
                for oc in range(8):
                    g2 = gi_base + oc
                    if g2 >= 1:
                        s.wait_ge(ctx.sem("stv"), 16 * g2)
                    for tb in range(16):
                        grp = g2 * 16 + tb
                        bank = grp % 2
                        s.wait_ge(ctx.sem("pev"), grp + 1)
                        ctx.inc(s.copy(out=v_out[:, tb, :], in_=pv[bank][:]), "evv", 1)

            @blk.gpsimd
            def _(g):
                for oc in range(8):
                    g2 = gi_base + oc
                    g.wait_ge(ctx.sem("evv"), (g2 + 1) * 16)
                    ctx.inc(g.dma_start(
                        out=v_tok.rearrange("(tb p) o -> p tb o", p=128)[:, 16 * hf:16 * hf + 16, oc * 512:(oc + 1) * 512],
                        in_=v_out[:]), "stv", 16)
        gi += 8

    with nc.Block(no_gpsimd_drain=True) as blk:
        @blk.gpsimd
        def _(g):
            g.wait_ge(ctx.sem("st0"), 16 * 48)
            g.wait_ge(ctx.sem("st1"), 16 * 48)
            g.wait_ge(ctx.sem("stv"), 16 * 16)

    return nc



def _phase_reset(nc, snap):
    nc.all_engine_barrier()
    nc.clear_and_free_semaphores(nc.allocated_since(snap))
    nc.all_engine_barrier()
    nc.restore_sems(snap)


def build_fused():
    nc = bass.Bass("TRN2", target_bir_lowering=False, num_devices=NCORES)
    io = {}
    EI = lambda n, s, d=F32: nc.dram_tensor(n, s, d, kind="ExternalInput").ap()
    io["xin_f"] = EI("xin_f", [D, 2 * T])
    io["w_qkv0"] = EI("w_qkv0", [D, 3 * D]); io["b_qkv0"] = EI("b_qkv0", [3 * D])
    io["w_o0"] = EI("w_o0", [D, D]); io["b_o0"] = EI("b_o0", [D])
    io["w_base"] = EI("w_base", [D, D]); io["b_base"] = EI("b_base", [D])
    io["w_A"] = EI("w_A", [D, 64]); io["w_B"] = EI("w_B", [8, 128, D])
    io["w_router"] = EI("w_router", [D, 16])
    io["w_qkv1"] = EI("w_qkv1", [D, 3 * D])
    io["w_gate"] = EI("w_gate", [D, D]); io["w_out1"] = EI("w_out1", [D, D])
    io["norm_w"] = EI("norm_w", [D])
    io["up"] = EI("up", [16, 256, 1024]); io["down"] = EI("down", [16, 1024, 256])
    io["sel"] = EI("sel", [128, 8])
    io["kv_init"] = EI("kv_init", [32, 128, 128])
    io["x4"] = nc.dram_tensor("x4", [D, T], F32, kind="ExternalOutput").ap()

    INT = lambda n, s, d: nc.dram_tensor(n, s, d, kind="Internal").ap()
    io["q_fm"] = INT("q_fm", [D, T], BF16)
    io["k_fm"] = INT("k_fm", [D, 2 * T], BF16)
    io["v_tok"] = INT("v_tok", [2 * T, D], BF16)
    io["x1"] = INT("x1", [D, T], F32)
    io["x2"] = INT("x2", [D, T], F32)
    io["q1_fm"] = INT("q1_fm", [D, T], BF16)
    io["k1_fm"] = INT("k1_fm", [D, T], BF16)
    io["v1_tok"] = INT("v1_tok", [T, D], BF16)
    io["attn_d"] = INT("attn_d", [D, T], BF16)
    io["kv8_out"] = INT("kv8_out", [32, 128, 128], F32)
    io["kv8_ag"] = INT("kv8_ag", [8 * 32 * 128, 128], F32)
    io["xin"] = io["xin_f"][:, 0:T]
    io["kv8c"] = io["kv8_ag"][0:32 * 128, :].rearrange("(h p) e -> h p e", p=128)
    io["x3"] = INT("x3", [D, T], F32)
    io["wbf_qkv1"] = INT("wbf_qkv1", [D, 3 * D], BF16)
    io["wgbf"] = INT("wgbf", [D, D], BF16)
    io["wobf"] = INT("wobf", [D, D], BF16)

    ges = ExitStack()
    gctx = Ctx(nc, ges, "G_")
    io["gctx"] = gctx
    for j in range(8):
        gctx.sem("Cwc%d" % j)
    gctx.sem("Ewcg")
    gctx.sem("Ewco")

    def prefetch_c():
        with nc.Block(no_gpsimd_drain=True) as blk:
            @blk.gpsimd
            def _(g):
                for j in range(8):
                    cs = slice(j * 1536, (j + 1) * 1536)
                    gctx.inc(g.dma_start(out=io["wbf_qkv1"][:, cs], in_=io["w_qkv1"][:, cs]),
                             "Cwc%d" % j, 16)

    def prefetch_e():
        with nc.Block(no_gpsimd_drain=True) as blk:
            @blk.gpsimd
            def _(g):
                for j in range(4):
                    cs = slice(j * 1024, (j + 1) * 1024)
                    gctx.inc(g.dma_start(out=io["wgbf"][:, cs], in_=io["w_gate"][:, cs]), "Ewcg", 16)
                    gctx.inc(g.dma_start(out=io["wobf"][:, cs], in_=io["w_out1"][:, cs]), "Ewco", 16)

    def soft_barrier(prev):
        waits = prev.finals()
        with nc.Block(no_gpsimd_drain=True) as blk:
            for eng in ("sync", "tensor", "vector", "scalar", "gpsimd"):
                def body(e, waits=waits):
                    for h, v in waits:
                        e.wait_ge(h, v)
                getattr(blk, eng)(body)

    groups = [
        [("P1_", emit_a1f), ("P2_", emit_a2)],
        [("P3_", emit_b), ("P4_", emit_c), ("P5_", emit_d), ("CC_", None)],
        [("P6_", emit_e), ("P7_", emit_f)],
    ]
    for gi, group in enumerate(groups):
        snap = nc.snapshot_sems()
        prev = None
        for pfx, emit in group:
            if pfx == "P2_":
                prefetch_c()
            if pfx == "P3_":
                prefetch_e()
            es = ExitStack()
            ctx = Ctx(nc, es, pfx)
            if prev is not None:
                soft_barrier(prev)
            if emit is None:
                with nc.Block(no_gpsimd_drain=True) as blk:
                    @blk.gpsimd
                    def _(g):
                        ins = g.collective_compute(
                            "AllGather", ALU.bypass,
                            replica_groups=[[0, 1, 2, 3, 4, 5, 6, 7]],
                            ins=[io["kv8_out"].rearrange("h p e -> (h p) e")],
                            outs=[io["kv8_ag"]])
                        ctx.inc(ins, "cc", 1)
                        g.wait_ge(ctx.sem("cc"), 1)
            else:
                emit(nc, ctx, es, io)
            es.close()
            prev = ctx
        _phase_reset(nc, snap)
    return nc




def _permuted_inputs(inp):
    P = np.asarray(inp["L1_ffn_perm"]).astype(np.int64)
    out = {}
    x = np.asarray(inp["L0_attn_x"])
    out["x"] = x[:, :, P]
    out["w_qkv0"] = np.ascontiguousarray(np.asarray(inp["L0_attn_attn_qkvw"])[P, :])
    out["b_qkv0"] = np.asarray(inp["L0_attn_attn_qkvb"])
    out["w_o0"] = np.ascontiguousarray(np.asarray(inp["L0_attn_attn_ow"])[:, P])
    out["b_o0"] = np.asarray(inp["L0_attn_attn_ob"])[P]
    bw = np.asarray(inp["L0_ffn_base_w"])
    out["w_base"] = np.ascontiguousarray(bw[P][:, P].T)
    out["b_base"] = np.asarray(inp["L0_ffn_base_b"])[P]
    out["w_A"] = np.ascontiguousarray(np.asarray(inp["L0_ffn_A"])[:, P].T)
    B = np.asarray(inp["L0_ffn_B"])
    Bp = B[:, P, :]
    BT = np.empty((8, 128, D), dtype=B.dtype)
    for kb in range(8):
        BT[kb, 0:64, :] = Bp[2 * kb].T
        BT[kb, 64:128, :] = Bp[2 * kb + 1].T
    out["w_B"] = np.ascontiguousarray(BT)
    out["w_router"] = np.ascontiguousarray(np.asarray(inp["L0_ffn_router_w"])[:, P].T)
    out["w_qkv1"] = np.ascontiguousarray(np.asarray(inp["L1_attn_qkv_w"])[:, P].T)
    out["w_gate"] = np.ascontiguousarray(np.asarray(inp["L1_attn_gate_w"])[:, P].T)
    out["w_out1"] = np.ascontiguousarray(np.asarray(inp["L1_attn_out_w"])[P, :].T)
    out["norm_w"] = np.asarray(inp["L1_attn_norm_w"])
    out["up"] = np.asarray(inp["L1_ffn_up"])
    out["down"] = np.asarray(inp["L1_ffn_down"])
    out["P"] = P
    return out


def _core_x_fm(xp, c):
    b, h = c // 2, c % 2
    return np.ascontiguousarray(xp[b, h * T:(h + 1) * T, :].T.astype(np.float32))


def _weight_feed(pre):
    return {
        "w_qkv0": pre["w_qkv0"], "b_qkv0": pre["b_qkv0"],
        "w_o0": pre["w_o0"], "b_o0": pre["b_o0"],
        "w_base": pre["w_base"], "b_base": pre["b_base"],
        "w_A": pre["w_A"], "w_B": pre["w_B"], "w_router": pre["w_router"],
        "w_qkv1": pre["w_qkv1"], "w_gate": pre["w_gate"], "w_out1": pre["w_out1"],
        "norm_w": pre["norm_w"], "up": pre["up"], "down": pre["down"],
    }


def kernel(**inputs):
    pre = _permuted_inputs(inputs)
    P = pre["P"]
    inv = np.argsort(P)
    xs = [_core_x_fm(pre["x"], c) for c in range(NCORES)]
    zeros_kv = np.zeros((32, 128, 128), np.float32)
    wfeed = _weight_feed(pre)

    nc = build_fused()
    in_maps = []
    for c in range(NCORES):
        own, partner = c, c ^ 1
        xin_f = np.ascontiguousarray(np.concatenate([xs[own], xs[partner]], axis=1))
        sel = np.zeros((128, 8), np.float32)
        if c % 2 == 1:
            sel[:, c - 1] = 1.0
        in_maps.append(dict(wfeed, xin_f=xin_f, sel=sel, kv_init=zeros_kv))
    res = run_bass_kernel_spmd(nc, in_maps, core_ids=list(range(NCORES))).results

    out = np.empty((4, 4096, 4096), np.float32)
    for c in range(NCORES):
        b, half = c // 2, c % 2
        out[b, half * T:(half + 1) * T, :] = np.asarray(res[c]["x4"]).T[:, inv]
    return out


def kernel_unfused(**inputs):
    pre = _permuted_inputs(inputs)
    P = pre["P"]
    inv = np.argsort(P)
    xs = [_core_x_fm(pre["x"], c) for c in range(NCORES)]
    zeros_kv = np.zeros((32, 128, 128), np.float32)

    nc = build_a1()
    r1 = run_bass_kernel_spmd(nc, [
        {"xin": xs[c], "w_qkv0": pre["w_qkv0"], "b_qkv0": pre["b_qkv0"]}
        for c in range(NCORES)], core_ids=list(range(NCORES))).results

    nc = build_a2()
    in2 = []
    for c in range(NCORES):
        b = c // 2
        kf = np.ascontiguousarray(np.concatenate(
            [np.asarray(r1[2 * b]["k_fm"]), np.asarray(r1[2 * b + 1]["k_fm"])], axis=1))
        vf = np.ascontiguousarray(np.concatenate(
            [np.asarray(r1[2 * b]["v_tok"]), np.asarray(r1[2 * b + 1]["v_tok"])], axis=0))
        in2.append({"q_fm": np.asarray(r1[c]["q_fm"]), "k_fm": kf, "v_tok": vf,
                    "xin": xs[c], "w_o0": pre["w_o0"], "b_o0": pre["b_o0"]})
    r2 = run_bass_kernel_spmd(nc, in2, core_ids=list(range(NCORES))).results

    nc = build_b()
    r3 = run_bass_kernel_spmd(nc, [
        {"x1": np.asarray(r2[c]["x1"]), "w_base": pre["w_base"], "b_base": pre["b_base"],
         "w_A": pre["w_A"], "w_B": pre["w_B"], "w_router": pre["w_router"]}
        for c in range(NCORES)], core_ids=list(range(NCORES))).results

    nc = build_c()
    r4 = run_bass_kernel_spmd(nc, [
        {"x2": np.asarray(r3[c]["x2"]), "w_qkv1": pre["w_qkv1"]}
        for c in range(NCORES)], core_ids=list(range(NCORES))).results

    nc = build_d()
    r5 = run_bass_kernel_spmd(nc, [
        {"q1_fm": np.asarray(r4[c]["q1_fm"]), "k1_fm": np.asarray(r4[c]["k1_fm"]),
         "v1_tok": np.asarray(r4[c]["v1_tok"]), "kv_init": zeros_kv}
        for c in range(NCORES)], core_ids=list(range(NCORES))).results

    nc = build_e()
    in6 = []
    for c in range(NCORES):
        kv8c = np.asarray(r5[c - 1]["kv8_out"]) if c % 2 == 1 else zeros_kv
        in6.append({"attn_d": np.asarray(r5[c]["attn_d"]), "kv8c": kv8c,
                    "q1_fm": np.asarray(r4[c]["q1_fm"]), "x2": np.asarray(r3[c]["x2"]),
                    "w_gate": pre["w_gate"], "w_out1": pre["w_out1"],
                    "norm_w": pre["norm_w"]})
    r6 = run_bass_kernel_spmd(nc, in6, core_ids=list(range(NCORES))).results

    nc = build_f()
    r7 = run_bass_kernel_spmd(nc, [
        {"x3": np.asarray(r6[c]["x3"]), "up": pre["up"], "down": pre["down"]}
        for c in range(NCORES)], core_ids=list(range(NCORES))).results

    out = np.empty((4, 4096, 4096), np.float32)
    for c in range(NCORES):
        b, half = c // 2, c % 2
        out[b, half * T:(half + 1) * T, :] = np.asarray(r7[c]["x4"]).T[:, inv]
    return out


if __name__ == "__main__":
    pass


# version 10
# speedup vs baseline: 1.1331x; 1.1100x over previous
import numpy as np
import ml_dtypes
from contextlib import ExitStack

import concourse.bass as bass
import concourse.mybir as mybir
from concourse.bass_utils import run_bass_kernel_spmd

F32 = mybir.dt.float32
BF16 = mybir.dt.bfloat16
AF = mybir.ActivationFunctionType
ALU = mybir.AluOpType
AX = mybir.AxisListType
BF16NP = ml_dtypes.bfloat16

D = 4096
T = 2048
NIB = D // 128
NTC = T // 512
NCORES = 8



class Ctx:

    def __init__(self, nc, es, pfx=""):
        self.nc = nc
        self.es = es
        self.pfx = pfx
        self.h = {}
        self.n = {}

    def sem(self, name):
        if name not in self.h:
            self.h[name] = self.nc.alloc_semaphore(self.pfx + name)
            self.n[name] = 0
        return self.h[name]

    def sbuf(self, name, shape, dt):
        return self.nc.sbuf_tensor(self.pfx + name, shape, dt)

    def psum(self, name, shape, dt):
        return self.nc.psum_tensor(self.pfx + name, shape, dt)

    def inline(self, data, name):
        return self.nc.inline_tensor(data, name=self.pfx + name).ap()

    def dram(self, name, shape, dt):
        return self.nc.dram_tensor(self.pfx + name, shape, dt, kind="Internal").ap()

    def finals(self):
        return [(self.h[n], self.n[n]) for n in self.h if self.n[n] > 0]

    def inc(self, ins, name, k=1):
        ins.then_inc(self.sem(name), k)
        self.n[name] += k
        return self.n[name]

    def val(self, name):
        self.sem(name)
        return self.n[name]


def cast_load(g, ctx, sem, dst, src):
    return ctx.inc(g.dma_start(out=dst, in_=src), sem, 16)



def emit_a1(nc, ctx, es, io):
    xin = io["xin"]
    w = io["w_qkv0"]
    bias = io["b_qkv0"]
    q_fm = io["q_fm"]
    k_fm = io["k_fm"]
    v_tok = io["v_tok"]
    wbf = ctx.dram("wbf_qkv0", [D, 3 * D], BF16)
    bbf = ctx.dram("bbf_qkv0", [3 * D], BF16)

    x_sb = es.enter_context(ctx.sbuf("x_sb", [128, NIB, T], BF16))
    ones = es.enter_context(ctx.sbuf("ones", [1, 512], BF16))
    with nc.Block(no_gpsimd_drain=True) as blk:
        @blk.gpsimd
        def _(g):
            cast_load(g, ctx, "xld", x_sb[:], xin.rearrange("(ib p) t -> p ib t", p=128))
            ctx.inc(g.dma_start(out=bbf[:], in_=bias), "wcb", 16)
            for j in range(8):
                cs = slice(j * 1536, (j + 1) * 1536)
                ctx.inc(g.dma_start(out=wbf[:, cs], in_=w[:, cs]), "wc%d" % j, 16)

        @blk.vector
        def _(v):
            ctx.inc(v.memset(ones[:], 1.0), "xld0", 1)

    sub = es
    wt = [sub.enter_context(ctx.sbuf(f"wt{i}", [128, NIB + 1, 128], BF16)) for i in range(2)]
    po = [sub.enter_context(ctx.psum(f"po{i}", [128, 512], F32)) for i in range(2)]
    ob_out = [sub.enter_context(ctx.sbuf(f"qk_out{i}", [128, T], BF16)) for i in range(2)]

    NOB_QK = 64
    with nc.Block(no_gpsimd_drain=True) as blk:
        @blk.sync
        def _(sp):
            for ob in range(NOB_QK):
                if ob >= 2:
                    sp.wait_ge(ctx.sem("pe"), (ob - 1) * NTC)
                sp.wait_ge(ctx.sem("wc%d" % ((ob * 128 + 127) // 1536)), 16)
                sp.wait_ge(ctx.sem("wcb"), 16)
                sem = "wld%d" % (ob % 2)
                ctx.inc(sp.dma_start(out=wt[ob % 2][:, 0:NIB, :],
                                     in_=wbf.rearrange("(ib p) o -> p ib o", p=128)[:, :, ob * 128:(ob + 1) * 128]),
                        sem, 16)
                ctx.inc(sp.dma_start(out=wt[ob % 2][0:1, NIB, :],
                                     in_=bbf[None, ob * 128:(ob + 1) * 128]),
                        sem, 16)

        @blk.tensor
        def _(t):
            t.wait_ge(ctx.sem("xld"), 16)
            t.wait_ge(ctx.sem("xld0"), 1)
            for ob in range(NOB_QK):
                t.wait_ge(ctx.sem("wld%d" % (ob % 2)), 32 * (ob // 2 + 1))
                for tc in range(NTC):
                    bank = (ob * NTC + tc) % 2
                    grp = ob * NTC + tc
                    if grp >= 2:
                        t.wait_ge(ctx.sem("ev"), grp - 1)
                    for ib in range(NIB):
                        t.matmul(po[bank][:], wt[ob % 2][:, ib, :],
                                 x_sb[:, ib, tc * 512:(tc + 1) * 512],
                                 start=(ib == 0), stop=False)
                    ins = t.matmul(po[bank][:], wt[ob % 2][0:1, NIB, :],
                                   ones[0:1, :], start=False, stop=True)
                    ctx.inc(ins, "pe", 1)

        @blk.scalar
        def _(s):
            for ob in range(NOB_QK):
                if ob >= 2:
                    s.wait_ge(ctx.sem("st%d" % (ob % 2)), 16 * (ob // 2))
                for tc in range(NTC):
                    grp = ob * NTC + tc
                    bank = grp % 2
                    s.wait_ge(ctx.sem("pe"), grp + 1)
                    ctx.inc(s.copy(out=ob_out[ob % 2][:, tc * 512:(tc + 1) * 512],
                                   in_=po[bank][:]), "ev", 1)

        @blk.gpsimd
        def _(g):
            for ob in range(NOB_QK):
                g.wait_ge(ctx.sem("ev"), (ob + 1) * NTC)
                dst = q_fm if ob < 32 else k_fm
                row = (ob % 32) * 128
                ctx.inc(g.dma_start(out=dst[row:row + 128, :], in_=ob_out[ob % 2][:]),
                        "st%d" % (ob % 2), 16)


    sub = es
    wv = sub.enter_context(ctx.sbuf("wv", [128, NIB + 1, 512], BF16))
    pv = [sub.enter_context(ctx.psum(f"pv{i}", [128, 512], F32)) for i in range(2)]
    v_out = sub.enter_context(ctx.sbuf("v_out", [128, 16, 512], BF16))

    NOC = 8
    with nc.Block(no_gpsimd_drain=True) as blk:
        @blk.sync
        def _(sp):
            for oc in range(NOC):
                if oc >= 1:
                    sp.wait_ge(ctx.sem("pev"), oc * 16)
                sp.wait_ge(ctx.sem("wc%d" % ((2 * D + oc * 512 + 511) // 1536)), 16)
                sp.wait_ge(ctx.sem("wcb"), 16)
                ctx.inc(sp.dma_start(out=wv[:, 0:NIB, :],
                                     in_=wbf.rearrange("(ib p) o -> p ib o", p=128)[:, :, 2 * D + oc * 512: 2 * D + (oc + 1) * 512]),
                        "wvld", 16)
                ctx.inc(sp.dma_start(out=wv[0:1, NIB, :],
                                     in_=bbf[None, 2 * D + oc * 512: 2 * D + (oc + 1) * 512]),
                        "wvld", 16)

        @blk.tensor
        def _(t):
            for oc in range(NOC):
                t.wait_ge(ctx.sem("wvld"), 32 * (oc + 1))
                for tb in range(16):
                    grp = oc * 16 + tb
                    bank = grp % 2
                    if grp >= 2:
                        t.wait_ge(ctx.sem("evv"), grp - 1)
                    for ib in range(NIB):
                        t.matmul(pv[bank][:], x_sb[:, ib, tb * 128:(tb + 1) * 128],
                                 wv[:, ib, :], start=(ib == 0), stop=False)
                    ins = t.matmul(pv[bank][:], ones[0:1, 0:128],
                                   wv[0:1, NIB, :], start=False, stop=True)
                    ctx.inc(ins, "pev", 1)

        @blk.scalar
        def _(s):
            for oc in range(NOC):
                if oc >= 1:
                    s.wait_ge(ctx.sem("stv"), 16 * oc)
                for tb in range(16):
                    grp = oc * 16 + tb
                    bank = grp % 2
                    s.wait_ge(ctx.sem("pev"), grp + 1)
                    ctx.inc(s.copy(out=v_out[:, tb, :], in_=pv[bank][:]), "evv", 1)

        @blk.gpsimd
        def _(g):
            for oc in range(NOC):
                g.wait_ge(ctx.sem("evv"), (oc + 1) * 16)
                ctx.inc(g.dma_start(
                    out=v_tok.rearrange("(tb p) o -> p tb o", p=128)[:, :, oc * 512:(oc + 1) * 512],
                    in_=v_out[:]), "stv", 16)
            g.wait_ge(ctx.sem("st0"), 16 * (NOB_QK // 2))
            g.wait_ge(ctx.sem("st1"), 16 * (NOB_QK // 2))
            g.wait_ge(ctx.sem("stv"), 16 * NOC)



def build_a1():
    nc = bass.Bass("TRN2", target_bir_lowering=False)
    io = {}
    io["xin"] = nc.dram_tensor("xin", [D, T], F32, kind="ExternalInput").ap()
    io["w_qkv0"] = nc.dram_tensor("w_qkv0", [D, 3 * D], F32, kind="ExternalInput").ap()
    io["b_qkv0"] = nc.dram_tensor("b_qkv0", [3 * D], F32, kind="ExternalInput").ap()
    io["q_fm"] = nc.dram_tensor("q_fm", [D, T], BF16, kind="ExternalOutput").ap()
    io["k_fm"] = nc.dram_tensor("k_fm", [D, T], BF16, kind="ExternalOutput").ap()
    io["v_tok"] = nc.dram_tensor("v_tok", [T, D], BF16, kind="ExternalOutput").ap()
    es = ExitStack()
    ctx = Ctx(nc, es, "1_")
    emit_a1(nc, ctx, es, io)
    return nc




SQ128 = float(1.0 / np.sqrt(128.0))


def emit_a2(nc, ctx, es, io):
    q_fm = io["q_fm"]
    k_fm = io["k_fm"]
    v_tok = io["v_tok"]
    xin = io["xin"]
    w_o = io["w_o0"]
    b_o = io["b_o0"]
    x1 = io["x1"]

    wbf = ctx.dram("wbf_o0", [D, D], BF16)
    bbf = ctx.dram("bbf_o0", [D], BF16)

    NH = 32
    NKB = 32
    NQC = NTC
    NNC = 12
    NEXP = NQC * NKB

    ctx_sb = es.enter_context(ctx.sbuf("ctx_sb", [128, NH, T], BF16))
    ones = es.enter_context(ctx.sbuf("ones", [1, 512], BF16))
    onescol = es.enter_context(ctx.sbuf("onescol", [128, 1], BF16))
    ones32 = es.enter_context(ctx.sbuf("ones32", [1, 128], F32))
    u_mx = es.enter_context(ctx.sbuf("u_mx", [1, 4], F32))
    st1 = ExitStack()
    qh = [st1.enter_context(ctx.sbuf(f"qh{i}", [128, T], BF16)) for i in range(2)]
    kh = [st1.enter_context(ctx.sbuf(f"kh{i}", [128, 2 * T], BF16)) for i in range(2)]
    vh = [st1.enter_context(ctx.sbuf(f"vh{i}", [128, NKB, 128], BF16)) for i in range(2)]
    e_sb = [st1.enter_context(ctx.sbuf(f"e_sb{i}", [128, 1024], BF16)) for i in range(3)]
    sq = [st1.enter_context(ctx.sbuf(f"sq{i}", [128, 512], BF16)) for i in range(2)]
    bias_u = [st1.enter_context(ctx.sbuf(f"bias_u{i}", [128, 1], F32)) for i in range(2)]
    zrow = st1.enter_context(ctx.sbuf("zrow", [1, 512], F32))
    bz_sb = st1.enter_context(ctx.sbuf("bz_sb", [128, 512], F32))

    psc = [es.enter_context(ctx.psum(f"psc{i}", [128, 2, 512], F32)) for i in range(2)]
    pctx = [es.enter_context(ctx.psum(f"pctx{i}", [128, 512], F32)) for i in range(2)]
    pz = [es.enter_context(ctx.psum("pz0", [1, 512], F32))]
    pz.append(pz[0])
    pb = es.enter_context(ctx.psum("pb", [128, 512], F32))
    pbu = pb[:, 0:1]

    NPR = NKB // 2
    NEXP2 = NQC * NPR

    with nc.Block(no_gpsimd_drain=True) as blk:
        @blk.vector
        def _(v):
            v.memset(ones[:], 1.0)
            v.memset(ones32[:], 1.0)
            ctx.inc(v.memset(onescol[:], 1.0), "init", 1)

    with nc.Block(no_gpsimd_drain=True) as blk:
        @blk.gpsimd
        def _(g):
            ctx.inc(g.dma_start(out=bbf[:], in_=b_o), "wcast", 16)
            for j in range(4):
                cs = slice(j * 1024, (j + 1) * 1024)
                ctx.inc(g.dma_start(out=wbf[:, cs], in_=w_o[:, cs]), "wcast", 16)

        @blk.sync
        def _(sp):
            for h in range(NH):
                if h >= 2:
                    sp.wait_ge(ctx.sem("peb"), NQC * (h - 1))
                sem = "hld%d" % (h % 2)
                r = slice(h * 128, (h + 1) * 128)
                ctx.inc(sp.dma_start(out=qh[h % 2][:], in_=q_fm[r, :]), sem, 16)
                ctx.inc(sp.dma_start(out=kh[h % 2][:], in_=k_fm[r, :]), sem, 16)
                ctx.inc(sp.dma_start(
                    out=vh[h % 2][:],
                    in_=v_tok.rearrange("(tb p) o -> p tb o", p=128)[:, :, r]), sem, 16)

        @blk.scalar
        def _(s):
            for h in range(NH):
                s.wait_ge(ctx.sem("hld%d" % (h % 2)), 48 * (h // 2 + 1))
                for j in range(NNC):
                    src, c = (0, j) if j < NQC else (1, j - NQC)
                    buf = qh[h % 2] if src == 0 else kh[h % 2]
                    if h * NNC + j >= 2:
                        s.wait_ge(ctx.sem("pen"), h * NNC + j - 1)
                    ctx.inc(s.activation(out=sq[j % 2][:], in_=buf[:, c * 512:(c + 1) * 512],
                                         func=AF.Square), "actn", 1)
                s.wait_ge(ctx.sem("dven"), h + 1)
                for qc in range(NQC):
                    for pr in range(NPR):
                        i = (h * NQC + qc) * NPR + pr
                        s.wait_ge(ctx.sem("pes"), i + 1)
                        if i >= 3:
                            s.wait_ge(ctx.sem("pez"), i - 2)
                        ctx.inc(s.activation(out=e_sb[i % 3][:], in_=psc[i % 2][:],
                                             func=AF.Exp, bias=bias_u[h % 2][:],
                                             scale=SQ128), "acte", 1)

        @blk.tensor
        def _(t):
            t.wait_ge(ctx.sem("init"), 1)
            for h in range(NH):
                for j in range(NNC):
                    t.wait_ge(ctx.sem("actn"), h * NNC + j + 1)
                    if h * NNC + j >= 2:
                        t.wait_ge(ctx.sem("dvemx"), h * NNC + j - 1)
                    if h >= 1 and j < 2:
                        t.wait_ge(ctx.sem("acte"), h * NEXP2)
                    ctx.inc(t.matmul(psc[j % 2][0:1, 0, :], onescol[:], sq[j % 2][:],
                                     start=True, stop=True), "pen", 1)
                t.wait_ge(ctx.sem("dveu"), h + 1)
                if h >= 1:
                    t.wait_ge(ctx.sem("dven"), h)
                    t.wait_ge(ctx.sem("dvec"), h * NQC)
                ctx.inc(t.matmul(pbu, ones32[0:1, :], u_mx[0:1, 2:3],
                                 start=True, stop=True), "pebu", 1)
                for qc in range(NQC):
                    g = h * NQC + qc
                    qsl = slice(qc * 512, (qc + 1) * 512)
                    for pr in range(NPR + 1):
                        if pr < NPR:
                            i = g * NPR + pr
                            if i >= 2:
                                t.wait_ge(ctx.sem("acte"), i - 1)
                            if pr < 2:
                                t.wait_ge(ctx.sem("dvemx"), (h + 1) * NNC)
                            t.matmul(psc[i % 2][:, 0, :], kh[h % 2][:, (2 * pr) * 128:(2 * pr + 1) * 128],
                                     qh[h % 2][:, qsl], start=True, stop=True)
                            ctx.inc(t.matmul(psc[i % 2][:, 1, :], kh[h % 2][:, (2 * pr + 1) * 128:(2 * pr + 2) * 128],
                                             qh[h % 2][:, qsl], start=True, stop=True), "pes", 1)
                        if pr >= 1:
                            pc_ = pr - 1
                            i = g * NPR + pc_
                            t.wait_ge(ctx.sem("acte"), i + 1)
                            if pc_ == 0 and g >= 2:
                                t.wait_ge(ctx.sem("dvec"), g - 1)
                            if pc_ == 0 and g >= 1:
                                t.wait_ge(ctx.sem("dvez"), 2 * g)
                            t.matmul(pctx[g % 2][:], vh[h % 2][:, 2 * pc_, :], e_sb[i % 3][:, 0:512],
                                     start=(pc_ == 0), stop=False)
                            t.matmul(pctx[g % 2][:], vh[h % 2][:, 2 * pc_ + 1, :], e_sb[i % 3][:, 512:1024],
                                     start=False, stop=(pc_ == NPR - 1))
                            t.matmul(pz[0][:], onescol[:], e_sb[i % 3][:, 0:512],
                                     start=(pc_ == 0), stop=False)
                            ctx.inc(t.matmul(pz[0][:], onescol[:], e_sb[i % 3][:, 512:1024],
                                             start=False, stop=(pc_ == NPR - 1)), "pez", 1)
                    t.wait_ge(ctx.sem("dvez"), 2 * g + 2)
                    if g >= 1:
                        t.wait_ge(ctx.sem("dvec"), g)
                    if qc == 0:
                        t.wait_ge(ctx.sem("dven"), h + 1)
                    ctx.inc(t.matmul(pb[:], ones32[0:1, :], zrow[0:1, :],
                                     start=True, stop=True), "peb", 1)

        @blk.vector
        def _(v):
            for h in range(NH):
                v.memset(u_mx[0:1, 0:1], -1e30)
                for j in range(NNC):
                    v.wait_ge(ctx.sem("pen"), h * NNC + j + 1)
                    v.tensor_reduce(u_mx[0:1, 1:2], psc[j % 2][0:1, 0, :], axis=AX.X, op=ALU.max)
                    v.drain()
                    ctx.inc(v.tensor_tensor(out=u_mx[0:1, 0:1], in0=u_mx[0:1, 0:1],
                                            in1=u_mx[0:1, 1:2], op=ALU.max), "dvemx", 1)
                    v.drain()
                if h >= 1:
                    v.wait_ge(ctx.sem("pebu"), h)
                v.drain()
                ctx.inc(v.tensor_scalar(out=u_mx[0:1, 2:3], in0=u_mx[0:1, 0:1],
                                        scalar1=-SQ128, scalar2=None, op0=ALU.mult), "dveu", 1)
                v.wait_ge(ctx.sem("pebu"), h + 1)
                if h >= 2:
                    v.wait_ge(ctx.sem("acte"), (h - 1) * NEXP2)
                ctx.inc(v.tensor_copy(bias_u[h % 2][:], pbu), "dven", 1)
                for qc in range(NQC):
                    g = h * NQC + qc
                    qsl = slice(qc * 512, (qc + 1) * 512)
                    v.wait_ge(ctx.sem("pez"), (g + 1) * NPR)
                    ctx.inc(v.tensor_scalar(out=zrow[:], in0=pz[0][:],
                                            scalar1=1e-30, scalar2=None, op0=ALU.add), "dvez", 1)
                    v.drain()
                    ctx.inc(v.reciprocal(zrow[:], zrow[:]), "dvez", 1)
                    v.wait_ge(ctx.sem("peb"), g + 1)
                    v.tensor_copy(bz_sb[:], pb[:])
                    v.drain()
                    ctx.inc(v.tensor_tensor(out=ctx_sb[:, h, qsl], in0=pctx[g % 2][:],
                                            in1=bz_sb[:], op=ALU.mult), "dvec", 1)

    NG1 = NH * NQC

    st1.close()
    wo = [es.enter_context(ctx.sbuf(f"wo{i}", [128, NIB + 1, 128], BF16)) for i in range(2)]

    def sync_loads(sp, tc, ob, i):
        if i == 0:
            sp.wait_ge(ctx.sem("wcast"), 16 * 5)
            sp.wait_ge(ctx.sem("dvec"), NG1)
        if i >= 2:
            sp.wait_ge(ctx.sem("Ape2"), i - 1)
        sem = "wld2%d" % (i % 2)
        ctx.inc(sp.dma_start(out=wo[i % 2][:, 0:NIB, :],
                             in_=wbf.rearrange("(ib p) o -> p ib o", p=128)[:, :, ob * 128:(ob + 1) * 128]),
                sem, 16)
        ctx.inc(sp.dma_start(out=wo[i % 2][0:1, NIB, :],
                             in_=bbf[None, ob * 128:(ob + 1) * 128]), sem, 16)

    def pe_mms(t, tc, ob, i, psum):
        if i == 0:
            t.wait_ge(ctx.sem("dvec"), NG1)
        t.wait_ge(ctx.sem("wld2%d" % (i % 2)), 32 * (i // 2 + 1))
        for ib in range(NIB):
            t.matmul(psum[:], wo[i % 2][:, ib, :],
                     ctx_sb[:, ib, tc * 512:(tc + 1) * 512],
                     start=(ib == 0), stop=False)
        return t.matmul(psum[:], wo[i % 2][0:1, NIB, :], ones[0:1, :],
                        start=False, stop=True)

    ln_out_phase(nc, ctx, es, "A", xin, x1, pe_mms, sync_loads,
                 [psc[0][:, 0, :], psc[1][:, 0, :]],
                 [psc[0][0:1, 1, :], psc[1][0:1, 1, :]],
                 pb, pctx[0], ones32, onescol)



def build_a2():
    nc = bass.Bass("TRN2", target_bir_lowering=False)
    io = {}
    io["q_fm"] = nc.dram_tensor("q_fm", [D, T], BF16, kind="ExternalInput").ap()
    io["k_fm"] = nc.dram_tensor("k_fm", [D, 2 * T], BF16, kind="ExternalInput").ap()
    io["v_tok"] = nc.dram_tensor("v_tok", [2 * T, D], BF16, kind="ExternalInput").ap()
    io["xin"] = nc.dram_tensor("xin", [D, T], F32, kind="ExternalInput").ap()
    io["w_o0"] = nc.dram_tensor("w_o0", [D, D], F32, kind="ExternalInput").ap()
    io["b_o0"] = nc.dram_tensor("b_o0", [D], F32, kind="ExternalInput").ap()
    io["x1"] = nc.dram_tensor("x1", [D, T], F32, kind="ExternalOutput").ap()
    es = ExitStack()
    ctx = Ctx(nc, es, "2_")
    emit_a2(nc, ctx, es, io)
    return nc




def ln_out_phase(nc, ctx, es, pfx, xres, xout, pe_mms, sync_loads,
                 psc, pz, pb, pbm, ones32, onescol,
                 pre_sync=None, pre_tensor=None, pre_vector=None,
                 pre_scalar=None, pre_gpsimd=None, first_tensor_wait=None,
                 sc_ob=None, nob=NIB, eps=1e-5, dmodel=None):
    S = lambda n: ctx.sem(pfx + n)
    V = lambda n: ctx.val(pfx + n)
    dmodel = dmodel or (128 * nob)

    xt = [es.enter_context(ctx.sbuf(pfx + f"xt{i}", [128, 512], F32)) for i in range(2)]
    ysb = es.enter_context(ctx.sbuf(pfx + "ysb", [128, nob, 512], BF16))
    y2 = [es.enter_context(ctx.sbuf(pfx + f"y2_{i}", [128, 512], BF16)) for i in range(2)]
    m_row = es.enter_context(ctx.sbuf(pfx + "m_row", [1, 512], F32))
    v_row = es.enter_context(ctx.sbuf(pfx + "v_row", [1, 512], F32))
    m2_row = es.enter_context(ctx.sbuf(pfx + "m2_row", [1, 512], F32))
    ve_row = es.enter_context(ctx.sbuf(pfx + "ve_row", [1, 512], F32))
    s_row = es.enter_context(ctx.sbuf(pfx + "s_row", [1, 512], F32))
    t_row = m2_row
    r_row = es.enter_context(ctx.sbuf(pfx + "r_row", [1, 512], F32))
    mr_row = ve_row
    b1_sb = es.enter_context(ctx.sbuf(pfx + "b1_sb", [128, 512], F32))
    b2_sb = es.enter_context(ctx.sbuf(pfx + "b2_sb", [128, 512], F32))
    xoA1 = es.enter_context(ctx.sbuf(pfx + "xoA", [128, 512], F32))
    xoB = [es.enter_context(ctx.sbuf(pfx + f"xoB{i}", [128, 512], F32)) for i in range(2)]

    with nc.Block(no_gpsimd_drain=True) as blk:
        @blk.sync
        def _(sp):
            for tc in range(NTC):
                if pre_sync:
                    pre_sync(sp, tc)
                for ob in range(nob):
                    i = tc * nob + ob
                    sync_loads(sp, tc, ob, i)
                    if i >= 2:
                        sp.wait_ge(S("dvey"), i - 1)
                    ctx.inc(sp.dma_start(out=xt[i % 2][:],
                                         in_=xres[ob * 128:(ob + 1) * 128, tc * 512:(tc + 1) * 512]),
                            pfx + "xld%d" % (i % 2), 16)

        @blk.tensor
        def _(t):
            if first_tensor_wait:
                first_tensor_wait(t)
            for tc in range(NTC):
                if pre_tensor:
                    pre_tensor(t, tc)

                def stats(ob):
                    i = tc * nob + ob
                    t.wait_ge(S("dvey"), i + 1)
                    if ob == 0 and tc >= 1:
                        t.wait_ge(S("dvesm"), 2 * tc)
                    t.matmul(pz[0][:], onescol[:], ysb[:, ob, :],
                             start=(ob == 0), stop=(ob == nob - 1))
                    t.wait_ge(S("act2"), i + 1)
                    ctx.inc(t.matmul(pz[1][:], onescol[:], y2[i % 2][:],
                                     start=(ob == 0), stop=(ob == nob - 1)), pfx + "pest", 1)

                for ob in range(nob):
                    i = tc * nob + ob
                    if i >= 2:
                        t.wait_ge(S("dvey"), i - 1)
                    ctx.inc(pe_mms(t, tc, ob, i, psc[i % 2]), pfx + "pe2", 1)
                    if ob >= 1:
                        stats(ob - 1)
                stats(nob - 1)
                t.wait_ge(S("dvesf"), tc + 1)
                if tc >= 1:
                    t.wait_ge(S("dveb"), tc)
                t.matmul(pb[:], ones32[0:1, :], r_row[0:1, :], start=True, stop=True)
                ctx.inc(t.matmul(pbm[:], ones32[0:1, :], mr_row[0:1, :],
                                 start=True, stop=True), pfx + "peb2", 1)

        @blk.vector
        def _(v):
            for tc in range(NTC):
                if pre_vector:
                    pre_vector(v, tc)
                for ob in range(nob):
                    i = tc * nob + ob
                    v.wait_ge(S("pe2"), i + 1)
                    v.wait_ge(S("xld%d" % (i % 2)), 16 * (i // 2 + 1))
                    ctx.inc(v.tensor_tensor(out=ysb[:, ob, :], in0=psc[i % 2][:],
                                            in1=xt[i % 2][:], op=ALU.add), pfx + "dvey", 1)
                v.wait_ge(S("pest"), (tc + 1) * nob)
                v.tensor_scalar(out=m_row[:], in0=pz[0][:], scalar1=1.0 / dmodel, scalar2=None, op0=ALU.mult)
                ctx.inc(v.tensor_scalar(out=v_row[:], in0=pz[1][:], scalar1=1.0 / dmodel,
                                        scalar2=None, op0=ALU.mult), pfx + "dvesm", 2)
                v.drain()
                v.wait_ge(S("actm"), tc + 1)
                v.tensor_tensor(out=ve_row[:], in0=v_row[:], in1=m2_row[:], op=ALU.subtract)
                v.drain()
                ctx.inc(v.tensor_scalar(out=ve_row[:], in0=ve_row[:], scalar1=eps,
                                        scalar2=None, op0=ALU.add), pfx + "dvev", 1)
                v.wait_ge(S("acts2"), tc + 1)
                v.reciprocal(t_row[:], s_row[:])
                v.drain()
                v.tensor_tensor(out=t_row[:], in0=ve_row[:], in1=t_row[:], op=ALU.mult)
                v.drain()
                v.tensor_tensor(out=t_row[:], in0=t_row[:], in1=s_row[:], op=ALU.add)
                v.drain()
                v.tensor_scalar(out=t_row[:], in0=t_row[:], scalar1=0.5, scalar2=None, op0=ALU.mult)
                v.drain()
                v.reciprocal(r_row[:], t_row[:])
                v.drain()
                ctx.inc(v.scalar_tensor_tensor(out=mr_row[:], in0=m_row[:], scalar=-1.0,
                                               in1=r_row[:], op0=ALU.mult, op1=ALU.mult), pfx + "dvesf", 1)
                v.wait_ge(S("peb2"), tc + 1)
                v.tensor_copy(b1_sb[:], pb[:])
                ctx.inc(v.tensor_copy(b2_sb[:], pbm[:]), pfx + "dveb", 1)
                v.drain()
                for ob in range(nob):
                    i = tc * nob + ob
                    if i >= 2:
                        v.wait_ge(S("st2%d" % (i % 2)), 16 * (i // 2))
                    v.tensor_tensor(out=xoA1[:], in0=ysb[:, ob, :], in1=b1_sb[:], op=ALU.mult)
                    v.drain()
                    ctx.inc(v.tensor_tensor(out=xoB[ob % 2][:], in0=xoA1[:],
                                            in1=b2_sb[:], op=ALU.add), pfx + "dvex", 1)
                    v.drain()

        @blk.scalar
        def _(s):
            for tc in range(NTC):
                if pre_scalar:
                    pre_scalar(s, tc)
                for ob in range(nob):
                    i = tc * nob + ob
                    if sc_ob:
                        sc_ob(s, tc, ob, i)
                    s.wait_ge(S("dvey"), i + 1)
                    if i >= 2:
                        s.wait_ge(S("pest"), i - 1)
                    ctx.inc(s.activation(out=y2[i % 2][:], in_=ysb[:, ob, :], func=AF.Square), pfx + "act2", 1)
                s.wait_ge(S("dvesm"), 2 * (tc + 1))
                ctx.inc(s.activation(out=m2_row[:], in_=m_row[:], func=AF.Square), pfx + "actm", 1)
                s.wait_ge(S("dvev"), tc + 1)
                ctx.inc(s.activation(out=s_row[:], in_=ve_row[:], func=AF.Sqrt), pfx + "acts2", 1)

        @blk.gpsimd
        def _(g):
            for tc in range(NTC):
                if pre_gpsimd:
                    pre_gpsimd(g, tc)
                for ob in range(nob):
                    i = tc * nob + ob
                    g.wait_ge(S("dvex"), i + 1)
                    ctx.inc(g.dma_start(out=xout[ob * 128:(ob + 1) * 128, tc * 512:(tc + 1) * 512],
                                        in_=xoB[ob % 2][:]), pfx + "st2%d" % (i % 2), 16)
            g.wait_ge(S("st20"), 16 * NTC * nob // 2)
            g.wait_ge(S("st21"), 16 * NTC * nob // 2)



def emit_b(nc, ctx, es, io):
    x1 = io["x1"]
    w_base = io["w_base"]
    b_base = io["b_base"]
    w_A = io["w_A"]
    w_B = io["w_B"]
    w_router = io["w_router"]
    x2 = io["x2"]

    wbf = ctx.dram("wbf_base", [D, D], BF16)
    bbf = ctx.dram("bbf_base", [D], BF16)
    btbf = ctx.dram("btbf", [8, 128, D], BF16)
    x1bf = ctx.dram("x1bf", [D, T], BF16)

    ones = es.enter_context(ctx.sbuf("ones", [1, 512], BF16))
    onescol = es.enter_context(ctx.sbuf("onescol", [128, 1], BF16))
    ones32 = es.enter_context(ctx.sbuf("ones32", [1, 128], F32))
    ident = es.enter_context(ctx.sbuf("ident", [128, 128], F32))
    abf_sb = es.enter_context(ctx.sbuf("abf_sb", [128, NIB, 64], BF16))
    rbf_sb = es.enter_context(ctx.sbuf("rbf_sb", [128, NIB, 16], BF16))
    rh_sb = es.enter_context(ctx.sbuf("rh_sb", [64, T], BF16))
    ctT_sb = es.enter_context(ctx.sbuf("ctT_sb", [16, T], F32))
    crow = [es.enter_context(ctx.sbuf(f"crow{i}", [1, 512], BF16)) for i in range(2)]

    ident_c = ctx.inline(np.eye(128, dtype=np.float32), "ident_c")

    psc = [es.enter_context(ctx.psum(f"psc{i}", [128, 512], F32)) for i in range(2)]
    pz = [es.enter_context(ctx.psum(f"pz{i}", [1, 512], F32)) for i in range(2)]
    pb = es.enter_context(ctx.psum("pb", [128, 512], F32))
    pbm = es.enter_context(ctx.psum("pbm", [128, 512], F32))
    pbc = [es.enter_context(ctx.psum(f"pbc{i}", [64, 512], F32)) for i in range(2)]
    with nc.Block(no_gpsimd_drain=True) as blk:
        @blk.vector
        def _(v):
            v.memset(ones[:], 1.0)
            v.memset(ones32[:], 1.0)
            ctx.inc(v.memset(onescol[:], 1.0), "init", 1)

        @blk.gpsimd
        def _(g):
            ctx.inc(g.dma_start(out=abf_sb[:],
                                in_=w_A.rearrange("(ib p) r -> p ib r", p=128)), "wcast", 16)
            ctx.inc(g.dma_start(out=rbf_sb[:],
                                in_=w_router.rearrange("(ib p) e -> p ib e", p=128)), "wcast", 16)
            ctx.inc(g.dma_start(out=ident[:], in_=ident_c), "wcast", 16)
            for j in range(4):
                rs = slice(j * 1024, (j + 1) * 1024)
                ctx.inc(g.dma_start(out=x1bf[rs, :], in_=x1[rs, :]), "wcast", 16)

    st1 = ExitStack()
    xb = [st1.enter_context(ctx.sbuf(f"xb{i}", [128, NIB, 512], BF16)) for i in range(2)]
    l_sb = st1.enter_context(ctx.sbuf("l_sb", [128, 16], F32))
    pe_sb = st1.enter_context(ctx.sbuf("pe_sb", [128, 16], F32))
    pn_sb = st1.enter_context(ctx.sbuf("pn_sb", [128, 16], F32))
    pm_sb = st1.enter_context(ctx.sbuf("pm_sb", [128, 16], F32))
    mk1_sb = st1.enter_context(ctx.sbuf("mk1_sb", [128, 16], F32))
    mk2_sb = st1.enter_context(ctx.sbuf("mk2_sb", [128, 16], F32))
    comb_sb = st1.enter_context(ctx.sbuf("comb_sb", [128, 16], F32))
    cstat = st1.enter_context(ctx.sbuf("cstat", [128, 6], F32))

    with nc.Block(no_gpsimd_drain=True) as blk:
        @blk.gpsimd
        def _(g):
            ctx.inc(g.dma_start(out=bbf[:], in_=b_base), "wcb2", 16)
            for j in range(4):
                cs = slice(j * 1024, (j + 1) * 1024)
                ctx.inc(g.dma_start(out=wbf[:, cs], in_=w_base[:, cs]), "wcb2", 16)
            ctx.inc(g.dma_start(out=btbf.rearrange("k p o -> p k o"),
                                in_=w_B.rearrange("k p o -> p k o")), "wcb2", 16)

        @blk.sync
        def _(sp):
            sp.wait_ge(ctx.sem("wcast"), 16 * 7)
            for tc in range(NTC):
                if tc >= 2:
                    sp.wait_ge(ctx.sem("pep"), (tc - 1) * 5)
                ctx.inc(sp.dma_start(
                    out=xb[tc % 2][:],
                    in_=x1bf.rearrange("(ib p) t -> p ib t", p=128)[:, :, tc * 512:(tc + 1) * 512]),
                    "xbld%d" % (tc % 2), 16)

        @blk.tensor
        def _(t):
            t.wait_ge(ctx.sem("init"), 1)
            for tc in range(NTC):
                t.wait_ge(ctx.sem("xbld%d" % (tc % 2)), 16 * (tc // 2 + 1))
                if tc >= 1:
                    t.wait_ge(ctx.sem("dvep"), tc)
                ins = None
                for ib in range(NIB):
                    ins = t.matmul(psc[0][0:64, :], abf_sb[:, ib, :], xb[tc % 2][:, ib, :],
                                   start=(ib == 0), stop=(ib == NIB - 1))
                ctx.inc(ins, "pep", 1)
                for tt in range(4):
                    g = 4 * tc + tt
                    if g >= 1:
                        t.wait_ge(ctx.sem("dvel"), g)
                    ins = None
                    for ib in range(NIB):
                        ins = t.matmul(psc[1][:, 0:16], xb[tc % 2][:, ib, tt * 128:(tt + 1) * 128],
                                       rbf_sb[:, ib, :], start=(ib == 0), stop=(ib == NIB - 1))
                    ctx.inc(ins, "pep", 1)
                    t.wait_ge(ctx.sem("dvec2"), g + 1)
                    if g >= 1:
                        t.wait_ge(ctx.sem("dvet"), g)
                    ctx.inc(t.transpose(pb[0:16, 0:128], comb_sb[:], ident[:]), "pept", 1)

        @blk.vector
        def _(v):
            for tc in range(NTC):
                v.wait_ge(ctx.sem("pep"), tc * 5 + 1)
                ctx.inc(v.tensor_copy(rh_sb[:, tc * 512:(tc + 1) * 512], psc[0][0:64, :]), "dvep", 1)
                for tt in range(4):
                    g = 4 * tc + tt
                    v.wait_ge(ctx.sem("pep"), tc * 5 + tt + 2)
                    v.tensor_reduce(cstat[:, 0:1], psc[1][:, 0:16], axis=AX.X, op=ALU.max)
                    v.drain()
                    v.tensor_scalar(out=cstat[:, 1:2], in0=cstat[:, 0:1], scalar1=-1.0, scalar2=None, op0=ALU.mult)
                    v.drain()
                    ctx.inc(v.tensor_copy(l_sb[:], psc[1][:, 0:16]), "dvel", 1)
                    v.wait_ge(ctx.sem("actp"), g + 1)
                    v.reciprocal(cstat[:, 2:3], cstat[:, 3:4])
                    v.drain()
                    v.tensor_scalar(out=pn_sb[:], in0=pe_sb[:], scalar1=cstat[:, 2:3], scalar2=None, op0=ALU.mult)
                    v.drain()
                    v.tensor_reduce(cstat[:, 0:1], pn_sb[:], axis=AX.X, op=ALU.max)
                    v.drain()
                    v.tensor_scalar(out=mk1_sb[:], in0=pn_sb[:], scalar1=cstat[:, 0:1], scalar2=None, op0=ALU.is_equal)
                    v.drain()
                    v.scalar_tensor_tensor(out=pm_sb[:], in0=mk1_sb[:], scalar=-1e30, in1=pn_sb[:],
                                           op0=ALU.mult, op1=ALU.add)
                    v.drain()
                    v.tensor_reduce(cstat[:, 1:2], pm_sb[:], axis=AX.X, op=ALU.max)
                    v.drain()
                    v.tensor_scalar(out=mk2_sb[:], in0=pm_sb[:], scalar1=cstat[:, 1:2], scalar2=None, op0=ALU.is_equal)
                    v.drain()
                    v.tensor_tensor(out=cstat[:, 2:3], in0=cstat[:, 0:1], in1=cstat[:, 1:2], op=ALU.add)
                    v.drain()
                    v.tensor_scalar(out=cstat[:, 2:3], in0=cstat[:, 2:3], scalar1=1e-6, scalar2=None, op0=ALU.add)
                    v.drain()
                    v.reciprocal(cstat[:, 2:3], cstat[:, 2:3])
                    v.drain()
                    v.tensor_scalar(out=cstat[:, 4:5], in0=cstat[:, 0:1], scalar1=cstat[:, 2:3],
                                    scalar2=0.5, op0=ALU.mult, op1=ALU.mult)
                    v.drain()
                    v.tensor_scalar(out=cstat[:, 5:6], in0=cstat[:, 1:2], scalar1=cstat[:, 2:3],
                                    scalar2=0.5, op0=ALU.mult, op1=ALU.mult)
                    v.drain()
                    v.tensor_scalar(out=comb_sb[:], in0=mk1_sb[:], scalar1=cstat[:, 4:5], scalar2=None, op0=ALU.mult)
                    v.drain()
                    ctx.inc(v.scalar_tensor_tensor(out=comb_sb[:], in0=mk2_sb[:], scalar=cstat[:, 5:6],
                                                   in1=comb_sb[:], op0=ALU.mult, op1=ALU.add), "dvec2", 1)
                    v.wait_ge(ctx.sem("pept"), g + 1)
                    ctx.inc(v.tensor_copy(ctT_sb[:, tc * 512 + tt * 128: tc * 512 + (tt + 1) * 128],
                                          pb[0:16, 0:128]), "dvet", 1)

        @blk.scalar
        def _(s):
            for tc in range(NTC):
                for tt in range(4):
                    g = 4 * tc + tt
                    s.wait_ge(ctx.sem("dvel"), g + 1)
                    ctx.inc(s.activation(out=pe_sb[:], in_=l_sb[:], func=AF.Exp,
                                         bias=cstat[:, 1:2], scale=1.0,
                                         accum_out=cstat[:, 3:4]), "actp", 1)

    st1.close()
    wsl = [es.enter_context(ctx.sbuf(f"wsl{i}", [128, NIB + 9, 128], BF16)) for i in range(2)]
    xb2 = [es.enter_context(ctx.sbuf(f"xb2_{i}", [128, NIB, 512], BF16)) for i in range(2)]
    h_sb = [es.enter_context(ctx.sbuf(f"h_sb{i}", [128, 8, 512], BF16)) for i in range(2)]

    def sync_loads(sp, tc, ob, i):
        if i == 0:
            sp.wait_ge(ctx.sem("wcb2"), 16 * 6)
        if i >= 2:
            sp.wait_ge(ctx.sem("Bpe2"), i - 1)
        sem = "Bwld%d" % (i % 2)
        ctx.inc(sp.dma_start(out=wsl[i % 2][:, 0:NIB, :],
                             in_=wbf.rearrange("(ib p) o -> p ib o", p=128)[:, :, ob * 128:(ob + 1) * 128]),
                sem, 16)
        ctx.inc(sp.dma_start(out=wsl[i % 2][:, NIB:NIB + 8, :],
                             in_=btbf.rearrange("k p o -> p k o")[:, :, ob * 128:(ob + 1) * 128]),
                sem, 16)
        ctx.inc(sp.dma_start(out=wsl[i % 2][0:1, NIB + 8, :],
                             in_=bbf[None, ob * 128:(ob + 1) * 128]), sem, 16)

    def pre_sync(sp, tc):
        if tc == 0:
            sp.wait_ge(ctx.sem("dvet"), 16)
        if tc >= 2:
            sp.wait_ge(ctx.sem("Bpe2"), (tc - 1) * NIB)
        ctx.inc(sp.dma_start(
            out=xb2[tc % 2][:],
            in_=x1bf.rearrange("(ib p) t -> p ib t", p=128)[:, :, tc * 512:(tc + 1) * 512]),
            "Bxbld%d" % (tc % 2), 16)

    def pre_gpsimd(g, tc):
        for e in range(16):
            ge = tc * 16 + e
            if ge == 0:
                g.wait_ge(ctx.sem("dvet"), 16)
            if ge >= 2:
                g.wait_ge(ctx.sem("Bpeh"), ge - 1)
            ctx.inc(g.dma_start(out=crow[e % 2][:],
                                in_=ctT_sb[e:e + 1, tc * 512:(tc + 1) * 512]), "crw%d" % (ge % 2), 16)

    def pre_tensor(t, tc):
        for e in range(16):
            ge = tc * 16 + e
            t.wait_ge(ctx.sem("crw%d" % (ge % 2)), 16 * (ge // 2 + 1))
            if ge >= 2:
                t.wait_ge(ctx.sem("Bdveh"), ge - 1)
            ctx.inc(t.matmul(pbc[ge % 2][:], ones[0:1, 0:64], crow[e % 2][0:1, :],
                             start=True, stop=True), "Bpeh", 1)

    def pre_vector(v, tc):
        for e in range(16):
            ge = tc * 16 + e
            v.wait_ge(ctx.sem("Bpeh"), ge + 1)
            if tc >= 2 and e == 0:
                v.wait_ge(ctx.sem("Bpe2"), (tc - 1) * NIB)
            ctx.inc(v.tensor_tensor(out=h_sb[tc % 2][(e % 2) * 64:(e % 2) * 64 + 64, e // 2, :],
                                    in0=rh_sb[0:64, tc * 512:(tc + 1) * 512],
                                    in1=pbc[ge % 2][:],
                                    op=ALU.mult), "Bdveh", 1)

    def pe_mms(t, tc, ob, i, psum):
        t.wait_ge(ctx.sem("Bwld%d" % (i % 2)), 48 * (i // 2 + 1))
        t.wait_ge(ctx.sem("Bxbld%d" % (tc % 2)), 16 * (tc // 2 + 1))
        t.wait_ge(ctx.sem("Bdveh"), (tc + 1) * 16)
        for ib in range(NIB):
            t.matmul(psum[:], wsl[i % 2][:, ib, :], xb2[tc % 2][:, ib, :],
                     start=(ib == 0), stop=False)
        for kb in range(8):
            t.matmul(psum[:], wsl[i % 2][:, NIB + kb, :], h_sb[tc % 2][:, kb, :],
                     start=False, stop=False)
        return t.matmul(psum[:], wsl[i % 2][0:1, NIB + 8, :], ones[0:1, :],
                        start=False, stop=True)

    def first_wait(t):
        t.wait_ge(ctx.sem("dvel"), 16)
        t.wait_ge(ctx.sem("dvet"), 16)
        t.wait_ge(ctx.sem("dvep"), 4)

    ln_out_phase(nc, ctx, es, "B", x1, x2, pe_mms, sync_loads,
                 psc, pz, pb, pbm, ones32, onescol,
                 pre_sync=pre_sync, pre_tensor=pre_tensor, pre_vector=pre_vector,
                 pre_gpsimd=pre_gpsimd, first_tensor_wait=first_wait)


def build_b():
    nc = bass.Bass("TRN2", target_bir_lowering=False)
    io = {}
    io["x1"] = nc.dram_tensor("x1", [D, T], F32, kind="ExternalInput").ap()
    io["w_base"] = nc.dram_tensor("w_base", [D, D], F32, kind="ExternalInput").ap()
    io["b_base"] = nc.dram_tensor("b_base", [D], F32, kind="ExternalInput").ap()
    io["w_A"] = nc.dram_tensor("w_A", [D, 64], F32, kind="ExternalInput").ap()
    io["w_B"] = nc.dram_tensor("w_B", [8, 128, D], F32, kind="ExternalInput").ap()
    io["w_router"] = nc.dram_tensor("w_router", [D, 16], F32, kind="ExternalInput").ap()
    io["x2"] = nc.dram_tensor("x2", [D, T], F32, kind="ExternalOutput").ap()
    es = ExitStack()
    ctx = Ctx(nc, es, "B_")
    emit_b(nc, ctx, es, io)
    return nc




def emit_c(nc, ctx, es, io):
    xin = io["x2"]
    w = io["w_qkv1"]
    q_fm = io["q1_fm"]
    k_fm = io["k1_fm"]
    v_tok = io["v1_tok"]
    g2 = io.get("gctx")
    wbf = io.get("wbf_qkv1") or ctx.dram("wbf_qkv1", [D, 3 * D], BF16)
    if g2 is not None:
        wcw = lambda sp, j: sp.wait_ge(g2.sem("Cwc%d" % j), 16)
    else:
        wcw = lambda sp, j: sp.wait_ge(ctx.sem("wc%d" % j), 16)

    x_sb = es.enter_context(ctx.sbuf("x_sb", [128, NIB, T], BF16))
    with nc.Block(no_gpsimd_drain=True) as blk:
        @blk.gpsimd
        def _(g):
            ctx.inc(g.dma_start(out=x_sb[:], in_=xin.rearrange("(ib p) t -> p ib t", p=128)), "xld", 16)
            if g2 is None:
                for j in range(8):
                    cs = slice(j * 1536, (j + 1) * 1536)
                    ctx.inc(g.dma_start(out=wbf[:, cs], in_=w[:, cs]), "wc%d" % j, 16)

    wt = [es.enter_context(ctx.sbuf(f"wt{i}", [128, NIB, 128], BF16)) for i in range(2)]
    po = [es.enter_context(ctx.psum(f"po{i}", [128, 512], F32)) for i in range(2)]
    ob_out = [es.enter_context(ctx.sbuf(f"qk_out{i}", [128, T], BF16)) for i in range(2)]

    NOB_QK = 64
    with nc.Block(no_gpsimd_drain=True) as blk:
        @blk.sync
        def _(sp):
            for ob in range(NOB_QK):
                h, which = ob // 2, ob % 2
                off = h * 384 + which * 128
                if ob >= 2:
                    sp.wait_ge(ctx.sem("pe"), (ob - 1) * NTC)
                wcw(sp, (off + 127) // 1536)
                ctx.inc(sp.dma_start(out=wt[ob % 2][:],
                                     in_=wbf.rearrange("(ib p) o -> p ib o", p=128)[:, :, off:off + 128]),
                        "wld%d" % (ob % 2), 16)

        @blk.tensor
        def _(t):
            t.wait_ge(ctx.sem("xld"), 16)
            for ob in range(NOB_QK):
                t.wait_ge(ctx.sem("wld%d" % (ob % 2)), 16 * (ob // 2 + 1))
                for tc in range(NTC):
                    grp = ob * NTC + tc
                    bank = grp % 2
                    if grp >= 2:
                        t.wait_ge(ctx.sem("ev"), grp - 1)
                    ins = None
                    for ib in range(NIB):
                        ins = t.matmul(po[bank][:], wt[ob % 2][:, ib, :],
                                       x_sb[:, ib, tc * 512:(tc + 1) * 512],
                                       start=(ib == 0), stop=(ib == NIB - 1))
                    ctx.inc(ins, "pe", 1)

        @blk.scalar
        def _(s):
            for ob in range(NOB_QK):
                if ob >= 2:
                    s.wait_ge(ctx.sem("st%d" % (ob % 2)), 16 * (ob // 2))
                for tc in range(NTC):
                    grp = ob * NTC + tc
                    bank = grp % 2
                    s.wait_ge(ctx.sem("pe"), grp + 1)
                    ctx.inc(s.activation(out=ob_out[ob % 2][:, tc * 512:(tc + 1) * 512],
                                         in_=po[bank][:], func=AF.Silu), "ev", 1)

        @blk.gpsimd
        def _(g):
            for ob in range(NOB_QK):
                g.wait_ge(ctx.sem("ev"), (ob + 1) * NTC)
                h, which = ob // 2, ob % 2
                dst = q_fm if which == 0 else k_fm
                row = h * 128
                ctx.inc(g.dma_start(out=dst[row:row + 128, :], in_=ob_out[ob % 2][:]),
                        "st%d" % (ob % 2), 16)

    wv = es.enter_context(ctx.sbuf("wv", [128, NIB, 4, 128], BF16))
    pv = [es.enter_context(ctx.psum(f"pv{i}", [128, 512], F32)) for i in range(2)]
    v_out = es.enter_context(ctx.sbuf("v_out", [128, 16, 512], BF16))

    NOC = 8
    wr = wbf.rearrange("i (h x) -> i h x", h=32)
    with nc.Block(no_gpsimd_drain=True) as blk:
        @blk.sync
        def _(sp):
            for oc in range(NOC):
                if oc >= 1:
                    sp.wait_ge(ctx.sem("pev"), oc * 16)
                wcw(sp, 7)
                wcw(sp, (4 * oc * 384 + 256) // 1536)
                for hh in range(4):
                    ctx.inc(sp.dma_start(
                        out=wv[:, :, hh, :],
                        in_=wr.rearrange("(ib p) h x -> p ib h x", p=128)[:, :, 4 * oc + hh, 256:384]),
                        "wvld", 16)

        @blk.tensor
        def _(t):
            for oc in range(NOC):
                t.wait_ge(ctx.sem("wvld"), 64 * (oc + 1))
                for tb in range(16):
                    grp = oc * 16 + tb
                    bank = grp % 2
                    if grp >= 2:
                        t.wait_ge(ctx.sem("evv"), grp - 1)
                    ins = None
                    for ib in range(NIB):
                        ins = t.matmul(pv[bank][:], x_sb[:, ib, tb * 128:(tb + 1) * 128],
                                       wv[:, ib, :, :], start=(ib == 0), stop=(ib == NIB - 1))
                    ctx.inc(ins, "pev", 1)

        @blk.scalar
        def _(s):
            for oc in range(NOC):
                if oc >= 1:
                    s.wait_ge(ctx.sem("stv"), 64 * oc)
                for tb in range(16):
                    grp = oc * 16 + tb
                    bank = grp % 2
                    s.wait_ge(ctx.sem("pev"), grp + 1)
                    ctx.inc(s.activation(out=v_out[:, tb, :], in_=pv[bank][:], func=AF.Silu), "evv", 1)

        @blk.gpsimd
        def _(g):
            for oc in range(NOC):
                g.wait_ge(ctx.sem("evv"), (oc + 1) * 16)
                for hh in range(4):
                    ctx.inc(g.dma_start(
                        out=v_tok.rearrange("(tb p) o -> p tb o", p=128)[:, :, (4 * oc + hh) * 128:(4 * oc + hh + 1) * 128],
                        in_=v_out[:, :, hh * 128:(hh + 1) * 128]), "stv", 16)
            g.wait_ge(ctx.sem("st0"), 16 * (NOB_QK // 2))
            g.wait_ge(ctx.sem("st1"), 16 * (NOB_QK // 2))
            g.wait_ge(ctx.sem("stv"), 64 * NOC)



def build_c():
    nc = bass.Bass("TRN2", target_bir_lowering=False)
    io = {}
    io["x2"] = nc.dram_tensor("x2", [D, T], F32, kind="ExternalInput").ap()
    io["w_qkv1"] = nc.dram_tensor("w_qkv1", [D, 3 * D], F32, kind="ExternalInput").ap()
    io["q1_fm"] = nc.dram_tensor("q1_fm", [D, T], BF16, kind="ExternalOutput").ap()
    io["k1_fm"] = nc.dram_tensor("k1_fm", [D, T], BF16, kind="ExternalOutput").ap()
    io["v1_tok"] = nc.dram_tensor("v1_tok", [T, D], BF16, kind="ExternalOutput").ap()
    es = ExitStack()
    ctx = Ctx(nc, es, "C_")
    emit_c(nc, ctx, es, io)
    return nc



def _l1_decays():
    HEADS, BLOCK = 32, 256
    LAYER_IDX, N_LAYERS = 1, 32
    base = 1.0 / 2.0 ** (8.0 / HEADS)
    exponent = np.arange(HEADS, dtype=np.float64) + 1.0
    factor = 1.0 - LAYER_IDX / (N_LAYERS - 1 + 1e-05) + 1e-05
    slope = (base ** exponent * factor)
    r = np.arange(BLOCK, dtype=np.float64) + 1.0
    q_decay = np.exp(-slope[:, None] * r[None, :])
    k_decay = np.exp(-slope[:, None] * (BLOCK - r)[None, :])
    diff = r[:, None] - r[None, :]
    diag = slope[:, None, None] * diff[None, :, :]
    dd = np.where(diag >= 0, np.exp(-np.maximum(diag, 0.0)), 0.0)
    block_decay = np.exp(-slope * BLOCK)
    return (q_decay.astype(np.float32), k_decay.astype(np.float32),
            dd.astype(np.float32), block_decay.astype(np.float32))



def emit_d(nc, ctx, es, io):
    q_fm = io["q1_fm"]
    k_fm = io["k1_fm"]
    v_tok = io["v1_tok"]
    kv_init = io["kv_init"]
    attn_out = io["attn_d"]
    kv8_out = io["kv8_out"]

    qd, kd, dd, bd = _l1_decays()
    dct = np.ascontiguousarray(np.transpose(dd, (0, 2, 1)).reshape(32, 2, 128, 256))
    dct_c = ctx.inline(dct, "dct_c")
    qd_t = np.ascontiguousarray(np.tile(qd, (1, 8)))
    qd_c = ctx.inline(qd_t, "qd_c")
    kdT = np.ascontiguousarray(np.transpose(kd.reshape(32, 2, 128), (2, 1, 0)))
    kd_c = ctx.inline(kdT, "kd_c")
    ident_c = ctx.inline(np.eye(128, dtype=ml_dtypes.bfloat16).astype(ml_dtypes.bfloat16), "ident_cb")

    NB = 8

    attn_sb = es.enter_context(ctx.sbuf("attn_sb", [128, 32, T], BF16))
    ones32 = es.enter_context(ctx.sbuf("ones32", [1, 128], F32))
    ident = es.enter_context(ctx.sbuf("ident", [128, 128], BF16))
    kd_sb = es.enter_context(ctx.sbuf("kd_sb", [128, 2, 32], F32))
    qh = [es.enter_context(ctx.sbuf(f"qh{i}", [128, T], BF16)) for i in range(2)]
    kh = [es.enter_context(ctx.sbuf(f"kh{i}", [128, T], BF16)) for i in range(2)]
    vh = [es.enter_context(ctx.sbuf(f"vh{i}", [128, 16, 128], BF16)) for i in range(2)]
    dct_sb = [es.enter_context(ctx.sbuf(f"dct_sb{i}", [128, 2, 256], F32)) for i in range(2)]
    qdr = [es.enter_context(ctx.sbuf(f"qdr{i}", [1, T], F32)) for i in range(2)]
    kvi = [es.enter_context(ctx.sbuf(f"kvi{i}", [128, 128], F32)) for i in range(2)]
    qp_sb = es.enter_context(ctx.sbuf("qp_sb", [128, T], BF16))
    msk = es.enter_context(ctx.sbuf("msk", [128, 2, 256], BF16))
    ckp = es.enter_context(ctx.sbuf("ckp", [128, 2, 128], BF16))
    kvf = [es.enter_context(ctx.sbuf(f"kvf{i}", [128, 128], F32)) for i in range(2)]
    kvb = es.enter_context(ctx.sbuf("kvb", [128, 128], BF16))

    pqp = es.enter_context(ctx.psum("pqp", [128, 512], F32))
    ps2 = [es.enter_context(ctx.psum(f"ps2_{i}", [128, 2, 256], F32)) for i in range(2)]
    ptr = es.enter_context(ctx.psum("ptr", [128, 2, 128], BF16))
    po = [es.enter_context(ctx.psum(f"po{i}", [128, 256], F32)) for i in range(2)]
    pkv = es.enter_context(ctx.psum("pkv", [128, 128], F32))
    with nc.Block(no_gpsimd_drain=True) as blk:
        @blk.vector
        def _(v):
            ctx.inc(v.memset(ones32[:], 1.0), "init", 1)

        @blk.gpsimd
        def _(g):
            ctx.inc(g.dma_start(out=ident[:], in_=ident_c), "cld", 16)
            ctx.inc(g.dma_start(out=kd_sb[:], in_=kd_c), "cld", 16)

    with nc.Block(no_gpsimd_drain=True) as blk:
        @blk.sync
        def _(sp):
            for h in range(32):
                if h >= 2:
                    sp.wait_ge(ctx.sem("pek"), (h - 1) * NB)
                sem = "hld%d" % (h % 2)
                r = slice(h * 128, (h + 1) * 128)
                ctx.inc(sp.dma_start(out=qh[h % 2][:], in_=q_fm[r, :]), sem, 16)
                ctx.inc(sp.dma_start(out=kh[h % 2][:], in_=k_fm[r, :]), sem, 16)
                ctx.inc(sp.dma_start(out=vh[h % 2][:],
                                     in_=v_tok.rearrange("(tb p) o -> p tb o", p=128)[:, :, r]), sem, 16)
                ctx.inc(sp.dma_start(out=dct_sb[h % 2][:], in_=dct_c[h].rearrange("kb p q -> p kb q")), sem, 16)
                ctx.inc(sp.dma_start(out=qdr[h % 2][:], in_=qd_c[h][None, :]), sem, 16)
                ctx.inc(sp.dma_start(out=kvi[h % 2][:], in_=kv_init[h]), sem, 16)

        @blk.tensor
        def _(t):
            t.wait_ge(ctx.sem("init"), 1)
            t.wait_ge(ctx.sem("cld"), 32)
            for h in range(32):
                t.wait_ge(ctx.sem("hld%d" % (h % 2)), 96 * (h // 2 + 1))
                for c in range(NTC):
                    if h * NTC + c >= 1:
                        t.wait_ge(ctx.sem("dveq"), h * NTC + c)
                    ctx.inc(t.matmul(pqp[:], ones32[0:1, :], qdr[h % 2][0:1, c * 512:(c + 1) * 512],
                                     start=True, stop=True), "peq", 1)
                for n in range(NB):
                    gi = h * NB + n
                    if gi >= 2:
                        t.wait_ge(ctx.sem("dvem"), 2 * (gi - 1))
                    t.wait_ge(ctx.sem("dveq"), h * NTC + NTC)
                    t.matmul(ps2[gi % 2][:, 0, :], kh[h % 2][:, n * 256:n * 256 + 128],
                             qh[h % 2][:, n * 256:(n + 1) * 256], start=True, stop=False)
                    ctx.inc(t.matmul(ps2[gi % 2][:, 1, :], kh[h % 2][:, n * 256 + 128:(n + 1) * 256],
                                     qh[h % 2][:, n * 256:(n + 1) * 256], start=False, stop=True), "pes", 1)
                    if gi >= 1:
                        t.wait_ge(ctx.sem("dvec"), 2 * gi)
                    t.transpose(ptr[:, 0, :], kh[h % 2][:, n * 256:n * 256 + 128], ident[:])
                    ctx.inc(t.transpose(ptr[:, 1, :], kh[h % 2][:, n * 256 + 128:(n + 1) * 256],
                                        ident[:]), "pet", 1)
                    t.wait_ge(ctx.sem("dvem"), 2 * gi + 2)
                    if gi >= 2:
                        t.wait_ge(ctx.sem("dveo"), gi - 1)
                    t.matmul(po[gi % 2][:], vh[h % 2][:, 2 * n, :], msk[:, 0, :],
                             start=True, stop=False)
                    t.matmul(po[gi % 2][:], vh[h % 2][:, 2 * n + 1, :], msk[:, 1, :],
                             start=False, stop=False)
                    t.wait_ge(ctx.sem("dvek"), h * (NB + 1) + n + 1)
                    ctx.inc(t.matmul(po[gi % 2][:], kvb[:], qp_sb[:, n * 256:(n + 1) * 256],
                                     start=False, stop=True), "peo", 1)
                    t.wait_ge(ctx.sem("dvec"), 2 * gi + 2)
                    t.matmul(pkv[:], ckp[:, 0, :], vh[h % 2][:, 2 * n, :], start=True, stop=False)
                    ctx.inc(t.matmul(pkv[:], ckp[:, 1, :], vh[h % 2][:, 2 * n + 1, :],
                                     start=False, stop=True), "pek", 1)

        @blk.vector
        def _(v):
            for h in range(32):
                for c in range(NTC):
                    v.wait_ge(ctx.sem("peq"), h * NTC + c + 1)
                    ctx.inc(v.tensor_tensor(out=qp_sb[:, c * 512:(c + 1) * 512],
                                            in0=qh[h % 2][:, c * 512:(c + 1) * 512],
                                            in1=pqp[:], op=ALU.mult), "dveq", 1)
                if h >= 1:
                    v.wait_ge(ctx.sem("pek"), h * NB)
                    v.wait_ge(ctx.sem("stk%d" % ((h - 1) % 2)), 16 * ((h - 1) // 2 + 1))
                v.tensor_copy(kvf[0][:], kvi[h % 2][:])
                v.drain()
                ctx.inc(v.tensor_copy(kvb[:], kvf[0][:]), "dvek", 1)
                for n in range(NB):
                    gi = h * NB + n
                    v.wait_ge(ctx.sem("pes"), gi + 1)
                    v.tensor_tensor(out=msk[:, 0, :], in0=ps2[gi % 2][:, 0, :],
                                    in1=dct_sb[h % 2][:, 0, :], op=ALU.mult)
                    ctx.inc(v.tensor_tensor(out=msk[:, 1, :], in0=ps2[gi % 2][:, 1, :],
                                            in1=dct_sb[h % 2][:, 1, :], op=ALU.mult), "dvem", 2)
                    v.wait_ge(ctx.sem("pet"), gi + 1)
                    v.tensor_scalar(out=ckp[:, 0, :], in0=ptr[:, 0, :],
                                    scalar1=kd_sb[:, 0, h:h + 1], scalar2=None, op0=ALU.mult)
                    ctx.inc(v.tensor_scalar(out=ckp[:, 1, :], in0=ptr[:, 1, :],
                                            scalar1=kd_sb[:, 1, h:h + 1], scalar2=None, op0=ALU.mult),
                            "dvec", 2)
                    v.wait_ge(ctx.sem("peo"), gi + 1)
                    ctx.inc(v.tensor_copy(attn_sb[:, h, n * 256:(n + 1) * 256], po[gi % 2][:]),
                            "dveo", 1)
                    v.wait_ge(ctx.sem("pek"), gi + 1)
                    v.scalar_tensor_tensor(out=kvf[(n + 1) % 2][:], in0=kvf[n % 2][:],
                                           scalar=float(bd[h]), in1=pkv[:],
                                           op0=ALU.mult, op1=ALU.add)
                    v.drain()
                    ctx.inc(v.tensor_copy(kvb[:], kvf[(n + 1) % 2][:]), "dvek", 1)
                    v.drain()

        @blk.gpsimd
        def _(g):
            for h in range(32):
                g.wait_ge(ctx.sem("dvek"), h * (NB + 1) + NB + 1)
                ctx.inc(g.dma_start(out=kv8_out[h], in_=kvf[NB % 2][:]), "stk%d" % (h % 2), 16)
            g.wait_ge(ctx.sem("dveo"), 32 * NB)
            for ib in range(32):
                ctx.inc(g.dma_start(out=attn_out[ib * 128:(ib + 1) * 128, :],
                                    in_=attn_sb[:, ib, :]), "sta", 16)
            g.wait_ge(ctx.sem("sta"), 16 * 32)
            g.wait_ge(ctx.sem("stk0"), 16 * 16)
            g.wait_ge(ctx.sem("stk1"), 16 * 16)



def build_d():
    nc = bass.Bass("TRN2", target_bir_lowering=False)
    io = {}
    io["q1_fm"] = nc.dram_tensor("q1_fm", [D, T], BF16, kind="ExternalInput").ap()
    io["k1_fm"] = nc.dram_tensor("k1_fm", [D, T], BF16, kind="ExternalInput").ap()
    io["v1_tok"] = nc.dram_tensor("v1_tok", [T, D], BF16, kind="ExternalInput").ap()
    io["kv_init"] = nc.dram_tensor("kv_init", [32, 128, 128], F32, kind="ExternalInput").ap()
    io["attn_d"] = nc.dram_tensor("attn_d", [D, T], BF16, kind="ExternalOutput").ap()
    io["kv8_out"] = nc.dram_tensor("kv8_out", [32, 128, 128], F32, kind="ExternalOutput").ap()
    es = ExitStack()
    ctx = Ctx(nc, es, "D_")
    emit_d(nc, ctx, es, io)
    return nc





def emit_e(nc, ctx, es, io):
    flag = io.get("flag")
    kv8ag = io.get("kv8_ag")
    sel = io.get("sel")
    attn_d = io["attn_d"]
    kv8c = io["kv8c"]
    q1_fm = io["q1_fm"]
    x2 = io["x2"]
    w_gate = io["w_gate"]
    w_out1 = io["w_out1"]
    norm_w = io["norm_w"]
    x3 = io["x3"]

    g2 = io.get("gctx")
    wgbf = io.get("wgbf") or ctx.dram("wgbf", [D, D], BF16)
    wobf = io.get("wobf") or ctx.dram("wobf", [D, D], BF16)
    x2bf = ctx.dram("x2bf_e", [D, T], BF16)

    qd, kd, dd, bd = _l1_decays()
    qcorr = np.empty((32, T), np.float32)
    for h in range(32):
        for m in range(8):
            qcorr[h, m * 256:(m + 1) * 256] = qd[h] * (bd[h] ** m)
    qcorr_c = ctx.inline(qcorr.astype(ml_dtypes.bfloat16), "qcorr_c")

    onescol = es.enter_context(ctx.sbuf("onescol", [128, 1], BF16))
    ones32 = es.enter_context(ctx.sbuf("ones32", [1, 128], F32))
    onesb = es.enter_context(ctx.sbuf("onesb", [1, 128], BF16))
    normw_sb = es.enter_context(ctx.sbuf("normw_sb", [128, NIB], F32))
    kv8b = es.enter_context(ctx.sbuf("kv8b", [128, 32, 128], BF16))
    flag_sb = es.enter_context(ctx.sbuf("flag_sb", [128, 1], F32))
    if kv8ag is not None:
        kvr = [es.enter_context(ctx.sbuf(f"kvr{i}", [128, 8, 128], BF16)) for i in range(2)]
        sel_sb = es.enter_context(ctx.sbuf("sel_sb", [128, 8], F32))
    at_sb = es.enter_context(ctx.sbuf("at_sb", [128, NIB, 512], BF16))
    xg_sb = es.enter_context(ctx.sbuf("xg_sb", [128, NIB, 512], BF16))
    q1t = [es.enter_context(ctx.sbuf(f"q1t{i}", [128, 512], BF16)) for i in range(2)]
    qcr = [es.enter_context(ctx.sbuf(f"qcr{i}", [1, 512], BF16)) for i in range(2)]
    cqs = es.enter_context(ctx.sbuf("cqs", [128, 512], BF16))
    rms2 = [es.enter_context(ctx.sbuf(f"rms2_{i}", [128, 512], BF16)) for i in range(2)]
    gt_sb = [es.enter_context(ctx.sbuf(f"gt_sb{i}", [128, 512], BF16)) for i in range(2)]
    sE = es.enter_context(ctx.sbuf("sE", [1, 512], F32))
    veE = es.enter_context(ctx.sbuf("veE", [1, 512], F32))
    tE = es.enter_context(ctx.sbuf("tE", [1, 512], F32))
    rE = es.enter_context(ctx.sbuf("rE", [1, 512], F32))
    wg = [es.enter_context(ctx.sbuf(f"wg{i}", [128, NIB, 128], BF16)) for i in range(2)]
    wo = [es.enter_context(ctx.sbuf(f"wo{i}", [128, NIB, 128], BF16)) for i in range(2)]

    psc = [es.enter_context(ctx.psum(f"psc{i}", [128, 512], F32)) for i in range(2)]
    pz = [es.enter_context(ctx.psum(f"pz{i}", [1, 512], F32)) for i in range(2)]
    pb = es.enter_context(ctx.psum("pb", [128, 512], F32))
    pbm = es.enter_context(ctx.psum("pbm", [128, 512], F32))
    pq = es.enter_context(ctx.psum("pq", [128, 512], F32))
    pc = es.enter_context(ctx.psum("pc", [128, 512], F32))
    with nc.Block(no_gpsimd_drain=True) as blk:
        @blk.vector
        def _(v):
            v.memset(ones32[:], 1.0)
            v.memset(onesb[:], 1.0)
            ctx.inc(v.memset(onescol[:], 1.0), "init", 1)

        @blk.gpsimd
        def _(g):
            for j in range(4):
                cs = slice(j * 1024, (j + 1) * 1024)
                if g2 is None:
                    ctx.inc(g.dma_start(out=wgbf[:, cs], in_=w_gate[:, cs]), "wcg", 16)
                    ctx.inc(g.dma_start(out=wobf[:, cs], in_=w_out1[:, cs]), "wco", 16)
                ctx.inc(g.dma_start(out=x2bf[cs, :], in_=x2[cs, :]), "wcx", 16)
            if kv8ag is None:
                ctx.inc(g.dma_start(out=kv8b[:], in_=kv8c.rearrange("h p e -> p h e")), "wck", 16)
            else:
                ctx.inc(g.dma_start(out=flag_sb[0:1, 0:1], in_=kv8c[0][0:1, 0:1]), "wck", 16)
            ctx.inc(g.dma_start(out=normw_sb[:], in_=norm_w.rearrange("(ib p) -> p ib", p=128), allow_slow_non_contiguous=True), "wck", 16)
            if flag is not None:
                ctx.inc(g.dma_start(out=flag_sb[:], in_=flag), "wck", 16)

    if kv8ag is not None:
        with nc.Block(no_gpsimd_drain=True) as blk:
            @blk.gpsimd
            def _(g):
                ctx.inc(g.dma_start(out=sel_sb[:], in_=sel), "Esel", 16)
                for h in range(32):
                    if h >= 2:
                        g.wait_ge(ctx.sem("Ekv"), h - 1)
                    for r in range(8):
                        ctx.inc(g.dma_start(out=kvr[h % 2][:, r, :],
                                            in_=kv8ag[(r * 32 + h) * 128:(r * 32 + h + 1) * 128, :]),
                                "Ekvl%d" % (h % 2), 16)

            @blk.vector
            def _(v):
                v.wait_ge(ctx.sem("Esel"), 16)
                for h in range(32):
                    v.wait_ge(ctx.sem("Ekvl%d" % (h % 2)), 128 * (h // 2 + 1))
                    v.tensor_scalar(out=kv8b[:, h, :], in0=kvr[h % 2][:, 0, :],
                                    scalar1=sel_sb[:, 0:1], scalar2=None, op0=ALU.mult)
                    v.drain()
                    for r in range(1, 8):
                        v.scalar_tensor_tensor(out=kv8b[:, h, :], in0=kvr[h % 2][:, r, :],
                                               scalar=sel_sb[:, r:r + 1], in1=kv8b[:, h, :],
                                               op0=ALU.mult, op1=ALU.add)
                        v.drain()
                    ctx.inc(v.nop(), "Ekv", 1)

    if flag is not None:
        with nc.Block(no_gpsimd_drain=True) as blk:
            @blk.vector
            def _(v):
                v.wait_ge(ctx.sem("wck"), 48)
                ctx.inc(v.tensor_scalar(out=kv8b[:].rearrange("p h e -> p (h e)"),
                                        in0=kv8b[:].rearrange("p h e -> p (h e)"),
                                        scalar1=flag_sb[:], scalar2=None, op0=ALU.mult), "Efl", 1)

    def pre_sync(sp, tc):
        if tc == 0:
            sp.wait_ge(ctx.sem("wcx"), 64)
            if g2 is None:
                sp.wait_ge(ctx.sem("wcg"), 64)
                sp.wait_ge(ctx.sem("wco"), 64)
            else:
                sp.wait_ge(g2.sem("Ewcg"), 64)
                sp.wait_ge(g2.sem("Ewco"), 64)
        if tc >= 1:
            sp.wait_ge(ctx.sem("Lpe2"), tc * NIB)
        ctx.inc(sp.dma_start(out=at_sb[:],
                             in_=attn_d.rearrange("(ib p) t -> p ib t", p=128)[:, :, tc * 512:(tc + 1) * 512]),
                "Eat%d" % (tc % 2), 16)
        ctx.inc(sp.dma_start(out=xg_sb[:],
                             in_=x2bf.rearrange("(ib p) t -> p ib t", p=128)[:, :, tc * 512:(tc + 1) * 512]),
                "Exg%d" % (tc % 2), 16)
        for h in range(32):
            i = tc * 32 + h
            if i >= 2:
                sp.wait_ge(ctx.sem("Edvq"), i - 1)
            ctx.inc(sp.dma_start(out=q1t[h % 2][:],
                                 in_=q1_fm[h * 128:(h + 1) * 128, tc * 512:(tc + 1) * 512]),
                    "Eq1%d" % (i % 2), 16)
            ctx.inc(sp.dma_start(out=qcr[h % 2][:],
                                 in_=qcorr_c[h][None, tc * 512:(tc + 1) * 512]),
                    "Eq1%d" % (i % 2), 16)
        for ob in range(NIB):
            i = tc * NIB + ob
            if i >= 2:
                sp.wait_ge(ctx.sem("Epeg"), i - 1)
            ctx.inc(sp.dma_start(out=wg[i % 2][:],
                                 in_=wgbf.rearrange("(ib p) o -> p ib o", p=128)[:, :, ob * 128:(ob + 1) * 128]),
                    "Ewg%d" % (i % 2), 16)


    def pre_tensor(t, tc):
        if tc == 0:
            t.wait_ge(ctx.sem("init"), 1)
            t.wait_ge(ctx.sem("wck"), 32)
            if flag is not None:
                t.wait_ge(ctx.sem("Efl"), 1)
            if kv8ag is not None:
                t.wait_ge(ctx.sem("Ekv"), 32)
        for h in range(32):
            i = tc * 32 + h
            t.wait_ge(ctx.sem("Eq1%d" % (i % 2)), 32 * (i // 2 + 1))
            if i >= 1:
                t.wait_ge(ctx.sem("Edvq"), i)
            if h == 0 and tc >= 1:
                t.wait_ge(ctx.sem("Eacg"), tc * NIB)
            ctx.inc(t.matmul(pq[:], onesb[0:1, :], qcr[h % 2][0:1, :],
                             start=True, stop=True), "Epq", 1)
            t.wait_ge(ctx.sem("Edvq"), i + 1)
            if i >= 1:
                t.wait_ge(ctx.sem("Edvc"), i)
            ctx.inc(t.matmul(pc[:], kv8b[:, h, :], cqs[:], start=True, stop=True), "Epc", 1)
        for ib in range(NIB):
            i = tc * NIB + ib
            t.wait_ge(ctx.sem("Eact2"), i + 1)
            ctx.inc(t.matmul(pb[0:1, :], onescol[:], rms2[i % 2][:],
                             start=(ib == 0), stop=(ib == NIB - 1)), "Epst", 1)
        t.wait_ge(ctx.sem("Edvr"), tc + 1)
        ctx.inc(t.matmul(pbm[:], ones32[0:1, :], rE[0:1, :], start=True, stop=True), "Epb1", 1)
        for ob in range(NIB):
            i = tc * NIB + ob
            t.wait_ge(ctx.sem("Ewg%d" % (i % 2)), 16 * (i // 2 + 1))
            t.wait_ge(ctx.sem("Exg%d" % (tc % 2)), 16 * (tc // 2 + 1))
            if i >= 2:
                t.wait_ge(ctx.sem("Eacg"), i - 1)
            bank = pq if ob % 2 == 0 else pc
            ins = None
            for ib in range(NIB):
                ins = t.matmul(bank[:], wg[i % 2][:, ib, :], xg_sb[:, ib, :],
                               start=(ib == 0), stop=(ib == NIB - 1))
            ctx.inc(ins, "Epeg", 1)

    def pre_vector(v, tc):
        for h in range(32):
            i = tc * 32 + h
            v.wait_ge(ctx.sem("Epq"), i + 1)
            if tc >= 1 and h == 0:
                v.wait_ge(ctx.sem("Lpe2"), tc * NIB)
            v.wait_ge(ctx.sem("Eat%d" % (tc % 2)), 16 * (tc // 2 + 1))
            ctx.inc(v.tensor_tensor(out=cqs[:], in0=q1t[h % 2][:], in1=pq[:], op=ALU.mult), "Edvq", 1)
            v.drain()
            v.wait_ge(ctx.sem("Epc"), i + 1)
            ctx.inc(v.tensor_tensor(out=at_sb[:, h, :], in0=at_sb[:, h, :], in1=pc[:], op=ALU.add), "Edvc", 1)
            v.drain()
        v.wait_ge(ctx.sem("Epst"), (tc + 1) * NIB)
        ctx.inc(v.tensor_scalar(out=veE[:], in0=pb[0:1, :], scalar1=1.0 / D, scalar2=1e-5,
                                op0=ALU.mult, op1=ALU.add), "Edvv", 1)
        v.drain()
        v.wait_ge(ctx.sem("Eacs"), tc + 1)
        v.reciprocal(tE[:], sE[:])
        v.drain()
        v.tensor_tensor(out=tE[:], in0=veE[:], in1=tE[:], op=ALU.mult)
        v.drain()
        v.tensor_tensor(out=tE[:], in0=tE[:], in1=sE[:], op=ALU.add)
        v.drain()
        v.tensor_scalar(out=tE[:], in0=tE[:], scalar1=0.5, scalar2=None, op0=ALU.mult)
        v.drain()
        ctx.inc(v.reciprocal(rE[:], tE[:]), "Edvr", 1)
        v.drain()
        v.wait_ge(ctx.sem("Epb1"), tc + 1)
        for ib in range(NIB):
            i = tc * NIB + ib
            v.wait_ge(ctx.sem("Eact2"), i + 1)
            ctx.inc(v.scalar_tensor_tensor(out=at_sb[:, ib, :], in0=at_sb[:, ib, :],
                                           scalar=normw_sb[:, ib:ib + 1], in1=pbm[:],
                                           op0=ALU.mult, op1=ALU.mult), "Edvn", 1)
            v.drain()
        for ob in range(NIB):
            i = tc * NIB + ob
            v.wait_ge(ctx.sem("Eacg"), i + 1)
            ctx.inc(v.tensor_tensor(out=at_sb[:, ob, :], in0=at_sb[:, ob, :],
                                    in1=gt_sb[ob % 2][:], op=ALU.mult), "Edgn", 1)
            v.drain()

    def pre_scalar(s, tc):
        for ib in range(NIB):
            i = tc * NIB + ib
            s.wait_ge(ctx.sem("Edvc"), tc * 32 + ib + 1)
            if i >= 2:
                s.wait_ge(ctx.sem("Epst"), i - 1)
            ctx.inc(s.activation(out=rms2[i % 2][:], in_=at_sb[:, ib, :], func=AF.Square), "Eact2", 1)
        s.wait_ge(ctx.sem("Edvv"), tc + 1)
        ctx.inc(s.activation(out=sE[:], in_=veE[:], func=AF.Sqrt), "Eacs", 1)
        for ob in range(NIB):
            i = tc * NIB + ob
            s.wait_ge(ctx.sem("Epeg"), i + 1)
            s.wait_ge(ctx.sem("Edgn"), tc * NIB + max(0, ob - 1))
            bank = pq if ob % 2 == 0 else pc
            ctx.inc(s.activation(out=gt_sb[ob % 2][:], in_=bank[:], func=AF.Sigmoid), "Eacg", 1)

    def sync_loads(sp, tc, ob, i):
        if i >= 2:
            sp.wait_ge(ctx.sem("Lpe2"), i - 1)
        ctx.inc(sp.dma_start(out=wo[i % 2][:],
                             in_=wobf.rearrange("(ib p) o -> p ib o", p=128)[:, :, ob * 128:(ob + 1) * 128]),
                "Ewo%d" % (i % 2), 16)

    def pe_mms(t, tc, ob, i, psum):
        t.wait_ge(ctx.sem("Ewo%d" % (i % 2)), 16 * (i // 2 + 1))
        t.wait_ge(ctx.sem("Edgn"), (tc + 1) * NIB)
        ins = None
        for ib in range(NIB):
            ins = t.matmul(psum[:], wo[i % 2][:, ib, :], at_sb[:, ib, :],
                           start=(ib == 0), stop=(ib == NIB - 1))
        return ins

    ln_out_phase(nc, ctx, es, "L", x2, x3, pe_mms, sync_loads,
                 psc, pz, pb, pbm, ones32, onescol,
                 pre_sync=pre_sync, pre_tensor=pre_tensor, pre_vector=pre_vector,
                 pre_scalar=pre_scalar)


def build_e():
    nc = bass.Bass("TRN2", target_bir_lowering=False)
    io = {}
    io["attn_d"] = nc.dram_tensor("attn_d", [D, T], BF16, kind="ExternalInput").ap()
    io["kv8c"] = nc.dram_tensor("kv8c", [32, 128, 128], F32, kind="ExternalInput").ap()
    io["q1_fm"] = nc.dram_tensor("q1_fm", [D, T], BF16, kind="ExternalInput").ap()
    io["x2"] = nc.dram_tensor("x2", [D, T], F32, kind="ExternalInput").ap()
    io["w_gate"] = nc.dram_tensor("w_gate", [D, D], F32, kind="ExternalInput").ap()
    io["w_out1"] = nc.dram_tensor("w_out1", [D, D], F32, kind="ExternalInput").ap()
    io["norm_w"] = nc.dram_tensor("norm_w", [D], F32, kind="ExternalInput").ap()
    io["x3"] = nc.dram_tensor("x3", [D, T], F32, kind="ExternalOutput").ap()
    es = ExitStack()
    ctx = Ctx(nc, es, "E_")
    emit_e(nc, ctx, es, io)
    return nc





def emit_f(nc, ctx, es, io):
    x3 = io["x3"]
    up = io["up"]
    down = io["down"]
    x4 = io["x4"]

    upbf = ctx.dram("upbf", [16, 256, 1024], BF16)
    dnbf = ctx.dram("dnbf", [16, 1024, 256], BF16)
    x3bf = ctx.dram("x3bf", [D, T], BF16)

    onescol = es.enter_context(ctx.sbuf("onescol", [128, 1], BF16))
    ones32 = es.enter_context(ctx.sbuf("ones32", [1, 128], F32))
    xf_sb = [es.enter_context(ctx.sbuf(f"xf_sb{i}", [128, NIB, 512], BF16)) for i in range(2)]
    up_sb = [es.enter_context(ctx.sbuf(f"up_sb{i}", [128, 2, 1024], BF16)) for i in range(2)]
    dn_sb = [es.enter_context(ctx.sbuf(f"dn_sb{i}", [128, 8, 256], BF16)) for i in range(2)]
    mid_sb = [es.enter_context(ctx.sbuf(f"mid_sb{i}", [128, 8, 512], BF16)) for i in range(2)]

    psc = [es.enter_context(ctx.psum(f"psc{i}", [128, 512], F32)) for i in range(2)]
    pz = [es.enter_context(ctx.psum(f"pz{i}", [1, 512], F32)) for i in range(2)]
    pb = es.enter_context(ctx.psum("pb", [128, 512], F32))
    pbm = es.enter_context(ctx.psum("pbm", [128, 512], F32))
    pmid = [es.enter_context(ctx.psum(f"pmid{i}", [128, 512], F32)) for i in range(2)]
    with nc.Block(no_gpsimd_drain=True) as blk:
        @blk.vector
        def _(v):
            v.memset(ones32[:], 1.0)
            ctx.inc(v.memset(onescol[:], 1.0), "init", 1)

        @blk.gpsimd
        def _(g):
            ctx.inc(g.dma_start(out=upbf.rearrange("h d m -> (h d) m"),
                                in_=up.rearrange("h d m -> (h d) m")), "wcu", 16)
            ctx.inc(g.dma_start(out=dnbf.rearrange("h m d -> (h m) d"),
                                in_=down.rearrange("h m d -> (h m) d")), "wcu", 16)
            for j in range(4):
                cs = slice(j * 1024, (j + 1) * 1024)
                ctx.inc(g.dma_start(out=x3bf[cs, :], in_=x3[cs, :]), "wcu", 16)

    def pre_sync(sp, tc):
        if tc >= 2:
            sp.wait_ge(ctx.sem("Fpu"), 8 * ((tc - 1) * 16))
        if tc == 0:
            sp.wait_ge(ctx.sem("wcu"), 96)
        ctx.inc(sp.dma_start(out=xf_sb[tc % 2][:],
                             in_=x3bf.rearrange("(ib p) t -> p ib t", p=128)[:, :, tc * 512:(tc + 1) * 512]),
                "Fxf%d" % (tc % 2), 16)

    def sync_loads(sp, tc, ob, i):
        if ob % 2 == 1:
            return
        h = ob // 2
        g = tc * 16 + h
        if g >= 2:
            sp.wait_ge(ctx.sem("Fpu"), 8 * (g - 1))
        ctx.inc(sp.dma_start(out=up_sb[g % 2][:],
                             in_=upbf[h].rearrange("(ib p) m -> p ib m", p=128)), "Fup%d" % (g % 2), 16)
        if g >= 2:
            sp.wait_ge(ctx.sem("Fpe2"), tc * NIB + ob - 2)
        ctx.inc(sp.dma_start(out=dn_sb[g % 2][:],
                             in_=dnbf[h].rearrange("(mb p) d -> p mb d", p=128)), "Fup%d" % (g % 2), 16)

    def pe_mms(t, tc, ob, i, psum):
        h = ob // 2
        g = tc * 16 + h
        db = ob % 2
        if db == 0:
            t.wait_ge(ctx.sem("Fup%d" % (g % 2)), 32 * (g // 2 + 1))
            t.wait_ge(ctx.sem("Fxf%d" % (tc % 2)), 16 * (tc // 2 + 1))
            for mb in range(8):
                gm = g * 8 + mb
                if gm >= 2:
                    t.wait_ge(ctx.sem("Fgel"), gm - 1)
                t.matmul(pmid[gm % 2][:], up_sb[g % 2][:, 0, mb * 128:(mb + 1) * 128],
                         xf_sb[tc % 2][:, 2 * h, :], start=True, stop=False)
                ctx.inc(t.matmul(pmid[gm % 2][:], up_sb[g % 2][:, 1, mb * 128:(mb + 1) * 128],
                                 xf_sb[tc % 2][:, 2 * h + 1, :], start=False, stop=True), "Fpu", 1)
            t.wait_ge(ctx.sem("Fgel"), (g + 1) * 8)
        ins = None
        for mb in range(8):
            ins = t.matmul(psum[:], dn_sb[g % 2][:, mb, db * 128:(db + 1) * 128],
                           mid_sb[g % 2][:, mb, :], start=(mb == 0), stop=(mb == 7))
        return ins

    def sc_ob(s, tc, ob, i):
        if ob % 2 == 1:
            return
        h = ob // 2
        g = tc * 16 + h
        for mb in range(8):
            gm = g * 8 + mb
            s.wait_ge(ctx.sem("Fpu"), gm + 1)
            if g >= 2 and mb == 0:
                s.wait_ge(ctx.sem("Fpe2"), tc * NIB + ob - 1)
            ctx.inc(s.activation(out=mid_sb[g % 2][:, mb, :], in_=pmid[gm % 2][:],
                                 func=AF.Gelu), "Fgel", 1)

    def first_wait(t):
        t.wait_ge(ctx.sem("init"), 1)

    ln_out_phase(nc, ctx, es, "F", x3, x4, pe_mms, sync_loads,
                 psc, pz, pb, pbm, ones32, onescol,
                 pre_sync=pre_sync, sc_ob=sc_ob, first_tensor_wait=first_wait)


def build_f():
    nc = bass.Bass("TRN2", target_bir_lowering=False)
    io = {}
    io["x3"] = nc.dram_tensor("x3", [D, T], F32, kind="ExternalInput").ap()
    io["up"] = nc.dram_tensor("up", [16, 256, 1024], F32, kind="ExternalInput").ap()
    io["down"] = nc.dram_tensor("down", [16, 1024, 256], F32, kind="ExternalInput").ap()
    io["x4"] = nc.dram_tensor("x4", [D, T], F32, kind="ExternalOutput").ap()
    es = ExitStack()
    ctx = Ctx(nc, es, "F_")
    emit_f(nc, ctx, es, io)
    return nc





def emit_a1f(nc, ctx, es, io):
    xin = io["xin_f"]
    w = io["w_qkv0"]
    bias = io["b_qkv0"]
    q_fm, k_fm, v_tok = io["q_fm"], io["k_fm"], io["v_tok"]
    wbf = ctx.dram("wbf_qkv0", [D, 3 * D], BF16)
    bbf = ctx.dram("bbf_qkv0", [3 * D], BF16)

    x_sb = es.enter_context(ctx.sbuf("x_sb", [128, NIB, T], BF16))
    ones = es.enter_context(ctx.sbuf("ones", [1, 512], BF16))
    wt = [es.enter_context(ctx.sbuf(f"wt{i}", [128, NIB + 1, 128], BF16)) for i in range(2)]
    po = [es.enter_context(ctx.psum(f"po{i}", [128, 512], F32)) for i in range(2)]
    ob_out = [es.enter_context(ctx.sbuf(f"qk_out{i}", [128, T], BF16)) for i in range(2)]
    wv = es.enter_context(ctx.sbuf("wv", [128, NIB + 1, 512], BF16))
    pv = [es.enter_context(ctx.psum(f"pv{i}", [128, 512], F32)) for i in range(2)]
    v_out = es.enter_context(ctx.sbuf("v_out", [128, 16, 512], BF16))

    with nc.Block(no_gpsimd_drain=True) as blk:
        @blk.vector
        def _(v):
            ctx.inc(v.memset(ones[:], 1.0), "xld0", 1)

        @blk.gpsimd
        def _(g):
            ctx.inc(g.dma_start(out=bbf[:], in_=bias), "wcb", 16)
            ctx.inc(g.dma_start(out=wbf[:, 0:1536], in_=w[:, 0:1536]), "wc0", 16)
            for tcx in range(NTC):
                ctx.inc(g.dma_start(out=x_sb[:, :, tcx * 512:(tcx + 1) * 512],
                                    in_=xin.rearrange("(ib p) t -> p ib t", p=128)[:, :, tcx * 512:(tcx + 1) * 512]),
                        "xc%d" % tcx, 16)
            for j in range(1, 8):
                cs = slice(j * 1536, (j + 1) * 1536)
                ctx.inc(g.dma_start(out=wbf[:, cs], in_=w[:, cs]), "wc%d" % j, 16)

    oi = 0
    gi = 0
    for hf in range(2):
        obs = list(range(64)) if hf == 0 else list(range(32, 64))
        if hf == 1:
            pe_h0 = ctx.val("pe")
            pev_h0 = ctx.val("pev")
            ev_h0 = ctx.val("ev")
            evv_h0 = ctx.val("evv")
            with nc.Block(no_gpsimd_drain=True) as blk:
                @blk.gpsimd
                def _(g):
                    g.wait_ge(ctx.sem("pe"), pe_h0)
                    g.wait_ge(ctx.sem("pev"), pev_h0)
                    ctx.inc(g.dma_start(out=x_sb[:],
                                        in_=xin.rearrange("(ib p) t -> p ib t", p=128)[:, :, T:2 * T]),
                            "xldb", 16)

        oi_base = oi
        with nc.Block(no_gpsimd_drain=True) as blk:
            @blk.sync
            def _(sp):
                for j, ob in enumerate(obs):
                    o = oi_base + j
                    if o >= 2:
                        sp.wait_ge(ctx.sem("pe"), (o - 1) * NTC)
                    sp.wait_ge(ctx.sem("wc%d" % ((ob * 128 + 127) // 1536)), 16)
                    sp.wait_ge(ctx.sem("wcb"), 16)
                    sem = "wld%d" % (o % 2)
                    ctx.inc(sp.dma_start(out=wt[o % 2][:, 0:NIB, :],
                                         in_=wbf.rearrange("(ib p) o -> p ib o", p=128)[:, :, ob * 128:(ob + 1) * 128]),
                            sem, 16)
                    ctx.inc(sp.dma_start(out=wt[o % 2][0:1, NIB, :],
                                         in_=bbf[None, ob * 128:(ob + 1) * 128]),
                            sem, 16)

            @blk.tensor
            def _(t):
                if hf == 0:
                    t.wait_ge(ctx.sem("xld0"), 1)
                else:
                    t.wait_ge(ctx.sem("xldb"), 16)
                for j, ob in enumerate(obs):
                    o = oi_base + j
                    t.wait_ge(ctx.sem("wld%d" % (o % 2)), 32 * (o // 2 + 1))
                    for tc in range(NTC):
                        if hf == 0:
                            t.wait_ge(ctx.sem("xc%d" % tc), 16)
                        grp = o * NTC + tc
                        bank = grp % 2
                        if grp >= 2:
                            t.wait_ge(ctx.sem("ev"), grp - 1)
                        for ib in range(NIB):
                            t.matmul(po[bank][:], wt[o % 2][:, ib, :],
                                     x_sb[:, ib, tc * 512:(tc + 1) * 512],
                                     start=(ib == 0), stop=False)
                        ctx.inc(t.matmul(po[bank][:], wt[o % 2][0:1, NIB, :],
                                         ones[0:1, :], start=False, stop=True), "pe", 1)

            @blk.scalar
            def _(s):
                for j, ob in enumerate(obs):
                    o = oi_base + j
                    if o >= 2:
                        s.wait_ge(ctx.sem("st%d" % (o % 2)), 16 * (o // 2))
                    for tc in range(NTC):
                        grp = o * NTC + tc
                        bank = grp % 2
                        s.wait_ge(ctx.sem("pe"), grp + 1)
                        ctx.inc(s.copy(out=ob_out[o % 2][:, tc * 512:(tc + 1) * 512],
                                       in_=po[bank][:]), "ev", 1)

            @blk.gpsimd
            def _(g):
                for j, ob in enumerate(obs):
                    o = oi_base + j
                    g.wait_ge(ctx.sem("ev"), (o + 1) * NTC)
                    if ob < 32:
                        dst = q_fm[ob * 128:(ob + 1) * 128, :]
                    else:
                        dst = k_fm[(ob - 32) * 128:(ob - 31) * 128, hf * T:(hf + 1) * T]
                    ctx.inc(g.dma_start(out=dst, in_=ob_out[o % 2][:]),
                            "st%d" % (o % 2), 16)
        oi += len(obs)

        gi_base = gi
        with nc.Block(no_gpsimd_drain=True) as blk:
            @blk.sync
            def _(sp):
                for oc in range(8):
                    g2 = gi_base + oc
                    if g2 >= 1:
                        sp.wait_ge(ctx.sem("pev"), g2 * 16)
                    sp.wait_ge(ctx.sem("wc%d" % ((2 * D + oc * 512 + 511) // 1536)), 16)
                    sp.wait_ge(ctx.sem("wcb"), 16)
                    ctx.inc(sp.dma_start(out=wv[:, 0:NIB, :],
                                         in_=wbf.rearrange("(ib p) o -> p ib o", p=128)[:, :, 2 * D + oc * 512: 2 * D + (oc + 1) * 512]),
                            "wvld", 16)
                    ctx.inc(sp.dma_start(out=wv[0:1, NIB, :],
                                         in_=bbf[None, 2 * D + oc * 512: 2 * D + (oc + 1) * 512]),
                            "wvld", 16)

            @blk.tensor
            def _(t):
                for oc in range(8):
                    g2 = gi_base + oc
                    t.wait_ge(ctx.sem("wvld"), 32 * (g2 + 1))
                    for tb in range(16):
                        grp = g2 * 16 + tb
                        bank = grp % 2
                        if grp >= 2:
                            t.wait_ge(ctx.sem("evv"), grp - 1)
                        for ib in range(NIB):
                            t.matmul(pv[bank][:], x_sb[:, ib, tb * 128:(tb + 1) * 128],
                                     wv[:, ib, :], start=(ib == 0), stop=False)
                        ctx.inc(t.matmul(pv[bank][:], ones[0:1, 0:128],
                                         wv[0:1, NIB, :], start=False, stop=True), "pev", 1)

            @blk.scalar
            def _(s):
                for oc in range(8):
                    g2 = gi_base + oc
                    if g2 >= 1:
                        s.wait_ge(ctx.sem("stv"), 16 * g2)
                    for tb in range(16):
                        grp = g2 * 16 + tb
                        bank = grp % 2
                        s.wait_ge(ctx.sem("pev"), grp + 1)
                        ctx.inc(s.copy(out=v_out[:, tb, :], in_=pv[bank][:]), "evv", 1)

            @blk.gpsimd
            def _(g):
                for oc in range(8):
                    g2 = gi_base + oc
                    g.wait_ge(ctx.sem("evv"), (g2 + 1) * 16)
                    ctx.inc(g.dma_start(
                        out=v_tok.rearrange("(tb p) o -> p tb o", p=128)[:, 16 * hf:16 * hf + 16, oc * 512:(oc + 1) * 512],
                        in_=v_out[:]), "stv", 16)
        gi += 8

    with nc.Block(no_gpsimd_drain=True) as blk:
        @blk.gpsimd
        def _(g):
            g.wait_ge(ctx.sem("st0"), 16 * 48)
            g.wait_ge(ctx.sem("st1"), 16 * 48)
            g.wait_ge(ctx.sem("stv"), 16 * 16)

    return nc



def _phase_reset(nc, snap):
    nc.all_engine_barrier()
    nc.clear_and_free_semaphores(nc.allocated_since(snap))
    nc.all_engine_barrier()
    nc.restore_sems(snap)


def build_fused():
    nc = bass.Bass("TRN2", target_bir_lowering=False, num_devices=NCORES)
    io = {}
    EI = lambda n, s, d=F32: nc.dram_tensor(n, s, d, kind="ExternalInput").ap()
    io["xin_f"] = EI("xin_f", [D, 2 * T])
    io["w_qkv0"] = EI("w_qkv0", [D, 3 * D]); io["b_qkv0"] = EI("b_qkv0", [3 * D])
    io["w_o0"] = EI("w_o0", [D, D]); io["b_o0"] = EI("b_o0", [D])
    io["w_base"] = EI("w_base", [D, D]); io["b_base"] = EI("b_base", [D])
    io["w_A"] = EI("w_A", [D, 64]); io["w_B"] = EI("w_B", [8, 128, D])
    io["w_router"] = EI("w_router", [D, 16])
    io["w_qkv1"] = EI("w_qkv1", [D, 3 * D])
    io["w_gate"] = EI("w_gate", [D, D]); io["w_out1"] = EI("w_out1", [D, D])
    io["norm_w"] = EI("norm_w", [D])
    io["up"] = EI("up", [16, 256, 1024]); io["down"] = EI("down", [16, 1024, 256])
    io["sel"] = EI("sel", [128, 8])
    io["kv_init"] = EI("kv_init", [32, 128, 128])
    io["x4"] = nc.dram_tensor("x4", [D, T], F32, kind="ExternalOutput").ap()

    INT = lambda n, s, d: nc.dram_tensor(n, s, d, kind="Internal").ap()
    io["q_fm"] = INT("q_fm", [D, T], BF16)
    io["k_fm"] = INT("k_fm", [D, 2 * T], BF16)
    io["v_tok"] = INT("v_tok", [2 * T, D], BF16)
    io["x1"] = INT("x1", [D, T], F32)
    io["x2"] = INT("x2", [D, T], F32)
    io["q1_fm"] = INT("q1_fm", [D, T], BF16)
    io["k1_fm"] = INT("k1_fm", [D, T], BF16)
    io["v1_tok"] = INT("v1_tok", [T, D], BF16)
    io["attn_d"] = INT("attn_d", [D, T], BF16)
    io["kv8_out"] = INT("kv8_out", [32, 128, 128], F32)
    io["kv8_ag"] = INT("kv8_ag", [8 * 32 * 128, 128], F32)
    io["xin"] = io["xin_f"][:, 0:T]
    io["kv8c"] = io["kv8_ag"][0:32 * 128, :].rearrange("(h p) e -> h p e", p=128)
    io["x3"] = INT("x3", [D, T], F32)
    io["wbf_qkv1"] = INT("wbf_qkv1", [D, 3 * D], BF16)
    io["wgbf"] = INT("wgbf", [D, D], BF16)
    io["wobf"] = INT("wobf", [D, D], BF16)

    ges = ExitStack()
    gctx = Ctx(nc, ges, "G_")
    io["gctx"] = gctx
    for j in range(8):
        gctx.sem("Cwc%d" % j)
    gctx.sem("Ewcg")
    gctx.sem("Ewco")

    def prefetch_c():
        with nc.Block(no_gpsimd_drain=True) as blk:
            @blk.gpsimd
            def _(g):
                for j in range(8):
                    cs = slice(j * 1536, (j + 1) * 1536)
                    gctx.inc(g.dma_start(out=io["wbf_qkv1"][:, cs], in_=io["w_qkv1"][:, cs]),
                             "Cwc%d" % j, 16)

    def prefetch_e():
        with nc.Block(no_gpsimd_drain=True) as blk:
            @blk.gpsimd
            def _(g):
                for j in range(4):
                    cs = slice(j * 1024, (j + 1) * 1024)
                    gctx.inc(g.dma_start(out=io["wgbf"][:, cs], in_=io["w_gate"][:, cs]), "Ewcg", 16)
                    gctx.inc(g.dma_start(out=io["wobf"][:, cs], in_=io["w_out1"][:, cs]), "Ewco", 16)

    def soft_barrier(prev):
        waits = prev.finals()
        with nc.Block(no_gpsimd_drain=True) as blk:
            for eng in ("sync", "tensor", "vector", "scalar", "gpsimd"):
                def body(e, waits=waits):
                    for h, v in waits:
                        e.wait_ge(h, v)
                getattr(blk, eng)(body)

    groups = [
        [("P1_", emit_a1f), ("P2_", emit_a2)],
        [("P3_", emit_b), ("P4_", emit_c), ("P5_", emit_d), ("CC_", None)],
        [("P6_", emit_e), ("P7_", emit_f)],
    ]
    for gi, group in enumerate(groups):
        snap = nc.snapshot_sems()
        prev = None
        for pfx, emit in group:
            if pfx == "P2_":
                prefetch_c()
            if pfx == "P3_":
                prefetch_e()
            es = ExitStack()
            ctx = Ctx(nc, es, pfx)
            if prev is not None:
                soft_barrier(prev)
            if emit is None:
                with nc.Block(no_gpsimd_drain=True) as blk:
                    @blk.gpsimd
                    def _(g):
                        ins = g.collective_compute(
                            "AllGather", ALU.bypass,
                            replica_groups=[[0, 1, 2, 3, 4, 5, 6, 7]],
                            ins=[io["kv8_out"].rearrange("h p e -> (h p) e")],
                            outs=[io["kv8_ag"]])
                        ctx.inc(ins, "cc", 1)
                        g.wait_ge(ctx.sem("cc"), 1)
            else:
                emit(nc, ctx, es, io)
            es.close()
            prev = ctx
        _phase_reset(nc, snap)
    return nc




def _permuted_inputs(inp):
    P = np.asarray(inp["L1_ffn_perm"]).astype(np.int64)
    out = {}
    x = np.asarray(inp["L0_attn_x"])
    out["x"] = x[:, :, P]
    out["w_qkv0"] = np.ascontiguousarray(np.asarray(inp["L0_attn_attn_qkvw"])[P, :])
    out["b_qkv0"] = np.asarray(inp["L0_attn_attn_qkvb"])
    out["w_o0"] = np.ascontiguousarray(np.asarray(inp["L0_attn_attn_ow"])[:, P])
    out["b_o0"] = np.asarray(inp["L0_attn_attn_ob"])[P]
    bw = np.asarray(inp["L0_ffn_base_w"])
    out["w_base"] = np.ascontiguousarray(bw[P][:, P].T)
    out["b_base"] = np.asarray(inp["L0_ffn_base_b"])[P]
    out["w_A"] = np.ascontiguousarray(np.asarray(inp["L0_ffn_A"])[:, P].T)
    B = np.asarray(inp["L0_ffn_B"])
    Bp = B[:, P, :]
    BT = np.empty((8, 128, D), dtype=B.dtype)
    for kb in range(8):
        BT[kb, 0:64, :] = Bp[2 * kb].T
        BT[kb, 64:128, :] = Bp[2 * kb + 1].T
    out["w_B"] = np.ascontiguousarray(BT)
    out["w_router"] = np.ascontiguousarray(np.asarray(inp["L0_ffn_router_w"])[:, P].T)
    out["w_qkv1"] = np.ascontiguousarray(np.asarray(inp["L1_attn_qkv_w"])[:, P].T)
    out["w_gate"] = np.ascontiguousarray(np.asarray(inp["L1_attn_gate_w"])[:, P].T)
    out["w_out1"] = np.ascontiguousarray(np.asarray(inp["L1_attn_out_w"])[P, :].T)
    out["norm_w"] = np.asarray(inp["L1_attn_norm_w"])
    out["up"] = np.asarray(inp["L1_ffn_up"])
    out["down"] = np.asarray(inp["L1_ffn_down"])
    out["P"] = P
    return out


def _core_x_fm(xp, c):
    b, h = c // 2, c % 2
    return np.ascontiguousarray(xp[b, h * T:(h + 1) * T, :].T.astype(np.float32))


def _weight_feed(pre):
    return {
        "w_qkv0": pre["w_qkv0"], "b_qkv0": pre["b_qkv0"],
        "w_o0": pre["w_o0"], "b_o0": pre["b_o0"],
        "w_base": pre["w_base"], "b_base": pre["b_base"],
        "w_A": pre["w_A"], "w_B": pre["w_B"], "w_router": pre["w_router"],
        "w_qkv1": pre["w_qkv1"], "w_gate": pre["w_gate"], "w_out1": pre["w_out1"],
        "norm_w": pre["norm_w"], "up": pre["up"], "down": pre["down"],
    }


def kernel(**inputs):
    pre = _permuted_inputs(inputs)
    P = pre["P"]
    inv = np.argsort(P)
    xs = [_core_x_fm(pre["x"], c) for c in range(NCORES)]
    zeros_kv = np.zeros((32, 128, 128), np.float32)
    wfeed = _weight_feed(pre)

    nc = build_fused()
    in_maps = []
    for c in range(NCORES):
        own, partner = c, c ^ 1
        xin_f = np.ascontiguousarray(np.concatenate([xs[own], xs[partner]], axis=1))
        sel = np.zeros((128, 8), np.float32)
        if c % 2 == 1:
            sel[:, c - 1] = 1.0
        in_maps.append(dict(wfeed, xin_f=xin_f, sel=sel, kv_init=zeros_kv))
    res = run_bass_kernel_spmd(nc, in_maps, core_ids=list(range(NCORES))).results

    out = np.empty((4, 4096, 4096), np.float32)
    for c in range(NCORES):
        b, half = c // 2, c % 2
        out[b, half * T:(half + 1) * T, :] = np.asarray(res[c]["x4"]).T[:, inv]
    return out


def kernel_unfused(**inputs):
    pre = _permuted_inputs(inputs)
    P = pre["P"]
    inv = np.argsort(P)
    xs = [_core_x_fm(pre["x"], c) for c in range(NCORES)]
    zeros_kv = np.zeros((32, 128, 128), np.float32)

    nc = build_a1()
    r1 = run_bass_kernel_spmd(nc, [
        {"xin": xs[c], "w_qkv0": pre["w_qkv0"], "b_qkv0": pre["b_qkv0"]}
        for c in range(NCORES)], core_ids=list(range(NCORES))).results

    nc = build_a2()
    in2 = []
    for c in range(NCORES):
        b = c // 2
        kf = np.ascontiguousarray(np.concatenate(
            [np.asarray(r1[2 * b]["k_fm"]), np.asarray(r1[2 * b + 1]["k_fm"])], axis=1))
        vf = np.ascontiguousarray(np.concatenate(
            [np.asarray(r1[2 * b]["v_tok"]), np.asarray(r1[2 * b + 1]["v_tok"])], axis=0))
        in2.append({"q_fm": np.asarray(r1[c]["q_fm"]), "k_fm": kf, "v_tok": vf,
                    "xin": xs[c], "w_o0": pre["w_o0"], "b_o0": pre["b_o0"]})
    r2 = run_bass_kernel_spmd(nc, in2, core_ids=list(range(NCORES))).results

    nc = build_b()
    r3 = run_bass_kernel_spmd(nc, [
        {"x1": np.asarray(r2[c]["x1"]), "w_base": pre["w_base"], "b_base": pre["b_base"],
         "w_A": pre["w_A"], "w_B": pre["w_B"], "w_router": pre["w_router"]}
        for c in range(NCORES)], core_ids=list(range(NCORES))).results

    nc = build_c()
    r4 = run_bass_kernel_spmd(nc, [
        {"x2": np.asarray(r3[c]["x2"]), "w_qkv1": pre["w_qkv1"]}
        for c in range(NCORES)], core_ids=list(range(NCORES))).results

    nc = build_d()
    r5 = run_bass_kernel_spmd(nc, [
        {"q1_fm": np.asarray(r4[c]["q1_fm"]), "k1_fm": np.asarray(r4[c]["k1_fm"]),
         "v1_tok": np.asarray(r4[c]["v1_tok"]), "kv_init": zeros_kv}
        for c in range(NCORES)], core_ids=list(range(NCORES))).results

    nc = build_e()
    in6 = []
    for c in range(NCORES):
        kv8c = np.asarray(r5[c - 1]["kv8_out"]) if c % 2 == 1 else zeros_kv
        in6.append({"attn_d": np.asarray(r5[c]["attn_d"]), "kv8c": kv8c,
                    "q1_fm": np.asarray(r4[c]["q1_fm"]), "x2": np.asarray(r3[c]["x2"]),
                    "w_gate": pre["w_gate"], "w_out1": pre["w_out1"],
                    "norm_w": pre["norm_w"]})
    r6 = run_bass_kernel_spmd(nc, in6, core_ids=list(range(NCORES))).results

    nc = build_f()
    r7 = run_bass_kernel_spmd(nc, [
        {"x3": np.asarray(r6[c]["x3"]), "up": pre["up"], "down": pre["down"]}
        for c in range(NCORES)], core_ids=list(range(NCORES))).results

    out = np.empty((4, 4096, 4096), np.float32)
    for c in range(NCORES):
        b, half = c // 2, c % 2
        out[b, half * T:(half + 1) * T, :] = np.asarray(r7[c]["x4"]).T[:, inv]
    return out


if __name__ == "__main__":
    pass
